# version 02
# baseline (speedup 1.0000x reference)
import numpy as np
import ml_dtypes
from contextlib import ExitStack

import concourse.bass as bass
import concourse.mybir as mybir
from concourse.bass_utils import run_bass_kernel_spmd

F32 = mybir.dt.float32
BF16 = mybir.dt.bfloat16
AF = mybir.ActivationFunctionType
ALU = mybir.AluOpType
AX = mybir.AxisListType

ENGINES = ("sync", "scalar", "vector", "tensor", "gpsimd")

B, S, D = 4, 2048, 2048
HALF = S // 2
DSL = D // 2
NST = 16
CH = 512
NCHUNK = S // CH

BF = ml_dtypes.bfloat16
BIGF = 1.0e30
B_STAGE = None
B_NOSILU = False
B_SKIP = set()


class _Ctr:
    __slots__ = ("h", "n")


class Prog:
    def __init__(self, nc, arena=None, psarena=None):
        self.nc = nc
        self.q = {e: [] for e in ENGINES}
        self.stack = ExitStack()
        self.arena = arena
        self.psarena = psarena
        self.aptr = 0
        self.pptr = 0
        self.ctrs = []
        self.nsem = 0

    def sem(self, name):
        c = _Ctr()
        self.nsem += 1
        c.h = self.stack.enter_context(self.nc.semaphore(f"{name}_{self.nsem}"))
        c.n = 0
        self.ctrs.append(c)
        return c

    def arena_reset(self):
        self.aptr = 0
        self.pptr = 0

    def barrier(self):
        for eng in ENGINES:
            for c in self.ctrs:
                self.wait(eng, c, c.n)

    def sbuf(self, name, shape, dtype):
        if self.arena is None:
            return self.stack.enter_context(self.nc.sbuf_tensor(name, shape, dtype))
        cols = shape[1]
        f32c = cols if dtype == F32 else (cols + 1) // 2
        start = (self.aptr + 15) // 16 * 16
        assert start + f32c <= self.arena.shape[1], (name, shape, start)
        self.aptr = start + f32c
        view = self.arena[0:shape[0], start:start + f32c]
        if dtype == F32:
            return view
        return view.bitcast(BF16)[:, 0:cols]

    def psum(self, name, shape, dtype=F32):
        if self.psarena is None:
            return self.stack.enter_context(self.nc.psum_tensor(name, shape, dtype))
        assert dtype == F32
        cols = shape[1]
        start = (self.pptr + 511) // 512 * 512
        assert start + cols <= self.psarena.shape[1], (name, shape, start)
        self.pptr = start + cols
        return self.psarena[0:shape[0], start:start + cols]

    def wait(self, eng, c, val=None):
        v = c.n if val is None else val
        if v <= 0:
            return
        self.q[eng].append(lambda e, h=c.h, v=v: e.wait_ge(h, v))

    safe = False

    def add(self, eng, fn, inc=None, dma=False):
        k = 16 if dma else 1
        if inc is not None:
            inc.n += k

        def emit(e, fn=fn, inc=inc, k=k):
            inst = fn(e)
            if inc is not None:
                inst.then_inc(inc.h, k)
        self.q[eng].append(emit)
        if self.safe and inc is not None and not dma and eng in ("vector", "scalar"):
            self.q[eng].append(lambda e, h=inc.h, v=inc.n: e.wait_ge(h, v))
        return inc.n if inc is not None else 0

    def finalize(self):
        with self.nc.Block() as block:
            for eng in ENGINES:
                if self.q[eng]:
                    getattr(block, eng)(
                        lambda e, fns=tuple(self.q[eng]): [f(e) for f in fns])
        self.stack.close()



def mamba_drams(nc, kind="ExternalOutput"):
    io = {}
    io["xT"] = nc.dram_tensor("xT", [D, S], F32, kind="ExternalInput").ap()
    io["xpwT"] = nc.dram_tensor("xpwT", [D, 160], F32, kind="ExternalInput").ap()
    io["dtwT"] = nc.dram_tensor("dtwT", [128, DSL], F32, kind="ExternalInput").ap()
    io["dtb"] = nc.dram_tensor("dtb", [128, 8], F32, kind="ExternalInput").ap()
    io["alog"] = nc.dram_tensor("alog", [128, 128], F32, kind="ExternalInput").ap()
    io["dp"] = nc.dram_tensor("dp", [128, 8], F32, kind="ExternalInput").ap()
    io["ypre"] = nc.dram_tensor("ypre", [DSL, S], BF16, kind=kind).ap()
    e32 = np.zeros((32, 32 * 128), dtype=np.float32)
    for n in range(32):
        e32[n, n * 128:(n + 1) * 128] = 1.0
    io["E32"] = nc.inline_tensor(e32.astype(BF), name="E32").ap()
    return io


def build_mamba():
    nc = bass.Bass("TRN2", target_bir_lowering=False)
    io = mamba_drams(nc)
    p = Prog(nc)
    emit_mamba(nc, p, io)
    p.finalize()
    return nc


def emit_mamba(nc, p, io):
    xT = io["xT"]; xpwT = io["xpwT"]; dtwT = io["dtwT"]; dtb = io["dtb"]
    alog = io["alog"]; dp = io["dp"]; ypre = io["ypre"]; E32 = io["E32"]
    sq = p.sem("sq")
    sqx = [p.sem("sqx0"), p.sem("sqx1")]
    sg = p.sem("sg")
    sgo = [p.sem("sgo0"), p.sem("sgo1")]
    st = p.sem("st")
    ss = p.sem("ss")
    sv = p.sem("sv")

    xpw_f = p.sbuf("xpw_f", [128, 16 * 160], F32)
    xpw_b = p.sbuf("xpw_b", [128, 16 * 160], BF16)
    dtw_f = p.sbuf("dtw_f", [128, DSL], F32)
    dtw_b = p.sbuf("dtw_b", [128, DSL], BF16)
    dtb_t = p.sbuf("dtb_t", [128, 8], F32)
    alog_t = p.sbuf("alog_t", [128, 128], F32)
    a_t = p.sbuf("a_t", [128, 128], F32)
    dp_t = p.sbuf("dp_t", [128, 8], F32)
    dp1_t = p.sbuf("dp1_t", [128, 8], F32)
    e32_t = p.sbuf("e32_t", [32, 32 * 128], BF16)
    hstate = p.sbuf("hstate", [128, 128], F32)

    xf32 = [p.sbuf(f"xf32_{i}", [128, 16 * CH], F32) for i in range(2)]
    xbf = [p.sbuf(f"xbf_{i}", [128, 16 * CH], BF16) for i in range(2)]
    dtl_b = p.sbuf("dtl_b", [128, CH], BF16)
    bc16 = p.sbuf("bc16", [32, CH], BF16)
    dt_b = p.sbuf("dt_b", [128, 8 * CH], BF16)
    sp1 = p.sbuf("sp1", [128, CH], F32)
    sp2 = p.sbuf("sp2", [128, CH], F32)
    v_b = p.sbuf("v_b", [128, 8 * CH], BF16)
    b_bc = [p.sbuf(f"b_bc{i}", [128, CH], BF16) for i in range(2)]
    c_bc = [p.sbuf(f"c_bc{i}", [128, CH], BF16) for i in range(2)]
    ab_b = [p.sbuf(f"ab_b{i}", [128, CH], F32) for i in range(2)]
    bx_b = [p.sbuf(f"bx_b{i}", [128, CH], BF16) for i in range(2)]
    h_b = p.sbuf("h_b", [128, CH], BF16)
    tmp_b = p.sbuf("tmp_b", [128, CH], BF16)
    y32 = p.sbuf("y32", [128, 8 * CH], F32)
    out_b = [p.sbuf(f"out_b{i}", [128, 8 * CH], BF16) for i in range(2)]

    ps_xp = p.psum("ps_xp", [128, CH], F32)
    ps_bc = p.psum("ps_bc", [32, CH], F32)
    ps_dt = p.psum("ps_dt", [128, CH], F32)
    ps_bb = [p.psum(f"ps_bb{i}", [128, CH], F32) for i in range(2)]
    ps_cb = [p.psum(f"ps_cb{i}", [128, CH], F32) for i in range(2)]

    p.add("sync", lambda e: e.dma_start(
        out=xpw_f[:, :].rearrange("p (k m) -> p k m", k=16),
        in_=xpwT.rearrange("(k p) m -> p k m", p=128)), inc=sq, dma=True)
    p.add("sync", lambda e: e.dma_start(out=dtw_f[:, :], in_=dtwT), inc=sq, dma=True)
    p.add("sync", lambda e: e.dma_start(out=dtb_t[:, :], in_=dtb), inc=sq, dma=True)
    p.add("sync", lambda e: e.dma_start(out=alog_t[:, :], in_=alog), inc=sq, dma=True)
    p.add("sync", lambda e: e.dma_start(out=dp_t[:, :], in_=dp), inc=sq, dma=True)
    p.add("sync", lambda e: e.dma_start(out=e32_t[:, :], in_=E32), inc=sq, dma=True)
    n_static = sq.n

    xld = []
    for k in range(2):
        p.add("sync", lambda e, k=k: e.dma_start(
            out=xf32[k % 2][:, :].rearrange("p (k2 m) -> p k2 m", k2=16),
            in_=xT[:, k * CH:(k + 1) * CH].rearrange("(k2 p) m -> p k2 m", p=128)),
            inc=sqx[k % 2], dma=True)
        xld.append(sqx[k % 2].n)

    p.wait("vector", sq, n_static)
    p.add("vector", lambda e: e.tensor_copy(xpw_b[:, :], xpw_f[:, :]), inc=sv)
    p.add("vector", lambda e: e.tensor_copy(dtw_b[:, :], dtw_f[:, :]), inc=sv)
    p.add("vector", lambda e: e.tensor_scalar_add(dp1_t[:, :], dp_t[:, :], 1.0), inc=sv)
    p.add("vector", lambda e: e.memset(hstate[:, :], 0.0), inc=sv)
    p.wait("scalar", sq, n_static)
    p.add("scalar", lambda e: e.activation(a_t[:, :], alog_t[:, :], AF.Exp), inc=ss)
    p.wait("vector", ss, ss.n)
    p.add("vector", lambda e: e.tensor_scalar_mul(a_t[:, :], a_t[:, :], -1.0), inc=sv)

    prev = {
        "dtl_evac": 0, "bc_evac": 0, "sp_done": 0, "dt_add": 0,
        "bcast_done": [0, 0], "scan_done": [0, 0], "bx_done": [0, 0],
        "ymul_done": [0, 0], "out_store": [0, 0], "stt_done": [0, 0],
        "xcast": [0, 0],
    }

    for k in range(NCHUNK):
        buf = k % 2
        c0 = k * CH

        p.wait("vector", sqx[buf], xld[k])
        p.add("vector", lambda e, buf=buf: e.tensor_copy(xbf[buf][:, :], xf32[buf][:, :]), inc=sv)
        xcast = sv.n
        prev["xcast"][buf] = xcast

        if k + 2 < NCHUNK:
            p.wait("sync", sv, xcast)
            p.add("sync", lambda e, k=k, buf=buf: e.dma_start(
                out=xf32[buf][:, :].rearrange("p (k2 m) -> p k2 m", k2=16),
                in_=xT[:, (k + 2) * CH:(k + 3) * CH].rearrange("(k2 p) m -> p k2 m", p=128)),
                inc=sqx[buf], dma=True)
            xld.append(sqx[buf].n)

        p.wait("tensor", sv, xcast)
        if prev["dtl_evac"]:
            p.wait("tensor", ss, prev["dtl_evac"])
        for kt in range(16):
            p.add("tensor", lambda e, kt=kt, buf=buf: e.matmul(
                ps_xp[:, :], xpw_b[:, kt * 160:kt * 160 + 128],
                xbf[buf][:, kt * CH:(kt + 1) * CH],
                start=(kt == 0), stop=(kt == 15)), inc=st)
        xp_mm = st.n
        if prev["bc_evac"]:
            p.wait("tensor", ss, prev["bc_evac"])
        for kt in range(16):
            p.add("tensor", lambda e, kt=kt, buf=buf: e.matmul(
                ps_bc[:, :], xpw_b[:, kt * 160 + 128:(kt + 1) * 160],
                xbf[buf][:, kt * CH:(kt + 1) * CH],
                start=(kt == 0), stop=(kt == 15)), inc=st)
        bc_mm = st.n

        p.wait("scalar", st, xp_mm)
        if k > 0:
            p.wait("scalar", st, prev["dtmm_last"])
            p.wait("scalar", st, prev["bcast_last"])
        p.add("scalar", lambda e: e.copy(dtl_b[:, :], ps_xp[:, :]), inc=ss)
        prev["dtl_evac"] = ss.n
        p.wait("scalar", st, bc_mm)
        p.add("scalar", lambda e: e.copy(bc16[:, :], ps_bc[:, :]), inc=ss)
        prev["bc_evac"] = ss.n

        p.wait("tensor", ss, prev["dtl_evac"])
        for pt in range(8):
            if pt > 0 or k > 0:
                p.wait("tensor", ss, prev["sp_done"])
            p.add("tensor", lambda e, pt=pt: e.matmul(
                ps_dt[:, :], dtw_b[:, pt * 128:(pt + 1) * 128], dtl_b[:, :],
                start=True, stop=True), inc=st)
            mm_done = st.n
            p.wait("scalar", st, mm_done)
            if prev["dt_add"]:
                p.wait("scalar", sv, prev["dt_add"])
            p.add("scalar", lambda e, pt=pt: e.activation(
                sp1[:, :], ps_dt[:, :], AF.Abs, bias=dtb_t[:, pt:pt + 1]), inc=ss)
            p.add("scalar", lambda e: e.activation(
                sp2[:, :], sp1[:, :], AF.Exp, scale=-1.0), inc=ss)
            p.add("scalar", lambda e: e.activation(
                sp2[:, :], sp2[:, :], AF.Ln, bias=1.0), inc=ss)
            p.add("scalar", lambda e, pt=pt: e.activation(
                sp1[:, :], ps_dt[:, :], AF.Relu, bias=dtb_t[:, pt:pt + 1]), inc=ss)
            prev["sp_done"] = ss.n
            p.wait("vector", ss, prev["sp_done"])
            p.add("vector", lambda e, pt=pt: e.tensor_add(
                dt_b[:, pt * CH:(pt + 1) * CH], sp1[:, :], sp2[:, :]), inc=sv)
            prev["dt_add"] = sv.n
        prev["dtmm_last"] = st.n

        for pt in range(8):
            p.add("vector", lambda e, pt=pt, buf=buf: e.tensor_mul(
                v_b[:, pt * CH:(pt + 1) * CH],
                dt_b[:, pt * CH:(pt + 1) * CH],
                xbf[buf][:, pt * CH:(pt + 1) * CH]), inc=sv)
        v_done = sv.n

        p.wait("scalar", sv, v_done)
        p.wait("tensor", ss, prev["bc_evac"])

        for n in range(NST):
            par = n % 2
            if prev["bcast_done"][par]:
                p.wait("tensor", ss, prev["bcast_done"][par])
            p.add("tensor", lambda e, n=n, par=par: e.matmul(
                ps_bb[par][:, :], e32_t[:, n * 128:(n + 1) * 128], bc16[:, :],
                start=True, stop=True), inc=st)
            p.add("tensor", lambda e, n=n, par=par: e.matmul(
                ps_cb[par][:, :], e32_t[:, (16 + n) * 128:(17 + n) * 128], bc16[:, :],
                start=True, stop=True), inc=st)
            bb_mm = st.n
            prev["bcast_last"] = st.n
            p.wait("scalar", st, bb_mm)
            if prev["ymul_done"][par]:
                p.wait("scalar", sv, prev["ymul_done"][par])
            p.add("scalar", lambda e, par=par: e.copy(b_bc[par][:, :], ps_bb[par][:, :]), inc=ss)
            p.add("scalar", lambda e, par=par: e.copy(c_bc[par][:, :], ps_cb[par][:, :]), inc=ss)
            prev["bcast_done"][par] = ss.n

            for pt in range(8):
                ipar = (n * 8 + pt) % 2
                col = pt * NST + n
                if prev["scan_done"][ipar]:
                    p.wait("scalar", sv, prev["scan_done"][ipar])
                p.add("scalar", lambda e, pt=pt, col=col, ipar=ipar: e.activation(
                    ab_b[ipar][:, :], dt_b[:, pt * CH:(pt + 1) * CH], AF.Exp,
                    scale=a_t[:, col:col + 1]), inc=ss)
                ab_done = ss.n
                p.wait("vector", ss, prev["bcast_done"][par])
                p.add("vector", lambda e, pt=pt, par=par, ipar=ipar: e.tensor_mul(
                    bx_b[ipar][:, :], v_b[:, pt * CH:(pt + 1) * CH], b_bc[par][:, :]), inc=sv)
                prev["bx_done"][ipar] = sv.n
                p.wait("vector", ss, ab_done)
                p.add("vector", lambda e, col=col, ipar=ipar: e.tensor_tensor_scan(
                    h_b[:, :], ab_b[ipar][:, :], bx_b[ipar][:, :], hstate[:, col:col + 1],
                    ALU.mult, ALU.add), inc=sv)
                prev["scan_done"][ipar] = sv.n
                p.wait("vector", sv, prev["scan_done"][ipar])
                p.add("vector", lambda e, col=col: e.tensor_copy(
                    hstate[:, col:col + 1], h_b[:, CH - 1:CH]), inc=sv)
                if n == 0:
                    p.add("vector", lambda e, pt=pt, par=par: e.tensor_mul(
                        y32[:, pt * CH:(pt + 1) * CH], h_b[:, :], c_bc[par][:, :]), inc=sv)
                else:
                    p.add("vector", lambda e, par=par: e.tensor_mul(
                        tmp_b[:, :], h_b[:, :], c_bc[par][:, :]), inc=sv)
                    p.add("vector", lambda e, pt=pt: e.tensor_add(
                        y32[:, pt * CH:(pt + 1) * CH],
                        y32[:, pt * CH:(pt + 1) * CH], tmp_b[:, :]), inc=sv)
                prev["ymul_done"][par] = sv.n

        if prev["out_store"][buf]:
            p.wait("vector", sgo[buf], prev["out_store"][buf])
        for pt in range(8):
            p.add("vector", lambda e, pt=pt, buf=buf: e.scalar_tensor_tensor(
                out_b[buf][:, pt * CH:(pt + 1) * CH],
                xbf[buf][:, pt * CH:(pt + 1) * CH],
                dp1_t[:, pt:pt + 1],
                y32[:, pt * CH:(pt + 1) * CH],
                ALU.mult, ALU.add), inc=sv)
        stt_done = sv.n
        prev["stt_done"][buf] = stt_done
        p.wait("gpsimd", sv, stt_done)
        p.add("gpsimd", lambda e, c0=c0, buf=buf: e.dma_start(
            out=ypre[:, c0:c0 + CH].rearrange("(k2 p) m -> p k2 m", p=128),
            in_=out_b[buf][:, :].rearrange("p (k2 m) -> p k2 m", k2=8)), inc=sgo[buf], dma=True)
        prev["out_store"][buf] = sgo[buf].n

    p.wait("gpsimd", sgo[0], sgo[0].n)
    p.wait("gpsimd", sgo[1], sgo[1].n)
    return {"stores": [sgo[0], sgo[1]]}


def run_mamba(inp):
    x = inp["L0_attn_x"]
    xpw = inp["L0_attn_x_proj_w"]
    dtw = inp["L0_attn_dt_proj_w"]
    dtbv = inp["L0_attn_dt_proj_b"]
    alog = inp["L0_attn_A_log"]
    dpv = inp["L0_attn_D_param"]

    in_maps = []
    for c in range(8):
        b, j = c // 2, c % 2
        jsl = slice(j * DSL, (j + 1) * DSL)
        if j == 0:
            xb, xpwb = x[b], xpw
        else:
            xb = np.roll(x[b], -DSL, axis=1)
            xpwb = np.roll(xpw, -DSL, axis=1)
        in_maps.append({
            "xT": np.ascontiguousarray(xb.T),
            "xpwT": np.ascontiguousarray(xpwb.T),
            "dtwT": np.ascontiguousarray(dtw[jsl].T),
            "dtb": np.ascontiguousarray(dtbv[jsl].reshape(8, 128).T),
            "alog": np.ascontiguousarray(
                alog[jsl].reshape(8, 128, NST).transpose(1, 0, 2).reshape(128, 128)),
            "dp": np.ascontiguousarray(dpv[jsl].reshape(8, 128).T),
        })
    nc = build_mamba()
    res = run_bass_kernel_spmd(nc, in_maps, core_ids=list(range(8)))
    ypre = np.empty((B, D, S), BF)
    for c in range(8):
        b, j = c // 2, c % 2
        ypre[b, j * DSL:(j + 1) * DSL] = res.results[c]["ypre"]
    return ypre


NEXP = 6
HEXP = 1024
TH = 512


def ffn0_drams(nc, yp_kind="ExternalInput", x2_kind="ExternalOutput"):
    io = {}
    io["yp"] = nc.dram_tensor("yp", [D, HALF], BF16, kind=yp_kind).ap()
    io["wg"] = nc.dram_tensor("wg", [D, 8], F32, kind="ExternalInput").ap()
    io["bg"] = nc.dram_tensor("bg", [1, 8], F32, kind="ExternalInput").ap()
    io["wgate"] = nc.dram_tensor("wgate", [NEXP, D, HEXP], F32, kind="ExternalInput").ap()
    io["wup"] = nc.dram_tensor("wup", [NEXP, D, HEXP], F32, kind="ExternalInput").ap()
    io["wdown"] = nc.dram_tensor("wdown", [NEXP, HEXP, D], F32, kind="ExternalInput").ap()
    io["bgate"] = nc.dram_tensor("bgate", [128, 48], F32, kind="ExternalInput").ap()
    io["bup"] = nc.dram_tensor("bup", [128, 48], F32, kind="ExternalInput").ap()
    io["bdown6"] = nc.dram_tensor("bdown6", [NEXP, D], F32, kind="ExternalInput").ap()
    io["wsup"] = nc.dram_tensor("wsup", [D, D], F32, kind="ExternalInput").ap()
    io["bsup"] = nc.dram_tensor("bsup", [128, 16], F32, kind="ExternalInput").ap()
    io["wsdown"] = nc.dram_tensor("wsdown", [D, D], F32, kind="ExternalInput").ap()
    io["bsdown"] = nc.dram_tensor("bsdown", [128, 16], F32, kind="ExternalInput").ap()
    io["x2o"] = nc.dram_tensor("x2o", [D, HALF], BF16, kind=x2_kind).ap()
    io["ones_col"] = nc.inline_tensor(np.ones((128, 1), np.float32).astype(BF), name="ones_col").ap()
    io["ones_row"] = nc.inline_tensor(np.ones((1, 128), np.float32).astype(BF), name="ones_row").ap()
    io["idf"] = nc.inline_tensor(np.eye(128, dtype=np.float32), name="idf").ap()
    e6np = np.zeros((32, 6 * 128), dtype=np.float32)
    for n in range(6):
        e6np[n, n * 128:(n + 1) * 128] = 1.0
    io["E6B"] = nc.inline_tensor(e6np.astype(BF), name="E6B").ap()
    return io


def build_ffn0():
    nc = bass.Bass("TRN2", target_bir_lowering=False)
    io = ffn0_drams(nc)
    p = Prog(nc)
    emit_ffn0(nc, p, io)
    p.finalize()
    return nc


def emit_ffn0(nc, p, io):
    yp = io["yp"]; wg = io["wg"]; bg = io["bg"]; wgate = io["wgate"]
    wup = io["wup"]; wdown = io["wdown"]; bgate = io["bgate"]; bup = io["bup"]
    bdown6 = io["bdown6"]; wsup = io["wsup"]; bsup = io["bsup"]
    wsdown = io["wsdown"]; bsdown = io["bsdown"]; x2o = io["x2o"]
    ones_col = io["ones_col"]; ones_row = io["ones_row"]; idf = io["idf"]
    E6B = io["E6B"]
    sq = p.sem("sq"); sg = p.sem("sg"); st = p.sem("st"); ss = p.sem("ss"); sv = p.sem("sv")
    sqw = [p.sem("sqw0"), p.sem("sqw1")]

    x1 = p.sbuf("x1", [128, 16 * HALF], BF16)
    sqt = [p.sbuf(f"sqt{i}", [128, HALF], BF16) for i in range(2)]
    hid = p.sbuf("hid", [128, 64 * TH], BF16)
    sg_t = p.sbuf("sg_t", [128, 8 * TH], BF16)
    tmp_u = p.sbuf("tmp_u", [128, TH], BF16)
    tmp_h = p.sbuf("tmp_h", [128, TH], BF16)
    wq_f = [p.sbuf(f"wq_f{i}", [128, 2048], F32) for i in range(2)]
    wq_b = [p.sbuf(f"wq_b{i}", [128, 2048], BF16) for i in range(2)]
    s_t = p.sbuf("s_t", [128, 16 * TH], BF16)
    x2h = [p.sbuf("x2h0", [128, 16 * TH], BF16)] * 2
    wbc = p.sbuf("wbc", [128, 6 * TH], BF16)
    wgb = p.sbuf("wgb", [128, 8 * 16], BF16)
    wgf = p.sbuf("wgf", [128, 8 * 16], F32)
    bgr = p.sbuf("bgr", [1, 8], F32)
    bgr_b = p.sbuf("bgr_b", [1, 8], BF16)
    bg_bc = p.sbuf("bg_bc", [128, 8], F32)
    lg_t = p.sbuf("lg_t", [128, 8 * 8], F32)
    wt_t = p.sbuf("wt_t", [128, 8 * 8], F32)
    w_fm = p.sbuf("w_fm", [32, HALF], F32)
    w_fmb = p.sbuf("w_fmb", [32, HALF], BF16)
    r1 = p.sbuf("r1", [128, 1], F32)
    r2 = p.sbuf("r2", [128, 1], F32)
    r3 = p.sbuf("r3", [128, 1], F32)
    rt1 = p.sbuf("rt1", [128, 8], F32)
    rt2 = p.sbuf("rt2", [128, 8], F32)
    bgate_t = p.sbuf("bgate_t", [128, 48], F32)
    bup_t = p.sbuf("bup_t", [128, 48], F32)
    bdown_t = p.sbuf("bdown_t", [32, D], BF16)
    bsup_t = p.sbuf("bsup_t", [128, 16], F32)
    bsdown_t = p.sbuf("bsdown_t", [128, 16], F32)
    oc_t = p.sbuf("oc_t", [128, 1], BF16)
    or_t = p.sbuf("or_t", [1, 128], BF16)
    idf_t = p.sbuf("idf_t", [128, 128], F32)
    e6_t = p.sbuf("e6_t", [32, 6 * 128], BF16)
    row_mu = p.sbuf("row_mu", [1, HALF], F32)
    row_a = p.sbuf("row_a", [1, HALF], F32)
    row_b = p.sbuf("row_b", [1, HALF], F32)
    rowb_rstd = p.sbuf("rowb_rstd", [1, HALF], BF16)
    rowb_nmr = p.sbuf("rowb_nmr", [1, HALF], BF16)
    eps_row = p.sbuf("eps_row", [1, 1], F32)

    def ln_rows(sum_c0, sq_c0, tlen, invn=1.0 / 2048.0):
        p.add("scalar", lambda e: e.activation(
            row_mu[:, 0:tlen], PS[0:1, sum_c0:sum_c0 + tlen], AF.Copy, scale=invn), inc=ss)
        p.add("scalar", lambda e: e.activation(
            row_a[:, 0:tlen], PS[0:1, sq_c0:sq_c0 + tlen], AF.Copy, scale=invn), inc=ss)
        p.wait("scalar", ss)
        p.add("scalar", lambda e: e.activation(
            row_b[:, 0:tlen], row_mu[:, 0:tlen], AF.Square), inc=ss)
        p.wait("vector", ss)
        p.add("vector", lambda e: e.tensor_sub(
            row_a[:, 0:tlen], row_a[:, 0:tlen], row_b[:, 0:tlen]), inc=sv)
        p.wait("scalar", sv)
        p.add("scalar", lambda e: e.activation(
            row_a[:, 0:tlen], row_a[:, 0:tlen], AF.Ln, bias=eps_row[0:1, 0:1]), inc=ss)
        p.wait("scalar", ss)
        p.add("scalar", lambda e: e.activation(
            row_a[:, 0:tlen], row_a[:, 0:tlen], AF.Exp, scale=-0.5), inc=ss)
        p.wait("vector", ss)
        p.add("vector", lambda e: e.tensor_mul(
            row_b[:, 0:tlen], row_mu[:, 0:tlen], row_a[:, 0:tlen]), inc=sv)
        p.wait("vector", sv)
        p.add("vector", lambda e: e.tensor_scalar_mul(
            row_b[:, 0:tlen], row_b[:, 0:tlen], -1.0), inc=sv)
        p.wait("vector", sv)
        p.add("vector", lambda e: e.tensor_copy(rowb_rstd[:, 0:tlen], row_a[:, 0:tlen]), inc=sv)
        p.add("vector", lambda e: e.tensor_copy(rowb_nmr[:, 0:tlen], row_b[:, 0:tlen]), inc=sv)
        return sv.n
    ab_bc = p.sbuf("ab_bc", [128, HALF], BF16)
    bb_bc = p.sbuf("bb_bc", [128, HALF], BF16)

    PS = p.psum("PS", [128, 4096], F32)


    p.add("sync", lambda e: e.dma_start(
        out=x1[:, :].rearrange("p (k m) -> p k m", k=16),
        in_=yp.rearrange("(k p) m -> p k m", p=128)), inc=sq, dma=True)
    p.add("sync", lambda e: e.dma_start(
        out=wgf[:, :].rearrange("p (k m) -> p k m", k=16),
        in_=wg.rearrange("(k p) m -> p k m", p=128)), inc=sq, dma=True)
    p.add("sync", lambda e: e.dma_start(out=bgr[:, :], in_=bg), inc=sq, dma=True)
    p.add("sync", lambda e: e.dma_start(out=bgate_t[:, :], in_=bgate), inc=sq, dma=True)
    p.add("sync", lambda e: e.dma_start(out=bup_t[:, :], in_=bup), inc=sq, dma=True)
    p.add("sync", lambda e: e.dma_start(out=x2h[0][0:6, 0:4096].bitcast(F32), in_=bdown6), inc=sq, dma=True)
    p.add("sync", lambda e: e.dma_start(out=bsup_t[:, :], in_=bsup), inc=sq, dma=True)
    p.add("sync", lambda e: e.dma_start(out=bsdown_t[:, :], in_=bsdown), inc=sq, dma=True)
    p.add("sync", lambda e: e.dma_start(out=oc_t[:, :], in_=ones_col), inc=sq, dma=True)
    p.add("sync", lambda e: e.dma_start(out=or_t[:, :], in_=ones_row), inc=sq, dma=True)
    p.add("sync", lambda e: e.dma_start(out=idf_t[:, :], in_=idf), inc=sq, dma=True)
    p.add("sync", lambda e: e.dma_start(out=e6_t[:, :], in_=E6B), inc=sq, dma=True)
    n_static = sq.n

    p.wait("vector", sq, n_static)
    p.add("vector", lambda e: e.tensor_copy(wgb[:, :], wgf[:, :]), inc=sv)
    p.add("vector", lambda e: e.memset(bdown_t[:, :], 0.0), inc=sv)
    p.wait("vector", sv)
    p.add("vector", lambda e: e.tensor_copy(bdown_t[0:6, :], x2h[0][0:6, 0:4096].bitcast(F32)), inc=sv)
    p.add("vector", lambda e: e.memset(w_fm[:, :], 0.0), inc=sv)
    bdown_cast = sv.n
    p.add("vector", lambda e: e.tensor_copy(bgr_b[:, :], bgr[:, :]), inc=sv)
    p.add("vector", lambda e: e.memset(eps_row[:, :], 1e-5), inc=sv)
    p.wait("scalar", sv, sv.n)
    prep_v = sv.n

    p.wait("scalar", sq, n_static)
    p.wait("tensor", sq, n_static)
    sq_mm_done = [0, 0]
    for kt in range(16):
        par = kt % 2
        if sq_mm_done[par]:
            p.wait("scalar", st, sq_mm_done[par])
        p.add("scalar", lambda e, kt=kt, par=par: e.activation(
            sqt[par][:, :], x1[:, kt * HALF:(kt + 1) * HALF], AF.Square), inc=ss)
        sq_done = ss.n
        p.wait("tensor", ss, sq_done)
        for fh in range(2):
            p.add("tensor", lambda e, kt=kt, fh=fh: e.matmul(
                PS[0:1, fh * 512:(fh + 1) * 512], oc_t[:, :],
                x1[:, kt * HALF + fh * 512:kt * HALF + (fh + 1) * 512],
                start=(kt == 0), stop=(kt == 15)), inc=st)
            p.add("tensor", lambda e, kt=kt, fh=fh, par=par: e.matmul(
                PS[0:1, 1024 + fh * 512:1024 + (fh + 1) * 512], oc_t[:, :],
                sqt[par][:, fh * 512:(fh + 1) * 512],
                start=(kt == 0), stop=(kt == 15)), inc=st)
        sq_mm_done[par] = st.n
    stats_mm = st.n
    p.wait("scalar", st, stats_mm)
    rows_done = ln_rows(0, 1024, 1024)
    p.wait("tensor", sv, rows_done)
    for fh in range(2):
        p.add("tensor", lambda e, fh=fh: e.matmul(
            PS[:, 2048 + fh * 512:2048 + (fh + 1) * 512], or_t[0:1, :],
            rowb_rstd[0:1, fh * 512:(fh + 1) * 512], start=True, stop=True), inc=st)
        p.add("tensor", lambda e, fh=fh: e.matmul(
            PS[:, 3072 + fh * 512:3072 + (fh + 1) * 512], or_t[0:1, :],
            rowb_nmr[0:1, fh * 512:(fh + 1) * 512], start=True, stop=True), inc=st)
    bc_mm = st.n
    p.wait("scalar", st, bc_mm)
    p.add("scalar", lambda e: e.copy(ab_bc[:, :], PS[:, 2048:3072]), inc=ss)
    p.add("scalar", lambda e: e.copy(bb_bc[:, :], PS[:, 3072:4096]), inc=ss)
    bc_evac = ss.n
    p.wait("vector", ss, bc_evac)
    for kt in range(16):
        p.add("vector", lambda e, kt=kt: e.tensor_mul(
            x1[:, kt * HALF:(kt + 1) * HALF], x1[:, kt * HALF:(kt + 1) * HALF], ab_bc[:, :]), inc=sv)
        p.add("vector", lambda e, kt=kt: e.tensor_add(
            x1[:, kt * HALF:(kt + 1) * HALF], x1[:, kt * HALF:(kt + 1) * HALF], bb_bc[:, :]), inc=sv)
    x1_done = sv.n

    if B_STAGE == "ln":
        p.wait("gpsimd", sv, x1_done)
        p.add("gpsimd", lambda e: e.dma_start(
            out=x2o[:, :].rearrange("(k p) m -> p k m", p=128),
            in_=x1[:, :].rearrange("p (k m) -> p k m", k=16)), inc=sg, dma=True)
        p.wait("gpsimd", sg, sg.n)
        p.finalize()
        return nc

    p.wait("tensor", sv, x1_done)
    p.wait("tensor", ss, bc_evac)
    p.add("tensor", lambda e: e.matmul(
        PS[:, 2560:2568], or_t[0:1, :], bgr_b[0:1, :], start=True, stop=True), inc=st)
    bgbc_mm = st.n
    p.wait("scalar", st, bgbc_mm)
    p.add("scalar", lambda e: e.copy(bg_bc[:, :], PS[:, 2560:2568]), inc=ss)
    bgbc_done = ss.n
    for tt in range(8):
        for kt in range(16):
            p.add("tensor", lambda e, tt=tt, kt=kt: e.matmul(
                PS[:, 2048 + tt * 8:2048 + (tt + 1) * 8],
                x1[:, kt * HALF + tt * 128:kt * HALF + (tt + 1) * 128],
                wgb[:, kt * 8:(kt + 1) * 8],
                start=(kt == 0), stop=(kt == 15)), inc=st)
    lg_mm = st.n
    p.wait("scalar", st, lg_mm)
    p.add("scalar", lambda e: e.copy(lg_t[:, :], PS[:, 2048:2112]), inc=ss)
    lg_evac = ss.n
    if B_STAGE == "router1":
        p.wait("gpsimd", ss, lg_evac)
        p.add("gpsimd", lambda e: e.dma_start(out=x2o[0:128, 0:64], in_=lg_t[:, :]), inc=sg, dma=True)
        p.wait("gpsimd", sg, sg.n)
        p.finalize()
        return nc
    p.wait("vector", ss, lg_evac)
    BIG = 1.0e30
    for tt in range(8):
        lg = lg_t[:, tt * 8:(tt + 1) * 8]
        wt = wt_t[:, tt * 8:(tt + 1) * 8]
        def v(fn):
            p.add("vector", fn, inc=sv)
            p.wait("vector", sv)
        v(lambda e, lg=lg: e.tensor_add(lg, lg, bg_bc[:, :]))
        v(lambda e, lg=lg: e.tensor_scalar_mul(rt1[:, :], lg, -1.0))
        v(lambda e: e.reduce_max(r1[:, :], rt1[:, :], axis=AX.X))
        v(lambda e: e.tensor_single_scalar(rt2[:, :], rt1[:, :], r1[:, 0:1], ALU.is_equal))
        v(lambda e: e.scalar_tensor_tensor(
            rt2[:, :], rt2[:, :], -BIG, rt1[:, :], ALU.mult, ALU.add))
        v(lambda e: e.reduce_max(r2[:, :], rt2[:, :], axis=AX.X))
        v(lambda e: e.tensor_single_scalar(rt2[:, :], rt1[:, :], r2[:, 0:1], ALU.is_lt))
        v(lambda e, lg=lg: e.reduce_max(r1[:, :], lg, axis=AX.X))
        v(lambda e: e.tensor_scalar_mul(r3[:, :], r1[:, :], -1.0))
        sv_r = sv.n
        p.wait("scalar", sv, sv_r)
        p.add("scalar", lambda e, lg=lg, wt=wt: e.activation(wt, lg, AF.Exp, bias=r3[:, 0:1]), inc=ss)
        p.wait("scalar", ss)
        p.wait("vector", ss)
        v(lambda e, wt=wt: e.tensor_mul(wt, wt, rt2[:, :]))
        v(lambda e, wt=wt: e.reduce_sum(r1[:, :], wt, axis=AX.X))
        v(lambda e: e.reciprocal(r2[:, :], r1[:, :]))
        v(lambda e, wt=wt: e.tensor_single_scalar(wt, wt, r2[:, 0:1], ALU.mult))
    wt_done = sv.n
    if B_STAGE == "router2":
        p.wait("gpsimd", sv, wt_done)
        p.add("gpsimd", lambda e: e.dma_start(out=x2o[0:128, 0:64], in_=wt_t[:, :]), inc=sg, dma=True)
        p.wait("gpsimd", sg, sg.n)
        p.finalize()
        return nc
    p.wait("tensor", sv, wt_done)
    rowmoves = 0
    for n in range(6):
        if n > 0:
            p.wait("tensor", ss, ss.n)
        for tt in range(8):
            p.add("tensor", lambda e, n=n, tt=tt: e.matmul(
                PS[0:1, 2048 + tt * 128:2048 + (tt + 1) * 128],
                wt_t[:, tt * 8 + n:tt * 8 + n + 1], idf_t[:, :],
                start=True, stop=True), inc=st)
        p.wait("scalar", st, st.n)
        if n > 0:
            p.wait("scalar", sg, rowmoves)
        p.add("scalar", lambda e: e.copy(row_mu[0:1, :], PS[0:1, 2048:3072]), inc=ss)
        p.wait("gpsimd", ss, ss.n)
        p.add("gpsimd", lambda e, n=n: e.dma_start(
            out=w_fm[n:n + 1, :], in_=row_mu[0:1, :]), inc=sg, dma=True)
        rowmoves = sg.n
    p.wait("vector", sg, rowmoves)
    p.add("vector", lambda e: e.tensor_copy(w_fmb[:, :], w_fm[:, :]), inc=sv)
    wfmb_done = sv.n
    wbc_done = ss.n
    if B_STAGE == "router":
        p.wait("gpsimd", ss, wbc_done)
        p.wait("gpsimd", sv, sv.n)
        p.add("gpsimd", lambda e: e.dma_start(out=x2o[0:8, 0:HALF], in_=w_fm[0:8, :]), inc=sg, dma=True)
        p.wait("gpsimd", sg, sg.n)
        p.finalize()
        return nc

    wload = 0
    wcast = {}
    wmm = {}
    ecv = {0: 0, 1: 0}

    def stream_weight(par, src_ap, ncols, mw):
        nonlocal wload
        if "stream" in B_SKIP:
            if par in wmm and wmm[par]:
                p.wait("vector", st, wmm[par])
            p.add("vector", lambda e, par=par, ncols=ncols: e.memset(
                wq_b[par][:, 0:ncols], 0.01), inc=sv)
            wcast[par] = sv.n
            return sv.n
        if par in wcast and wcast[par]:
            p.wait("sync", sv, prep_v if wcast[par] == "PREP" else wcast[par])
        p.add("sync", lambda e, par=par, src_ap=src_ap, ncols=ncols, mw=mw: e.dma_start(
            out=wq_f[par][:, 0:ncols].rearrange("p (k m) -> p k m", m=mw),
            in_=src_ap), inc=sqw[par], dma=True)
        wload = sqw[par].n
        p.wait("vector", sqw[par], wload)
        if par in wmm and wmm[par]:
            p.wait("vector", st, wmm[par])
        p.add("vector", lambda e, par=par, ncols=ncols: e.tensor_copy(
            wq_b[par][:, 0:ncols], wq_f[par][:, 0:ncols]), inc=sv)
        wcast[par] = sv.n
        return sv.n

    if B_STAGE == "evac1":
        p.wait("scalar", sv, sv.n)
        p.wait("scalar", ss, ss.n)
        if "noact" not in B_SKIP:
            for ht in range(8):
                srcap = (lambda e, ht=ht: x1[:, ht * 512:ht * 512 + TH]) if "nops" in B_SKIP else (lambda e, ht=ht: PS[:, ht * 512:(ht + 1) * 512])
                p.add("scalar", lambda e, ht=ht, srcap=srcap: e.activation(
                    sg_t[:, ht * TH:(ht + 1) * TH], srcap(e, ht),
                    AF.Identity if B_NOSILU else AF.Silu,
                    bias=bgate_t[:, ht:ht + 1]), inc=ss)
        p.wait("gpsimd", ss, ss.n)
        if "simplestore" in B_SKIP:
            p.add("gpsimd", lambda e: e.dma_start(
                out=x2o[0:8, 0:HALF], in_=w_fm[0:8, :]), inc=sg, dma=True)
        else:
            p.add("gpsimd", lambda e: e.dma_start(
                out=x2o[0:1024, 0:TH].rearrange("(k p) m -> p k m", p=128),
                in_=sg_t[:, :].rearrange("p (k m) -> p k m", k=8)), inc=sg, dma=True)
        p.wait("gpsimd", sg, sg.n)
        p.finalize()
        return nc

    x2_store = [0, 0]
    down_last = 0
    wfmb_ok = sv.n
    for th in range(2):
        p.wait("tensor", sv, wfmb_ok)
        if down_last:
            p.wait("tensor", ss, ss.n)
        if "wbc" in B_SKIP:
            p.add("vector", lambda e: e.memset(wbc[:, :], 0.001), inc=sv)
        else:
            for n in range(6):
                p.add("tensor", lambda e, n=n, th=th: e.matmul(
                    PS[:, 2048:2560], e6_t[:, n * 128:(n + 1) * 128],
                    w_fmb[0:32, th * TH:(th + 1) * TH], start=True, stop=True), inc=st)
                wbc_mm = st.n
                p.wait("scalar", st, wbc_mm)
                p.add("scalar", lambda e, n=n: e.copy(
                    wbc[:, n * TH:(n + 1) * TH], PS[:, 2048:2560]), inc=ss)
                p.wait("tensor", ss, ss.n)
        wbc_ready = ss.n
        p.wait("vector", ss, wbc_ready)
        for n in range(NEXP):
            if "gmm" in B_SKIP:
                for ht in range(8):
                    p.add("vector", lambda e, ht=ht: e.memset(PS[:, ht * 512:(ht + 1) * 512], 0.25), inc=sv)
                p.wait("scalar", sv, sv.n)
            else:
                for q in range(8):
                    par = q % 2
                    src = wgate[n, q * 256:(q + 1) * 256, :].rearrange("(k p) m -> p k m", p=128)
                    cdone = stream_weight(par, src, 2048, 1024)
                    p.wait("tensor", sv, cdone)
                    for kl in range(2):
                        kt = q * 2 + kl
                        for ht in range(8):
                            p.add("tensor", lambda e, kl=kl, kt=kt, ht=ht, par=par, th=th: e.matmul(
                                PS[:, ht * 512:(ht + 1) * 512],
                                wq_b[par][:, kl * 1024 + ht * 128:kl * 1024 + (ht + 1) * 128],
                                x1[:, kt * HALF + th * TH:kt * HALF + th * TH + TH],
                                start=(kt == 0), stop=(kt == 15)), inc=st)
                    wmm[par] = st.n
            g_mm = st.n
            p.wait("scalar", st, g_mm)
            if ecv[0]:
                p.wait("scalar", sv, ecv[0])
            for ht in range(8):
                p.add("scalar", lambda e, n=n, ht=ht: e.activation(
                    sg_t[:, ht * TH:(ht + 1) * TH], PS[:, ht * 512:(ht + 1) * 512],
                    AF.Identity if B_NOSILU else AF.Silu,
                    bias=bgate_t[:, n * 8 + ht:n * 8 + ht + 1]), inc=ss)
            g_evac = ss.n
            if B_STAGE == "hid1" and n == 0 and th == 0:
                p.wait("gpsimd", ss, g_evac)
                p.add("gpsimd", lambda e: e.dma_start(
                    out=x2o[0:1024, 0:TH].rearrange("(k p) m -> p k m", p=128),
                    in_=sg_t[:, :].rearrange("p (k m) -> p k m", k=8)), inc=sg, dma=True)
                p.wait("gpsimd", sg, sg.n)
                p.finalize()
                return nc
            p.wait("tensor", ss, g_evac)
            for q in range(8):
                par = q % 2
                src = wup[n, q * 256:(q + 1) * 256, :].rearrange("(k p) m -> p k m", p=128)
                cdone = stream_weight(par, src, 2048, 1024)
                p.wait("tensor", sv, cdone)
                for kl in range(2):
                    kt = q * 2 + kl
                    for ht in range(8):
                        p.add("tensor", lambda e, kl=kl, kt=kt, ht=ht, par=par, th=th: e.matmul(
                            PS[:, ht * 512:(ht + 1) * 512],
                            wq_b[par][:, kl * 1024 + ht * 128:kl * 1024 + (ht + 1) * 128],
                            x1[:, kt * HALF + th * TH:kt * HALF + th * TH + TH],
                            start=(kt == 0), stop=(kt == 15)), inc=st)
                wmm[par] = st.n
            u_mm = st.n
            p.wait("scalar", st, u_mm)
            for ht in range(8):
                p.add("scalar", lambda e, n=n, ht=ht: e.activation(
                    tmp_u[:, :], PS[:, ht * 512:(ht + 1) * 512],
                    AF.Identity, bias=bup_t[:, n * 8 + ht:n * 8 + ht + 1]), inc=ss)
                u_ev = ss.n
                p.wait("vector", ss, u_ev)
                p.add("vector", lambda e, ht=ht: e.tensor_mul(
                    tmp_h[:, :], tmp_u[:, :], sg_t[:, ht * TH:(ht + 1) * TH]), inc=sv)
                p.wait("vector", sv)
                p.add("vector", lambda e, n=n, ht=ht, th=th: e.tensor_mul(
                    hid[:, (n * 8 + ht) * TH:(n * 8 + ht + 1) * TH],
                    tmp_h[:, :], wbc[:, n * TH:(n + 1) * TH]), inc=sv)
                p.wait("scalar", sv, sv.n - 1)
            ecv[0] = sv.n
        for half2 in range(2):
            for q in range(8):
                par = q % 2
                src = wsup[q * 256:(q + 1) * 256, half2 * 1024:(half2 + 1) * 1024].rearrange(
                    "(k p) m -> p k m", p=128)
                cdone = stream_weight(par, src, 2048, 1024)
                p.wait("tensor", sv, cdone)
                for kl in range(2):
                    kt = q * 2 + kl
                    for ht in range(8):
                        p.add("tensor", lambda e, kl=kl, kt=kt, ht=ht, par=par, th=th: e.matmul(
                            PS[:, ht * 512:(ht + 1) * 512],
                            wq_b[par][:, kl * 1024 + ht * 128:kl * 1024 + (ht + 1) * 128],
                            x1[:, kt * HALF + th * TH:kt * HALF + th * TH + TH],
                            start=(kt == 0), stop=(kt == 15)), inc=st)
                wmm[par] = st.n
            su_mm = st.n
            p.wait("scalar", st, su_mm)
            for ht in range(8):
                hcol = half2 * 8 + ht
                p.add("scalar", lambda e, hcol=hcol, ht=ht: e.activation(
                    hid[:, (48 + hcol) * TH:(48 + hcol + 1) * TH],
                    PS[:, ht * 512:(ht + 1) * 512],
                    AF.Silu, bias=bsup_t[:, hcol:hcol + 1]), inc=ss)
            p.wait("tensor", ss, ss.n)
        hid_done_s = ss.n
        hid_done_v = sv.n
        if B_STAGE == "hid" and th == 0:
            p.wait("gpsimd", ss, hid_done_s)
            p.wait("gpsimd", sv, hid_done_v)
            p.add("gpsimd", lambda e: e.dma_start(
                out=x2o[:, 0:TH].rearrange("(k p) m -> p k m", p=128),
                in_=hid[:, 0:16 * TH].rearrange("p (k m) -> p k m", k=16)), inc=sg, dma=True)
            p.wait("gpsimd", sg, sg.n)
            p.finalize()
            return nc

        p.wait("tensor", ss, hid_done_s)
        p.wait("tensor", sv, hid_done_v)
        for dg in range(4):
            if down_last:
                p.wait("tensor", ss, down_last)
            first = True
            for n in range(NEXP):
                for hc in range(2):
                    src = wdown[n, hc * 512:(hc + 1) * 512, dg * 512:(dg + 1) * 512].rearrange(
                        "(k p) m -> p k m", p=128)
                    par = (n * 2 + hc) % 2
                    cdone = stream_weight(par, src, 2048, 512)
                    p.wait("tensor", sv, cdone)
                    for kl in range(4):
                        kt = hc * 4 + kl
                        for dt_ in range(4):
                            p.add("tensor", lambda e, n=n, kl=kl, kt=kt, dt_=dt_, par=par, fr=first and kt == 0: e.matmul(
                                PS[:, dt_ * 512:(dt_ + 1) * 512],
                                wq_b[par][:, kl * 512 + dt_ * 128:kl * 512 + (dt_ + 1) * 128],
                                hid[:, (n * 8 + kt) * TH:(n * 8 + kt + 1) * TH],
                                start=fr, stop=False), inc=st)
                    first = False
                    wmm[par] = st.n
            for qc in range(4):
                src = wsdown[qc * 512:(qc + 1) * 512, dg * 512:(dg + 1) * 512].rearrange(
                    "(k p) m -> p k m", p=128)
                par = qc % 2
                cdone = stream_weight(par, src, 2048, 512)
                p.wait("tensor", sv, cdone)
                for kl in range(4):
                    kt = qc * 4 + kl
                    for dt_ in range(4):
                        p.add("tensor", lambda e, kl=kl, kt=kt, dt_=dt_, par=par: e.matmul(
                            PS[:, dt_ * 512:(dt_ + 1) * 512],
                            wq_b[par][:, kl * 512 + dt_ * 128:kl * 512 + (dt_ + 1) * 128],
                            hid[:, (48 + kt) * TH:(48 + kt + 1) * TH],
                            start=False, stop=False), inc=st)
                wmm[par] = st.n
            for dt_ in range(4):
                dtile = dg * 4 + dt_
                p.add("tensor", lambda e, dt_=dt_, dtile=dtile, th=th: e.matmul(
                    PS[:, dt_ * 512:(dt_ + 1) * 512],
                    bdown_t[0:32, dtile * 128:(dtile + 1) * 128],
                    w_fmb[0:32, th * TH:(th + 1) * TH],
                    start=False, stop=True), inc=st)
            dg_mm = st.n
            p.wait("scalar", st, dg_mm)
            for dt_ in range(4):
                dtile = dg * 4 + dt_
                p.add("scalar", lambda e, dt_=dt_, dtile=dtile: e.activation(
                    tmp_h[:, :], PS[:, dt_ * 512:(dt_ + 1) * 512],
                    AF.Identity, bias=bsdown_t[:, dtile:dtile + 1]), inc=ss)
                ev = ss.n
                p.wait("vector", ss, ev)
                p.add("vector", lambda e, dtile=dtile, th=th: e.tensor_add(
                    s_t[:, dtile * TH:(dtile + 1) * TH], tmp_h[:, :],
                    x1[:, dtile * HALF + th * TH:dtile * HALF + th * TH + TH]), inc=sv)
                p.wait("scalar", sv, sv.n)
            down_last = ss.n
        s_done_v = sv.n

        p.wait("tensor", sv, s_done_v)
        sq2_done = [0, 0]
        for dtile in range(16):
            par = dtile % 2
            if sq2_done[par]:
                p.wait("scalar", st, sq2_done[par])
            p.add("scalar", lambda e, dtile=dtile, par=par: e.activation(
                sqt[par][:, 0:TH], s_t[:, dtile * TH:(dtile + 1) * TH], AF.Square), inc=ss)
            sqd = ss.n
            p.wait("tensor", ss, sqd)
            p.add("tensor", lambda e, dtile=dtile: e.matmul(
                PS[:, 2048:2560][0:1, :], oc_t[:, :],
                s_t[:, dtile * TH:(dtile + 1) * TH],
                start=(dtile == 0), stop=(dtile == 15)), inc=st)
            p.add("tensor", lambda e, dtile=dtile, par=par: e.matmul(
                PS[:, 2560:3072][0:1, :], oc_t[:, :], sqt[par][:, 0:TH],
                start=(dtile == 0), stop=(dtile == 15)), inc=st)
            sq2_done[par] = st.n
        st_mm = st.n
        p.wait("scalar", st, st_mm)
        rws = ln_rows(2048, 2560, TH)
        p.wait("tensor", sv, rws)
        p.add("tensor", lambda e: e.matmul(
            PS[:, 3072:3584], or_t[0:1, :], rowb_rstd[0:1, 0:TH], start=True, stop=True), inc=st)
        p.add("tensor", lambda e: e.matmul(
            PS[:, 3584:4096], or_t[0:1, :], rowb_nmr[0:1, 0:TH], start=True, stop=True), inc=st)
        bmm = st.n
        p.wait("scalar", st, bmm)
        p.add("scalar", lambda e: e.copy(ab_bc[:, 0:TH], PS[:, 3072:3584]), inc=ss)
        p.add("scalar", lambda e: e.copy(bb_bc[:, 0:TH], PS[:, 3584:4096]), inc=ss)
        bev = ss.n
        p.wait("vector", ss, bev)
        if th == 1 and x2_store[0]:
            p.wait("vector", sg, x2_store[0])
        for dtile in range(16):
            p.add("vector", lambda e, dtile=dtile, th=th: e.tensor_mul(
                x2h[th][:, dtile * TH:(dtile + 1) * TH],
                s_t[:, dtile * TH:(dtile + 1) * TH], ab_bc[:, 0:TH]), inc=sv)
            p.add("vector", lambda e, dtile=dtile, th=th: e.tensor_add(
                x2h[th][:, dtile * TH:(dtile + 1) * TH],
                x2h[th][:, dtile * TH:(dtile + 1) * TH], bb_bc[:, 0:TH]), inc=sv)
        x2_done = sv.n
        p.wait("gpsimd", sv, x2_done)
        p.add("gpsimd", lambda e, th=th: e.dma_start(
            out=x2o[:, th * TH:(th + 1) * TH].rearrange("(k p) m -> p k m", p=128),
            in_=x2h[th][:, :].rearrange("p (k m) -> p k m", k=16)), inc=sg, dma=True)
        x2_store[th] = sg.n
        p.wait("tensor", ss, ss.n)

    p.wait("gpsimd", sg, sg.n)
    return {"stores": [sg]}


def run_ffn0(inp, ypre):
    wgv = inp["L0_ffn_Wg"]; bgv = inp["L0_ffn_bg"]
    wgate = inp["L0_ffn_Wgate"]; bgate = inp["L0_ffn_bgate"]
    wupv = inp["L0_ffn_Wup"]; bupv = inp["L0_ffn_bup"]
    wdownv = inp["L0_ffn_Wdown"]; bdownv = inp["L0_ffn_bdown"]
    wsupv = inp["L0_ffn_Ws_up"]; bsupv = inp["L0_ffn_bs_up"]
    wsdownv = inp["L0_ffn_Ws_down"]; bsdownv = inp["L0_ffn_bs_down"]

    base = {
        "wg": np.ascontiguousarray(wgv),
        "bg": np.ascontiguousarray(bgv.reshape(1, 8)),
        "wgate": np.ascontiguousarray(wgate),
        "wup": np.ascontiguousarray(wupv),
        "wdown": np.ascontiguousarray(wdownv),
        "bgate": np.ascontiguousarray(bgate.reshape(6, 8, 128).transpose(2, 0, 1).reshape(128, 48)),
        "bup": np.ascontiguousarray(bupv.reshape(6, 8, 128).transpose(2, 0, 1).reshape(128, 48)),
        "bdown6": np.ascontiguousarray(bdownv),
        "wsup": np.ascontiguousarray(wsupv),
        "bsup": np.ascontiguousarray(bsupv.reshape(16, 128).T),
        "wsdown": np.ascontiguousarray(wsdownv),
        "bsdown": np.ascontiguousarray(bsdownv.reshape(16, 128).T),
    }
    in_maps = []
    for c in range(8):
        b, j = c // 2, c % 2
        m = dict(base)
        m["yp"] = np.ascontiguousarray(ypre[b][:, j * HALF:(j + 1) * HALF])
        in_maps.append(m)
    nc = build_ffn0()
    res = run_bass_kernel_spmd(nc, in_maps, core_ids=list(range(8)))
    x2 = np.empty((B, D, S), BF)
    for c in range(8):
        b, j = c // 2, c % 2
        x2[b][:, j * HALF:(j + 1) * HALF] = res.results[c]["x2o"]
    return x2


NH = 16
DH = 128


def attn_drams(nc, x2f_kind="ExternalInput", x3_kind="ExternalOutput"):
    io = {}
    io["x2f"] = nc.dram_tensor("x2f", [D, S], BF16, kind=x2f_kind).ap()
    io["wqd"] = nc.dram_tensor("wqd", [D, D], F32, kind="ExternalInput").ap()
    io["wkd"] = nc.dram_tensor("wkd", [D, D], F32, kind="ExternalInput").ap()
    io["wvd"] = nc.dram_tensor("wvd", [D, D], F32, kind="ExternalInput").ap()
    io["wod"] = nc.dram_tensor("wod", [D, D], F32, kind="ExternalInput").ap()
    io["bq"] = nc.dram_tensor("bq", [128, 16], F32, kind="ExternalInput").ap()
    io["bk"] = nc.dram_tensor("bk", [128, 16], F32, kind="ExternalInput").ap()
    io["bvr"] = nc.dram_tensor("bvr", [1, D], F32, kind="ExternalInput").ap()
    io["bo"] = nc.dram_tensor("bo", [128, 16], F32, kind="ExternalInput").ap()
    io["lng"] = nc.dram_tensor("lng", [128, 16], F32, kind="ExternalInput").ap()
    io["lnb"] = nc.dram_tensor("lnb", [128, 16], F32, kind="ExternalInput").ap()
    io["minv"] = nc.dram_tensor("minv", [1, S], BF16, kind="ExternalInput").ap()
    io["x3o"] = nc.dram_tensor("x3o", [D, HALF], BF16, kind=x3_kind).ap()
    io["kscr"] = nc.dram_tensor("kscr", [D, S], BF16, kind="Internal").ap()
    io["vscr"] = nc.dram_tensor("vscr", [S, D], BF16, kind="Internal").ap()
    tri_np = np.where(np.arange(128)[None, :] <= np.arange(128)[:, None], 0.0, -1.0e30)
    io["TRI"] = nc.inline_tensor(tri_np.astype(np.float32), name="TRI").ap()
    io["ones_colC"] = nc.inline_tensor(np.ones((128, 1), np.float32).astype(BF), name="ones_colC").ap()
    io["ones_rowC"] = nc.inline_tensor(np.ones((1, 128), np.float32).astype(BF), name="ones_rowC").ap()
    io["idfC"] = nc.inline_tensor(np.eye(128, dtype=np.float32), name="idfC").ap()
    return io


def build_attn():
    nc = bass.Bass("TRN2", target_bir_lowering=False)
    io = attn_drams(nc)
    p = Prog(nc)
    emit_attn(nc, p, io)
    p.finalize()
    return nc


def emit_attn(nc, p, io):
    x2f = io["x2f"]; wqd = io["wqd"]; wkd = io["wkd"]; wvd = io["wvd"]; wod = io["wod"]
    bq = io["bq"]; bk = io["bk"]; bvr = io["bvr"]; bo = io["bo"]
    lng = io["lng"]; lnb = io["lnb"]; minv = io["minv"]; x3o = io["x3o"]
    kscr = io["kscr"]; vscr = io["vscr"]
    TRI = io["TRI"]; ones_col = io["ones_colC"]; ones_row = io["ones_rowC"]; idb = io["idfC"]
    sq = p.sem("sq"); sg = p.sem("sg"); st = p.sem("st"); ss = p.sem("ss"); sv = p.sem("sv")
    sqx = [p.sem("sqx0"), p.sem("sqx1")]
    sqw = [p.sem("sqw0"), p.sem("sqw1")]
    sgk = [p.sem("sgk0"), p.sem("sgk1")]
    sqh = [p.sem("sqh0"), p.sem("sqh1")]

    x2o_t = p.sbuf("x2o_t", [128, 16 * HALF], BF16)
    xn = p.sbuf("xn", [128, 16 * HALF], BF16)
    sqt = [p.sbuf(f"sqt{i}", [128, 512], BF16) for i in range(2)]
    q_sb = p.sbuf("q_sb", [128, 16 * HALF], BF16)
    kv_t = [p.sbuf(f"kv_t{i}", [128, 1024], BF16) for i in range(2)]
    wq_f = [p.sbuf(f"wq_f{i}", [128, 2048], F32) for i in range(2)]
    wq_b = [p.sbuf(f"wq_b{i}", [128, 2048], BF16) for i in range(2)]
    kh = [p.sbuf("kh0", [128, 2048], BF16)] * 2
    vh = [p.sbuf("vh0", [128, 2048], BF16)] * 2
    a_sb = p.sbuf("a_sb", [128, 2048], F32)
    at_sb = p.sbuf("at_sb", [128, 2048], BF16)
    av_sb = xn
    s_t = q_sb
    tri_t = p.sbuf("tri_t", [128, 128], F32)
    minv_b = p.sbuf("minv_b", [1, S], BF16)
    bq_t = p.sbuf("bq_t", [128, 16], F32)
    bk_t = p.sbuf("bk_t", [128, 16], F32)
    bv_b = p.sbuf("bv_b", [1, D], BF16)
    bo_t = p.sbuf("bo_t", [128, 16], F32)
    lng_t = p.sbuf("lng_t", [128, 16], F32)
    lnb_t = p.sbuf("lnb_t", [128, 16], F32)
    oc_t = p.sbuf("oc_t", [128, 1], BF16)
    or_t = p.sbuf("or_t", [1, 128], BF16)
    idb_t = p.sbuf("idb_t", [128, 128], F32)
    m1c = p.sbuf("m1c", [128, 1], F32)
    nm1c = p.sbuf("nm1c", [128, 1], F32)
    lc = p.sbuf("lc", [128, 1], F32)
    rlc = p.sbuf("rlc", [128, 1], F32)
    row_mu = p.sbuf("row_mu", [1, HALF], F32)
    row_a = p.sbuf("row_a", [1, HALF], F32)
    row_b = p.sbuf("row_b", [1, HALF], F32)
    rowb_rstd = p.sbuf("rowb_rstd", [1, HALF], BF16)
    rowb_nmr = p.sbuf("rowb_nmr", [1, HALF], BF16)
    eps_row = p.sbuf("eps_row", [1, 1], F32)
    ab_bc = p.sbuf("ab_bc", [128, HALF], BF16)
    bb_bc = p.sbuf("bb_bc", [128, HALF], BF16)

    PS = p.psum("PS", [128, 3584], F32)
    PST = p.psum("PST", [128, 256], F32)

    SCALE = float(1.0 / np.sqrt(np.float32(DH)))

    for dst, srct in [(bq_t, bq), (bk_t, bk), (bo_t, bo),
                      (lng_t, lng), (lnb_t, lnb), (minv_b, minv),
                      (tri_t, TRI), (oc_t, ones_col), (or_t, ones_row),
                      (idb_t, idb)]:
        p.add("sync", lambda e, dst=dst, srct=srct: e.dma_start(out=dst[:, :], in_=srct), inc=sq, dma=True)
    p.add("sync", lambda e: e.dma_start(out=wq_f[0][0:1, :], in_=bvr), inc=sq, dma=True)
    n_static = sq.n
    p.wait("vector", sq, n_static)
    p.add("vector", lambda e: e.memset(eps_row[:, :], 1e-5), inc=sv)
    p.add("vector", lambda e: e.tensor_copy(bv_b[:, :], wq_f[0][0:1, :]), inc=sv)
    prep_v = sv.n
    p.wait("scalar", sv, prep_v)
    p.wait("tensor", sv, prep_v)

    def ln_rows(sum_c0, sq_c0, tlen, invn=1.0 / 2048.0):
        p.add("scalar", lambda e: e.activation(
            row_mu[:, 0:tlen], PS[0:1, sum_c0:sum_c0 + tlen], AF.Copy, scale=invn), inc=ss)
        p.add("scalar", lambda e: e.activation(
            row_a[:, 0:tlen], PS[0:1, sq_c0:sq_c0 + tlen], AF.Copy, scale=invn), inc=ss)
        p.wait("scalar", ss)
        p.add("scalar", lambda e: e.activation(
            row_b[:, 0:tlen], row_mu[:, 0:tlen], AF.Square), inc=ss)
        p.wait("vector", ss)
        p.add("vector", lambda e: e.tensor_sub(
            row_a[:, 0:tlen], row_a[:, 0:tlen], row_b[:, 0:tlen]), inc=sv)
        p.wait("scalar", sv)
        p.add("scalar", lambda e: e.activation(
            row_a[:, 0:tlen], row_a[:, 0:tlen], AF.Ln, bias=eps_row[0:1, 0:1]), inc=ss)
        p.wait("scalar", ss)
        p.add("scalar", lambda e: e.activation(
            row_a[:, 0:tlen], row_a[:, 0:tlen], AF.Exp, scale=-0.5), inc=ss)
        p.wait("vector", ss)
        p.add("vector", lambda e: e.tensor_mul(
            row_b[:, 0:tlen], row_mu[:, 0:tlen], row_a[:, 0:tlen]), inc=sv)
        p.wait("vector", sv)
        p.add("vector", lambda e: e.tensor_scalar_mul(
            row_b[:, 0:tlen], row_b[:, 0:tlen], -1.0), inc=sv)
        p.wait("vector", sv)
        p.add("vector", lambda e: e.tensor_copy(rowb_rstd[:, 0:tlen], row_a[:, 0:tlen]), inc=sv)
        p.add("vector", lambda e: e.tensor_copy(rowb_nmr[:, 0:tlen], row_b[:, 0:tlen]), inc=sv)
        return sv.n

    wmm = {}
    wcast = {0: "PREP"}
    kv_store = [0, 0]

    def stream_w(par, src_ap):
        if par in wcast and wcast[par]:
            p.wait("sync", sv, prep_v if wcast[par] == "PREP" else wcast[par])
        p.add("sync", lambda e, par=par, src_ap=src_ap: e.dma_start(
            out=wq_f[par][:, :].rearrange("p (k m) -> p k m", k=16), in_=src_ap),
            inc=sqw[par], dma=True)
        p.wait("vector", sqw[par], sqw[par].n)
        if par in wmm and wmm[par]:
            p.wait("vector", st, wmm[par])
        p.add("vector", lambda e, par=par: e.tensor_copy(wq_b[par][:, :], wq_f[par][:, :]), inc=sv)
        wcast[par] = sv.n
        return sv.n

    def stream_w_chunk(par, src_ap):
        if par in wcast and wcast[par]:
            p.wait("sync", sv, prep_v if wcast[par] == "PREP" else wcast[par])
        p.add("sync", lambda e, par=par, src_ap=src_ap: e.dma_start(
            out=wq_f[par][:, 0:512], in_=src_ap), inc=sqw[par], dma=True)
        p.wait("vector", sqw[par], sqw[par].n)
        if par in wmm and wmm[par]:
            p.wait("vector", st, wmm[par])
        p.add("vector", lambda e, par=par: e.tensor_copy(wq_b[par][:, 0:512], wq_f[par][:, 0:512]), inc=sv)
        wcast[par] = sv.n
        return sv.n

    xn_users = 0
    for hf in range(2):
        if xn_users:
            p.wait("sync", st, xn_users)
        p.add("sync", lambda e, hf=hf: e.dma_start(
            out=xn[:, :].rearrange("p (k m) -> p k m", k=16),
            in_=x2f[:, hf * HALF:(hf + 1) * HALF].rearrange("(k p) m -> p k m", p=128)),
            inc=sqx[hf], dma=True)
        p.wait("vector", sqx[hf], sqx[hf].n)
        p.wait("scalar", sqx[hf], sqx[hf].n)
        p.wait("tensor", sqx[hf], sqx[hf].n)

        sqd = [0, 0]
        for kt in range(16):
            for fh in range(2):
                par = (kt * 2 + fh) % 2
                if sqd[par]:
                    p.wait("scalar", st, sqd[par])
                p.add("scalar", lambda e, kt=kt, fh=fh, par=par: e.activation(
                    sqt[par][:, :], xn[:, kt * HALF + fh * 512:kt * HALF + (fh + 1) * 512],
                    AF.Square), inc=ss)
                sd = ss.n
                p.wait("tensor", ss, sd)
                p.add("tensor", lambda e, kt=kt, fh=fh: e.matmul(
                    PS[0:1, 2048 + fh * 512:2048 + (fh + 1) * 512], oc_t[:, :],
                    xn[:, kt * HALF + fh * 512:kt * HALF + (fh + 1) * 512],
                    start=(kt == 0), stop=(kt == 15)), inc=st)
                p.add("tensor", lambda e, kt=kt, fh=fh, par=par: e.matmul(
                    PS[0:1, 1024 + fh * 512:1024 + (fh + 1) * 512], oc_t[:, :],
                    sqt[par][:, :], start=(kt == 0), stop=(kt == 15)), inc=st)
                sqd[par] = st.n
        p.wait("scalar", st, st.n)
        rws = ln_rows(2048, 1024, HALF)
        p.wait("tensor", sv, rws)
        for fh in range(2):
            p.add("tensor", lambda e, fh=fh: e.matmul(
                PS[:, 1024 + fh * 512:1024 + (fh + 1) * 512], or_t[0:1, :],
                rowb_rstd[0:1, fh * 512:(fh + 1) * 512], start=True, stop=True), inc=st)
            p.add("tensor", lambda e, fh=fh: e.matmul(
                PS[:, 2048 + fh * 512:2048 + (fh + 1) * 512], or_t[0:1, :],
                rowb_nmr[0:1, fh * 512:(fh + 1) * 512], start=True, stop=True), inc=st)
        bcmm = st.n
        p.wait("scalar", st, bcmm)
        p.add("scalar", lambda e: e.copy(ab_bc[:, :], PS[:, 1024:2048]), inc=ss)
        p.add("scalar", lambda e: e.copy(bb_bc[:, :], PS[:, 2048:3072]), inc=ss)
        bce = ss.n
        p.wait("vector", ss, bce)
        for kt in range(16):
            p.add("vector", lambda e, kt=kt: e.tensor_mul(
                xn[:, kt * HALF:(kt + 1) * HALF], xn[:, kt * HALF:(kt + 1) * HALF], ab_bc[:, :]), inc=sv)
            p.wait("vector", sv)
            p.add("vector", lambda e, kt=kt: e.tensor_add(
                xn[:, kt * HALF:(kt + 1) * HALF], xn[:, kt * HALF:(kt + 1) * HALF], bb_bc[:, :]), inc=sv)
        zdone = sv.n
        p.wait("scalar", sv, zdone)
        for kt in range(16):
            p.add("scalar", lambda e, kt=kt: e.activation(
                xn[:, kt * HALF:(kt + 1) * HALF], xn[:, kt * HALF:(kt + 1) * HALF],
                AF.Identity, bias=lnb_t[:, kt:kt + 1], scale=lng_t[:, kt:kt + 1]), inc=ss)
        xn_done = ss.n
        p.wait("tensor", ss, xn_done)
        p.wait("vector", ss, xn_done)
        if hf == 1:
            p.add("sync", lambda e: e.dma_start(
                out=x2o_t[:, :].rearrange("p (k m) -> p k m", k=16),
                in_=x2f[:, HALF:S].rearrange("(k p) m -> p k m", p=128)), inc=sq, dma=True)

        for et in range(16):
            par = et % 2
            cd = stream_w(par, wkd[:, et * 128:(et + 1) * 128].rearrange("(k p) m -> p k m", p=128))
            p.wait("tensor", sv, cd)
            if kv_store[par]:
                p.wait("tensor", sgk[par], kv_store[par])
            p.wait("tensor", ss, ss.n)
            for fq in range(2):
                bank = fq
                for kt in range(16):
                    p.add("tensor", lambda e, kt=kt, fq=fq, par=par, bank=bank: e.matmul(
                        PS[:, bank * 512:(bank + 1) * 512],
                        wq_b[par][:, kt * 128:(kt + 1) * 128],
                        xn[:, kt * HALF + fq * 512:kt * HALF + (fq + 1) * 512],
                        start=(kt == 0), stop=(kt == 15)), inc=st)
                wmm[par] = st.n
                p.wait("scalar", st, st.n)
                p.add("scalar", lambda e, et=et, fq=fq, par=par, bank=bank: e.activation(
                    kv_t[par][:, fq * 512:(fq + 1) * 512], PS[:, bank * 512:(bank + 1) * 512],
                    AF.Identity, bias=bk_t[:, et:et + 1]), inc=ss)
            ev = ss.n
            p.wait("gpsimd", ss, ev)
            p.add("gpsimd", lambda e, et=et, hf=hf, par=par: e.dma_start(
                out=kscr[et * 128:(et + 1) * 128, hf * HALF:(hf + 1) * HALF],
                in_=kv_t[par][:, :]), inc=sgk[par], dma=True)
            kv_store[par] = sgk[par].n

        for eh in range(4):
            for ttg in range(2):
                for kt in range(16):
                    par = kt % 2
                    cd = stream_w_chunk(par, wvd[kt * 128:(kt + 1) * 128, eh * 512:(eh + 1) * 512])
                    p.wait("tensor", sv, cd)
                    if kt == 0:
                        p.wait("tensor", ss, ss.n)
                    if kt < 2 and kv_store[(ttg * 4) % 2] == 0:
                        pass
                    for tl in range(4):
                        tt = ttg * 4 + tl
                        p.add("tensor", lambda e, kt=kt, tt=tt, tl=tl, par=par: e.matmul(
                            PS[:, tl * 512:(tl + 1) * 512],
                            xn[:, kt * HALF + tt * 128:kt * HALF + (tt + 1) * 128],
                            wq_b[par][:, 0:512],
                            start=(kt == 0), stop=False), inc=st)
                    wmm[par] = st.n
                for tl in range(4):
                    p.add("tensor", lambda e, tl=tl, eh=eh: e.matmul(
                        PS[:, tl * 512:(tl + 1) * 512], or_t[0:1, :],
                        bv_b[0:1, eh * 512:(eh + 1) * 512], start=False, stop=True), inc=st)
                vmm = st.n
                p.wait("scalar", st, vmm)
                for tl in range(4):
                    tt = ttg * 4 + tl
                    par2 = tl % 2
                    if kv_store[par2]:
                        p.wait("scalar", sgk[par2], kv_store[par2])
                    p.add("scalar", lambda e, tl=tl, par2=par2: e.copy(
                        kv_t[par2][:, 0:512], PS[:, tl * 512:(tl + 1) * 512]), inc=ss)
                    p.wait("gpsimd", ss, ss.n)
                    p.add("gpsimd", lambda e, hf=hf, tt=tt, eh=eh, par2=par2: e.dma_start(
                        out=vscr[hf * HALF + tt * 128:hf * HALF + (tt + 1) * 128,
                                 eh * 512:(eh + 1) * 512],
                        in_=kv_t[par2][:, 0:512]), inc=sgk[par2], dma=True)
                    kv_store[par2] = sgk[par2].n

        if hf == 1:
            for et in range(16):
                par = et % 2
                cd = stream_w(par, wqd[:, et * 128:(et + 1) * 128].rearrange("(k p) m -> p k m", p=128))
                p.wait("tensor", sv, cd)
                p.wait("tensor", ss, ss.n)
                for fq in range(2):
                    bank = fq
                    for kt in range(16):
                        p.add("tensor", lambda e, kt=kt, fq=fq, par=par, bank=bank: e.matmul(
                            PS[:, bank * 512:(bank + 1) * 512],
                            wq_b[par][:, kt * 128:(kt + 1) * 128],
                            xn[:, kt * HALF + fq * 512:kt * HALF + (fq + 1) * 512],
                            start=(kt == 0), stop=(kt == 15)), inc=st)
                    wmm[par] = st.n
                    p.wait("scalar", st, st.n)
                    p.add("scalar", lambda e, et=et, fq=fq, bank=bank: e.activation(
                        q_sb[:, et * HALF + fq * 512:et * HALF + (fq + 1) * 512],
                        PS[:, bank * 512:(bank + 1) * 512],
                        AF.Identity, bias=bq_t[:, et:et + 1]), inc=ss)
        xn_users = st.n

    p.wait("gpsimd", sgk[0], sgk[0].n)
    p.wait("gpsimd", sgk[1], sgk[1].n)
    p.wait("sync", sgk[0], sgk[0].n)
    p.wait("sync", sgk[1], sgk[1].n)
    q_ready = ss.n
    p.wait("tensor", ss, q_ready)

    kvld = [0, 0]
    av_evacs = 0
    at_cp = 0
    for h in range(NH):
        for qt in range(8):
            kv = 1024 + (qt + 1) * 128
            nkb = kv // 128
            par = (h * 8 + qt) % 2
            if kvld[0] or kvld[1]:
                p.wait("sync", st, max(kvld))
            p.add("sync", lambda e, h=h, kv=kv, par=par: e.dma_start(
                out=kh[par][:, 0:kv], in_=kscr[h * 128:(h + 1) * 128, 0:kv]),
                inc=sqh[par], dma=True)
            p.add("sync", lambda e, h=h, kv=kv, par=par: e.dma_start(
                out=vh[par][:, 0:kv].rearrange("p (k m) -> p k m", m=128),
                in_=vscr[0:kv, h * 128:(h + 1) * 128].rearrange("(k p) m -> p k m", p=128)),
                inc=sqh[par], dma=True)
            ld = sqh[par].n
            p.wait("tensor", sqh[par], ld)
            nf = kv
            c0 = 0
            while nf > 0:
                fl = min(512, nf)
                p.add("tensor", lambda e, h=h, qt=qt, c0=c0, fl=fl, par=par: e.matmul(
                    PS[:, c0:c0 + fl],
                    q_sb[:, h * HALF + qt * 128:h * HALF + (qt + 1) * 128],
                    kh[par][:, c0:c0 + fl], start=True, stop=False), inc=st)
                c0 += fl
                nf -= fl
            nf = kv
            c0 = 0
            while nf > 0:
                fl = min(512, nf)
                p.add("tensor", lambda e, c0=c0, fl=fl: e.matmul(
                    PS[:, c0:c0 + fl], or_t[0:1, :], minv_b[0:1, c0:c0 + fl],
                    start=False, stop=True), inc=st)
                c0 += fl
                nf -= fl
            smm = st.n
            p.wait("vector", st, smm)
            p.add("vector", lambda e, kv=kv: e.tensor_add(
                PS[:, kv - 128:kv], PS[:, kv - 128:kv], tri_t[:, :]), inc=sv)
            p.wait("vector", sv)
            p.add("vector", lambda e, kv=kv: e.reduce_max(
                m1c[:, :], PS[:, 0:kv], axis=AX.X), inc=sv)
            p.wait("vector", sv)
            p.add("vector", lambda e: e.tensor_scalar_mul(nm1c[:, :], m1c[:, :], -SCALE), inc=sv)
            expw = sv.n
            p.wait("scalar", sv, expw)
            if at_cp:
                p.wait("scalar", sv, at_cp)
            p.add("scalar", lambda e, kv=kv: e.activation(
                a_sb[:, 0:kv], PS[:, 0:kv], AF.Exp, scale=SCALE, bias=nm1c[:, 0:1],
                accum_out=lc[:, 0:1]), inc=ss)
            expd = ss.n
            p.wait("vector", ss, expd)
            p.add("vector", lambda e: e.reciprocal(rlc[:, :], lc[:, :]), inc=sv)
            p.wait("vector", sv)
            p.add("vector", lambda e, kv=kv: e.tensor_single_scalar(
                a_sb[:, 0:kv], a_sb[:, 0:kv], rlc[:, 0:1], ALU.mult), inc=sv)
            p.wait("vector", sv)
            anorm = sv.n
            p.wait("tensor", sv, anorm)
            for kb in range(nkb):
                pp = kb % 2
                p.add("tensor", lambda e, kb=kb, pp=pp: e.transpose(
                    PST[:, pp * 128:(pp + 1) * 128], a_sb[:, kb * 128:(kb + 1) * 128],
                    idb_t[:, :]), inc=st)
                tpd = st.n
                p.wait("vector", st, tpd)
                p.add("vector", lambda e, kb=kb, pp=pp: e.tensor_copy(
                    at_sb[:, kb * 128:(kb + 1) * 128], PST[:, pp * 128:(pp + 1) * 128]), inc=sv)
                at_cp = sv.n
                p.wait("tensor", sv, at_cp)
                p.add("tensor", lambda e, kb=kb, par=par, nkb=nkb: e.matmul(
                    PS[:, 3072:3200], vh[par][:, kb * 128:(kb + 1) * 128],
                    at_sb[:, kb * 128:(kb + 1) * 128],
                    start=(kb == 0), stop=(kb == nkb - 1)), inc=st)
            avmm = st.n
            p.wait("scalar", st, avmm)
            p.add("scalar", lambda e, h=h, qt=qt: e.copy(
                av_sb[:, h * HALF + qt * 128:h * HALF + (qt + 1) * 128], PS[:, 3072:3200]), inc=ss)
            av_evacs = ss.n
            kvld[par] = st.n

    p.wait("tensor", ss, av_evacs)
    p.wait("tensor", sq, sq.n)
    p.wait("scalar", sgk[0], sgk[0].n)
    p.wait("scalar", sgk[1], sgk[1].n)
    p.wait("vector", sq, sq.n)
    wo_ev = 0
    for dt_ in range(16):
        par = dt_ % 2
        cd = stream_w(par, wod[:, dt_ * 128:(dt_ + 1) * 128].rearrange("(k p) m -> p k m", p=128))
        p.wait("tensor", sv, cd)
        p.wait("tensor", ss, ss.n)
        for fq in range(2):
            bank = fq
            for kt in range(16):
                p.add("tensor", lambda e, kt=kt, fq=fq, par=par, bank=bank: e.matmul(
                    PS[:, bank * 512:(bank + 1) * 512],
                    wq_b[par][:, kt * 128:(kt + 1) * 128],
                    av_sb[:, kt * HALF + fq * 512:kt * HALF + (fq + 1) * 512],
                    start=(kt == 0), stop=(kt == 15)), inc=st)
            wmm[par] = st.n
            p.wait("scalar", st, st.n)
            p.add("scalar", lambda e, dt_=dt_, fq=fq, bank=bank: e.activation(
                kv_t[fq][:, 0:512], PS[:, bank * 512:(bank + 1) * 512],
                AF.Identity, bias=bo_t[:, dt_:dt_ + 1]), inc=ss)
            oev = ss.n
            p.wait("vector", ss, oev)
            p.add("vector", lambda e, dt_=dt_, fq=fq: e.tensor_add(
                s_t[:, dt_ * HALF + fq * 512:dt_ * HALF + (fq + 1) * 512],
                kv_t[fq][:, 0:512],
                x2o_t[:, dt_ * HALF + fq * 512:dt_ * HALF + (fq + 1) * 512]), inc=sv)
        wo_ev = sv.n

    p.wait("scalar", sv, wo_ev)
    p.wait("tensor", sv, wo_ev)
    sqd = [0, 0]
    for kt in range(16):
        for fh in range(2):
            par = (kt * 2 + fh) % 2
            if sqd[par]:
                p.wait("scalar", st, sqd[par])
            p.add("scalar", lambda e, kt=kt, fh=fh, par=par: e.activation(
                sqt[par][:, :], s_t[:, kt * HALF + fh * 512:kt * HALF + (fh + 1) * 512],
                AF.Square), inc=ss)
            sd = ss.n
            p.wait("tensor", ss, sd)
            p.add("tensor", lambda e, kt=kt, fh=fh: e.matmul(
                PS[0:1, 2048 + fh * 512:2048 + (fh + 1) * 512], oc_t[:, :],
                s_t[:, kt * HALF + fh * 512:kt * HALF + (fh + 1) * 512],
                start=(kt == 0), stop=(kt == 15)), inc=st)
            p.add("tensor", lambda e, kt=kt, fh=fh, par=par: e.matmul(
                PS[0:1, 1024 + fh * 512:1024 + (fh + 1) * 512], oc_t[:, :],
                sqt[par][:, :], start=(kt == 0), stop=(kt == 15)), inc=st)
            sqd[par] = st.n
    p.wait("scalar", st, st.n)
    rws = ln_rows(2048, 1024, HALF)
    p.wait("tensor", sv, rws)
    for fh in range(2):
        p.add("tensor", lambda e, fh=fh: e.matmul(
            PS[:, 1024 + fh * 512:1024 + (fh + 1) * 512], or_t[0:1, :],
            rowb_rstd[0:1, fh * 512:(fh + 1) * 512], start=True, stop=True), inc=st)
        p.add("tensor", lambda e, fh=fh: e.matmul(
            PS[:, 2048 + fh * 512:2048 + (fh + 1) * 512], or_t[0:1, :],
            rowb_nmr[0:1, fh * 512:(fh + 1) * 512], start=True, stop=True), inc=st)
    bcmm = st.n
    p.wait("scalar", st, bcmm)
    p.add("scalar", lambda e: e.copy(ab_bc[:, :], PS[:, 1024:2048]), inc=ss)
    p.add("scalar", lambda e: e.copy(bb_bc[:, :], PS[:, 2048:3072]), inc=ss)
    bce = ss.n
    p.wait("vector", ss, bce)
    for kt in range(16):
        p.add("vector", lambda e, kt=kt: e.tensor_mul(
            s_t[:, kt * HALF:(kt + 1) * HALF], s_t[:, kt * HALF:(kt + 1) * HALF], ab_bc[:, :]), inc=sv)
        p.wait("vector", sv)
        p.add("vector", lambda e, kt=kt: e.tensor_add(
            s_t[:, kt * HALF:(kt + 1) * HALF], s_t[:, kt * HALF:(kt + 1) * HALF], bb_bc[:, :]), inc=sv)
    x3_done = sv.n
    p.wait("gpsimd", sv, x3_done)
    p.add("gpsimd", lambda e: e.dma_start(
        out=x3o.rearrange("(k p) m -> p k m", p=128),
        in_=s_t[:, :].rearrange("p (k m) -> p k m", k=16)), inc=sg, dma=True)
    p.wait("gpsimd", sg, sg.n)
    return {"stores": [sg]}


def run_attn(inp, x2):
    wq = inp["L1_attn_Wq"]; wk = inp["L1_attn_Wk"]; wv = inp["L1_attn_Wv"]; wo = inp["L1_attn_Wo"]
    col16 = lambda v: np.ascontiguousarray(v.reshape(16, 128).T)
    base = {
        "wqd": np.ascontiguousarray(wq), "wkd": np.ascontiguousarray(wk),
        "wvd": np.ascontiguousarray(wv), "wod": np.ascontiguousarray(wo),
        "bq": col16(inp["L1_attn_bq"]), "bk": col16(inp["L1_attn_bk"]),
        "bvr": np.ascontiguousarray(inp["L1_attn_bv"].reshape(1, D)),
        "bo": col16(inp["L1_attn_bo"]),
        "lng": col16(inp["L1_attn_ln_g"]), "lnb": col16(inp["L1_attn_ln_b"]),
    }
    in_maps = []
    for c in range(8):
        b, j = c // 2, c % 2
        own = x2[b][:, j * HALF:(j + 1) * HALF]
        oth = x2[b][:, (1 - j) * HALF:(2 - j) * HALF]
        m = dict(base)
        m["x2f"] = np.ascontiguousarray(np.concatenate([oth, own], axis=1))
        mv = np.zeros((1, S), np.float32)
        if j == 0:
            mv[0, 0:HALF] = -1.0e30
        m["minv"] = mv.astype(BF)
        in_maps.append(m)
    nc = build_attn()
    res = run_bass_kernel_spmd(nc, in_maps, core_ids=list(range(8)))
    x3 = np.empty((B, D, S), BF)
    for c in range(8):
        b, j = c // 2, c % 2
        x3[b][:, j * HALF:(j + 1) * HALF] = res.results[c]["x3o"]
    return x3




def ffn1_drams(nc, x3_kind="ExternalInput", x3_ap=None):
    io = {}
    io["x3i"] = x3_ap if x3_ap is not None else nc.dram_tensor(
        "x3i", [D, HALF], BF16, kind=x3_kind).ap()
    io["rwt"] = nc.dram_tensor("rwt", [128, 128], F32, kind="ExternalInput").ap()
    io["shg"] = nc.dram_tensor("shg", [64, D, 128], F32, kind="ExternalInput").ap()
    io["shu"] = nc.dram_tensor("shu", [64, D, 128], F32, kind="ExternalInput").ap()
    io["shd"] = nc.dram_tensor("shd", [16, 8192, 128], F32, kind="ExternalInput").ap()
    io["rg"] = nc.dram_tensor("rg", [8, D, 128], F32, kind="ExternalInput").ap()
    io["ru"] = nc.dram_tensor("ru", [8, D, 128], F32, kind="ExternalInput").ap()
    io["rd"] = nc.dram_tensor("rd", [16, 1024, 128], F32, kind="ExternalInput").ap()
    io["xo"] = nc.dram_tensor("xo", [D, HALF], F32, kind="ExternalOutput").ap()
    io["ones_colD"] = nc.inline_tensor(np.ones((128, 1), np.float32).astype(BF), name="ones_colD").ap()
    io["ones_rowD"] = nc.inline_tensor(np.ones((1, 128), np.float32).astype(BF), name="ones_rowD").ap()
    io["idfD"] = nc.inline_tensor(np.eye(128, dtype=np.float32), name="idfD").ap()
    io["onesfD"] = nc.inline_tensor(np.ones((1, 128), np.float32), name="onesfD").ap()
    return io


def build_ffn1():
    nc = bass.Bass("TRN2", target_bir_lowering=False)
    io = ffn1_drams(nc)
    p = Prog(nc)
    emit_ffn1(nc, p, io)
    p.finalize()
    return nc


def emit_ffn1(nc, p, io):
    x3i = io["x3i"]; rwt = io["rwt"]; shg = io["shg"]; shu = io["shu"]
    shd = io["shd"]; rg = io["rg"]; ru = io["ru"]; rd = io["rd"]; xo = io["xo"]
    ones_col = io["ones_colD"]; ones_row = io["ones_rowD"]
    idf = io["idfD"]; onesf = io["onesfD"]
    sq = p.sem("sq"); sg = p.sem("sg"); st = p.sem("st"); ss = p.sem("ss"); sv = p.sem("sv")
    sqw = [p.sem("sqw0"), p.sem("sqw1")]

    x3_t = p.sbuf("x3_t", [128, 16 * HALF], BF16)
    hid = p.sbuf("hid", [128, 32 * TH], BF16)
    hid_r = p.sbuf("hid_r", [128, 8 * TH], BF16)
    acc = p.sbuf("acc", [128, 16 * TH], F32)
    s_bf = p.sbuf("s_bf", [128, 16 * TH], BF16)
    tmp_g = p.sbuf("tmp_g", [128, TH], BF16)
    tmp_u = p.sbuf("tmp_u", [128, TH], BF16)
    wq_f = [p.sbuf(f"wq_f{i}", [128, 2048], F32) for i in range(2)]
    wq_b = [p.sbuf(f"wq_b{i}", [128, 2048], BF16) for i in range(2)]
    sqt = [p.sbuf(f"sqt{i}", [128, TH], BF16) for i in range(2)]
    rw_f = p.sbuf("rw_f", [128, 128], F32)
    rw_b = p.sbuf("rw_b", [128, 128], BF16)
    lgr = p.sbuf("lgr", [128, 8 * 8], F32)
    combc = p.sbuf("combc", [128, 8], F32)
    comb_row = p.sbuf("comb_row", [1, HALF], F32)
    comb_rb = p.sbuf("comb_rb", [1, HALF], BF16)
    comb_bc = p.sbuf("comb_bc", [128, HALF], BF16)
    r1 = p.sbuf("r1", [128, 1], F32)
    r2 = p.sbuf("r2", [128, 1], F32)
    r3 = p.sbuf("r3", [128, 1], F32)
    rt1 = p.sbuf("rt1", [128, 8], F32)
    rt2 = p.sbuf("rt2", [128, 8], F32)
    oc_t = p.sbuf("oc_t", [128, 1], BF16)
    or_t = p.sbuf("or_t", [1, 128], BF16)
    idf_t = p.sbuf("idf_t", [128, 128], F32)
    row_mu = p.sbuf("row_mu", [1, TH], F32)
    row_a = p.sbuf("row_a", [1, TH], F32)
    row_b = p.sbuf("row_b", [1, TH], F32)
    eps_row = p.sbuf("eps_row", [1, 1], F32)
    onesf_t = p.sbuf("onesf_t", [1, 128], F32)
    ab_bc = p.sbuf("ab_bc", [128, TH], F32)
    bb_bc = p.sbuf("bb_bc", [128, TH], F32)

    PS = p.psum("PS", [128, 4096], F32)

    p.add("sync", lambda e: e.dma_start(
        out=x3_t[:, :].rearrange("p (k m) -> p k m", k=16),
        in_=x3i.rearrange("(k p) m -> p k m", p=128)), inc=sq, dma=True)
    p.add("sync", lambda e: e.dma_start(out=rw_f[:, :], in_=rwt), inc=sq, dma=True)
    p.add("sync", lambda e: e.dma_start(out=oc_t[:, :], in_=ones_col), inc=sq, dma=True)
    p.add("sync", lambda e: e.dma_start(out=or_t[:, :], in_=ones_row), inc=sq, dma=True)
    p.add("sync", lambda e: e.dma_start(out=idf_t[:, :], in_=idf), inc=sq, dma=True)
    p.add("sync", lambda e: e.dma_start(out=onesf_t[:, :], in_=onesf), inc=sq, dma=True)
    n_static = sq.n
    p.wait("vector", sq, n_static)
    p.add("vector", lambda e: e.memset(eps_row[:, :], 1e-5), inc=sv)
    p.add("vector", lambda e: e.tensor_copy(rw_b[:, :], rw_f[:, :]), inc=sv)
    prep_v = sv.n
    p.wait("scalar", sv, prep_v)
    p.wait("tensor", sv, prep_v)

    def ln_rows(sum_c0, sq_c0, tlen, invn=1.0 / 2048.0):
        p.add("scalar", lambda e: e.activation(
            row_mu[:, 0:tlen], PS[0:1, sum_c0:sum_c0 + tlen], AF.Copy, scale=invn), inc=ss)
        p.add("scalar", lambda e: e.activation(
            row_a[:, 0:tlen], PS[0:1, sq_c0:sq_c0 + tlen], AF.Copy, scale=invn), inc=ss)
        p.wait("scalar", ss)
        p.add("scalar", lambda e: e.activation(
            row_b[:, 0:tlen], row_mu[:, 0:tlen], AF.Square), inc=ss)
        p.wait("vector", ss)
        p.add("vector", lambda e: e.tensor_sub(
            row_a[:, 0:tlen], row_a[:, 0:tlen], row_b[:, 0:tlen]), inc=sv)
        p.wait("scalar", sv)
        p.add("scalar", lambda e: e.activation(
            row_a[:, 0:tlen], row_a[:, 0:tlen], AF.Ln, bias=eps_row[0:1, 0:1]), inc=ss)
        p.wait("scalar", ss)
        p.add("scalar", lambda e: e.activation(
            row_a[:, 0:tlen], row_a[:, 0:tlen], AF.Exp, scale=-0.5), inc=ss)
        p.wait("vector", ss)
        p.add("vector", lambda e: e.tensor_mul(
            row_b[:, 0:tlen], row_mu[:, 0:tlen], row_a[:, 0:tlen]), inc=sv)
        p.wait("vector", sv)
        p.add("vector", lambda e: e.tensor_scalar_mul(
            row_b[:, 0:tlen], row_b[:, 0:tlen], -1.0), inc=sv)
        p.wait("vector", sv)
        return sv.n

    wmm = {}
    wcast = {}

    def stream_w(par, src_ap, ncols):
        if par in wcast and wcast[par]:
            p.wait("sync", sv, wcast[par])
        p.add("sync", lambda e, par=par, src_ap=src_ap, ncols=ncols: e.dma_start(
            out=wq_f[par][:, 0:ncols].rearrange("p (k m) -> p k m", m=128),
            in_=src_ap.rearrange("(k p) m -> p k m", p=128)),
            inc=sqw[par], dma=True)
        p.wait("vector", sqw[par], sqw[par].n)
        if par in wmm and wmm[par]:
            p.wait("vector", st, wmm[par])
        p.add("vector", lambda e, par=par, ncols=ncols: e.tensor_copy(
            wq_b[par][:, 0:ncols], wq_f[par][:, 0:ncols]), inc=sv)
        wcast[par] = sv.n
        return sv.n

    for tt in range(8):
        for kt in range(16):
            p.add("tensor", lambda e, tt=tt, kt=kt: e.matmul(
                PS[:, tt * 8:(tt + 1) * 8],
                x3_t[:, kt * HALF + tt * 128:kt * HALF + (tt + 1) * 128],
                rw_b[:, kt * 8:(kt + 1) * 8],
                start=(kt == 0), stop=(kt == 15)), inc=st)
    lgmm = st.n
    p.wait("scalar", st, lgmm)
    p.add("scalar", lambda e: e.copy(lgr[:, :], PS[:, 0:64]), inc=ss)
    lgev = ss.n
    p.wait("vector", ss, lgev)
    for tt in range(8):
        lg = lgr[:, tt * 8:(tt + 1) * 8]

        def v(fn):
            p.add("vector", fn, inc=sv)
            p.wait("vector", sv)
        v(lambda e, lg=lg: e.reduce_max(r1[:, :], lg, axis=AX.X))
        v(lambda e, lg=lg: e.tensor_single_scalar(rt1[:, :], lg, r1[:, 0:1], ALU.is_equal))
        v(lambda e, lg=lg: e.scalar_tensor_tensor(rt2[:, :], rt1[:, :], -BIGF, lg, ALU.mult, ALU.add))
        v(lambda e: e.reduce_max(r2[:, :], rt2[:, :], axis=AX.X))
        v(lambda e: e.tensor_scalar_mul(r3[:, :], r1[:, :], -1.0))
        sv_r = sv.n
        p.wait("scalar", sv, sv_r)
        p.add("scalar", lambda e: e.activation(r2[:, :], r2[:, :], AF.Exp, bias=r3[:, 0:1]), inc=ss)
        p.wait("scalar", ss)
        p.wait("vector", ss)
        v(lambda e: e.tensor_scalar_add(r1[:, :], r2[:, :], 1.0))
        v(lambda e: e.reciprocal(r3[:, :], r1[:, :]))
        v(lambda e, tt=tt: e.tensor_scalar(
            combc[:, tt:tt + 1], r2[:, :][:, 0:1], 1.0, r3[:, 0:1], ALU.add, ALU.mult))
    comb_done = sv.n
    p.wait("tensor", sv, comb_done)
    for tt in range(8):
        p.add("tensor", lambda e, tt=tt: e.matmul(
            PS[0:1, 1024 + tt * 128:1024 + (tt + 1) * 128],
            combc[:, tt:tt + 1], idf_t[:, :], start=True, stop=True), inc=st)
    ctm = st.n
    p.wait("scalar", st, ctm)
    p.add("scalar", lambda e: e.copy(comb_row[:, :], PS[0:1, 1024:2048]), inc=ss)
    p.wait("scalar", ss)
    crow = ss.n
    p.wait("vector", ss, crow)
    p.add("vector", lambda e: e.tensor_copy(comb_rb[:, :], comb_row[:, :]), inc=sv)
    p.wait("vector", sv)
    crb = sv.n
    p.wait("tensor", sv, crb)
    for fh in range(2):
        p.add("tensor", lambda e, fh=fh: e.matmul(
            PS[:, 2048 + fh * 512:2048 + (fh + 1) * 512], or_t[0:1, :],
            comb_rb[0:1, fh * 512:(fh + 1) * 512], start=True, stop=True), inc=st)
    cbm = st.n
    p.wait("scalar", st, cbm)
    p.add("scalar", lambda e: e.copy(comb_bc[:, :], PS[:, 2048:3072]), inc=ss)
    cbc = ss.n
    p.wait("tensor", ss, cbc)
    p.wait("vector", ss, cbc)

    ghu_ev = 0
    acc_upd = 0
    dmm_last = 0
    out_store = 0
    for th in range(2):
        for hgroup in range(2):
            for ht in range(32):
                hta = hgroup * 32 + ht
                par = ht % 2
                bank = par
                cd = stream_w(par, shg[hta], 2048)
                p.wait("tensor", sv, cd)
                p.wait("tensor", ss, ss.n)
                for kt in range(16):
                    p.add("tensor", lambda e, kt=kt, th=th, par=par, bank=bank: e.matmul(
                        PS[:, bank * 512:(bank + 1) * 512],
                        wq_b[par][:, kt * 128:(kt + 1) * 128],
                        x3_t[:, kt * HALF + th * TH:kt * HALF + (th + 1) * TH],
                        start=(kt == 0), stop=(kt == 15)), inc=st)
                wmm[par] = st.n
                gmm = st.n
                p.wait("scalar", st, gmm)
                if ghu_ev:
                    p.wait("scalar", sv, ghu_ev)
                p.add("scalar", lambda e, bank=bank: e.activation(
                    tmp_g[:, :], PS[:, bank * 512:(bank + 1) * 512], AF.Silu), inc=ss)
                gsev = ss.n
                cd = stream_w(par, shu[hta], 2048)
                p.wait("tensor", sv, cd)
                p.wait("tensor", ss, gsev)
                for kt in range(16):
                    p.add("tensor", lambda e, kt=kt, th=th, par=par, bank=bank: e.matmul(
                        PS[:, bank * 512:(bank + 1) * 512],
                        wq_b[par][:, kt * 128:(kt + 1) * 128],
                        x3_t[:, kt * HALF + th * TH:kt * HALF + (th + 1) * TH],
                        start=(kt == 0), stop=(kt == 15)), inc=st)
                wmm[par] = st.n
                umm = st.n
                p.wait("scalar", st, umm)
                p.add("scalar", lambda e, bank=bank: e.activation(
                    tmp_u[:, :], PS[:, bank * 512:(bank + 1) * 512], AF.Identity), inc=ss)
                uev = ss.n
                p.wait("vector", ss, uev)
                p.add("vector", lambda e, ht=ht: e.tensor_mul(
                    hid[:, ht * TH:(ht + 1) * TH], tmp_g[:, :], tmp_u[:, :]), inc=sv)
                ghu_ev = sv.n
            p.wait("tensor", sv, ghu_ev)
            for dt_ in range(16):
                par = dt_ % 2
                bank = 4 + par
                if dmm_last:
                    p.wait("tensor", ss, ss.n)
                    p.wait("tensor", sv, sv.n)
                for hhalf in range(2):
                    cd = stream_w(par, shd[dt_, hgroup * 4096 + hhalf * 2048:
                                           hgroup * 4096 + (hhalf + 1) * 2048], 2048)
                    p.wait("tensor", sv, cd)
                    for kl in range(16):
                        kt = hhalf * 16 + kl
                        p.add("tensor", lambda e, kl=kl, kt=kt, par=par, bank=bank, hhalf=hhalf: e.matmul(
                            PS[:, bank * 512:(bank + 1) * 512],
                            wq_b[par][:, kl * 128:(kl + 1) * 128],
                            hid[:, kt * TH:(kt + 1) * TH],
                            start=(hhalf == 0 and kl == 0), stop=(hhalf == 1 and kl == 15)), inc=st)
                    wmm[par] = st.n
                dmm = st.n
                p.wait("scalar", st, dmm) if False else None
                if hgroup == 0 and th >= 0:
                    p.wait("scalar", st, dmm)
                    p.add("scalar", lambda e, dt_=dt_, bank=bank: e.copy(
                        acc[:, dt_ * TH:(dt_ + 1) * TH], PS[:, bank * 512:(bank + 1) * 512]), inc=ss)
                    acc_upd = ss.n
                else:
                    p.wait("vector", st, dmm)
                    p.add("vector", lambda e, dt_=dt_, bank=bank: e.tensor_add(
                        acc[:, dt_ * TH:(dt_ + 1) * TH], acc[:, dt_ * TH:(dt_ + 1) * TH],
                        PS[:, bank * 512:(bank + 1) * 512]), inc=sv)
                    acc_upd = sv.n
                dmm_last = dmm

        for ht in range(8):
            par = ht % 2
            bank = par
            cd = stream_w(par, rg[ht], 2048)
            p.wait("tensor", sv, cd)
            p.wait("tensor", ss, ss.n)
            for kt in range(16):
                p.add("tensor", lambda e, kt=kt, th=th, par=par, bank=bank: e.matmul(
                    PS[:, bank * 512:(bank + 1) * 512],
                    wq_b[par][:, kt * 128:(kt + 1) * 128],
                    x3_t[:, kt * HALF + th * TH:kt * HALF + (th + 1) * TH],
                    start=(kt == 0), stop=(kt == 15)), inc=st)
            wmm[par] = st.n
            p.wait("scalar", st, st.n)
            if ghu_ev:
                p.wait("scalar", sv, ghu_ev)
            p.add("scalar", lambda e, bank=bank: e.activation(
                tmp_g[:, :], PS[:, bank * 512:(bank + 1) * 512], AF.Silu), inc=ss)
            gsev = ss.n
            cd = stream_w(par, ru[ht], 2048)
            p.wait("tensor", sv, cd)
            p.wait("tensor", ss, gsev)
            for kt in range(16):
                p.add("tensor", lambda e, kt=kt, th=th, par=par, bank=bank: e.matmul(
                    PS[:, bank * 512:(bank + 1) * 512],
                    wq_b[par][:, kt * 128:(kt + 1) * 128],
                    x3_t[:, kt * HALF + th * TH:kt * HALF + (th + 1) * TH],
                    start=(kt == 0), stop=(kt == 15)), inc=st)
            wmm[par] = st.n
            p.wait("scalar", st, st.n)
            p.add("scalar", lambda e, bank=bank: e.activation(
                tmp_u[:, :], PS[:, bank * 512:(bank + 1) * 512], AF.Identity), inc=ss)
            uev = ss.n
            p.wait("vector", ss, uev)
            p.add("vector", lambda e, ht=ht: e.tensor_mul(
                tmp_u[:, :] if False else hid_r[:, ht * TH:(ht + 1) * TH],
                tmp_g[:, :], tmp_u[:, :]), inc=sv)
            p.wait("vector", sv)
            p.add("vector", lambda e, ht=ht, th=th: e.tensor_mul(
                hid_r[:, ht * TH:(ht + 1) * TH], hid_r[:, ht * TH:(ht + 1) * TH],
                comb_bc[:, th * TH:(th + 1) * TH]), inc=sv)
            ghu_ev = sv.n
        p.wait("tensor", sv, ghu_ev)
        for dt_ in range(16):
            par = dt_ % 2
            bank = 4 + par
            cd = stream_w(par, rd[dt_], 1024)
            p.wait("tensor", sv, cd)
            p.wait("tensor", sv, sv.n)
            for kt in range(8):
                p.add("tensor", lambda e, kt=kt, par=par, bank=bank: e.matmul(
                    PS[:, bank * 512:(bank + 1) * 512],
                    wq_b[par][:, kt * 128:(kt + 1) * 128],
                    hid_r[:, kt * TH:(kt + 1) * TH],
                    start=(kt == 0), stop=(kt == 7)), inc=st)
            wmm[par] = st.n
            rdm = st.n
            p.wait("vector", st, rdm)
            p.add("vector", lambda e, dt_=dt_, bank=bank: e.tensor_add(
                acc[:, dt_ * TH:(dt_ + 1) * TH], acc[:, dt_ * TH:(dt_ + 1) * TH],
                PS[:, bank * 512:(bank + 1) * 512]), inc=sv)
            p.wait("vector", sv)
            p.add("vector", lambda e, dt_=dt_, th=th: e.tensor_add(
                acc[:, dt_ * TH:(dt_ + 1) * TH], acc[:, dt_ * TH:(dt_ + 1) * TH],
                x3_t[:, dt_ * HALF + th * TH:dt_ * HALF + (th + 1) * TH]), inc=sv)
            p.wait("vector", sv)
            p.add("vector", lambda e, dt_=dt_: e.tensor_copy(
                s_bf[:, dt_ * TH:(dt_ + 1) * TH], acc[:, dt_ * TH:(dt_ + 1) * TH]), inc=sv)
        s_done = sv.n

        p.wait("scalar", sv, s_done)
        p.wait("tensor", sv, s_done)
        sqd = [0, 0]
        for dt_ in range(16):
            par = dt_ % 2
            if sqd[par]:
                p.wait("scalar", st, sqd[par])
            p.add("scalar", lambda e, dt_=dt_, par=par: e.activation(
                sqt[par][:, :], s_bf[:, dt_ * TH:(dt_ + 1) * TH], AF.Square), inc=ss)
            sd = ss.n
            p.wait("tensor", ss, sd)
            p.add("tensor", lambda e, dt_=dt_: e.matmul(
                PS[0:1, 1024:1536], oc_t[:, :], s_bf[:, dt_ * TH:(dt_ + 1) * TH],
                start=(dt_ == 0), stop=(dt_ == 15)), inc=st)
            p.add("tensor", lambda e, dt_=dt_, par=par: e.matmul(
                PS[0:1, 1536:2048], oc_t[:, :], sqt[par][:, :],
                start=(dt_ == 0), stop=(dt_ == 15)), inc=st)
            sqd[par] = st.n
        p.wait("scalar", st, st.n)
        rws = ln_rows(1024, 1536, TH)
        p.wait("tensor", sv, rws)
        p.add("tensor", lambda e: e.matmul(
            PS[:, 2048:2560], onesf_t[0:1, :], row_a[0:1, 0:TH], start=True, stop=True), inc=st)
        p.add("tensor", lambda e: e.matmul(
            PS[:, 2560:3072], onesf_t[0:1, :], row_b[0:1, 0:TH], start=True, stop=True), inc=st)
        bmm = st.n
        p.wait("scalar", st, bmm)
        p.add("scalar", lambda e: e.copy(ab_bc[:, :], PS[:, 2048:2560]), inc=ss)
        p.add("scalar", lambda e: e.copy(bb_bc[:, :], PS[:, 2560:3072]), inc=ss)
        bev = ss.n
        p.wait("vector", ss, bev)
        if out_store:
            p.wait("vector", sg, out_store)
        for dt_ in range(16):
            p.add("vector", lambda e, dt_=dt_: e.tensor_mul(
                acc[:, dt_ * TH:(dt_ + 1) * TH], acc[:, dt_ * TH:(dt_ + 1) * TH], ab_bc[:, :]), inc=sv)
            p.wait("vector", sv)
            p.add("vector", lambda e, dt_=dt_: e.tensor_add(
                acc[:, dt_ * TH:(dt_ + 1) * TH], acc[:, dt_ * TH:(dt_ + 1) * TH], bb_bc[:, :]), inc=sv)
        x4_done = sv.n
        p.wait("gpsimd", sv, x4_done)
        p.add("gpsimd", lambda e, th=th: e.dma_start(
            out=xo[:, th * TH:(th + 1) * TH].rearrange("(k p) m -> p k m", p=128),
            in_=acc[:, :].rearrange("p (k m) -> p k m", k=16)), inc=sg, dma=True)
        out_store = sg.n
        p.wait("scalar", sg, out_store)
        p.wait("vector", sg, out_store)

    p.wait("gpsimd", sg, sg.n)
    return {"stores": [sg]}


def run_ffn1(inp, x3):
    rw = inp["L1_ffn_router_weight"]
    t64 = lambda w: np.ascontiguousarray(
        w.T.reshape(D, w.shape[0] // 128, 128).transpose(1, 0, 2))
    base = {
        "rwt": np.ascontiguousarray(rw.T.reshape(16, 128, 8).transpose(1, 0, 2).reshape(128, 128)),
        "shg": t64(inp["L1_ffn_sh_gate"]),
        "shu": t64(inp["L1_ffn_sh_up"]),
        "shd": np.ascontiguousarray(
            inp["L1_ffn_sh_down"].T.reshape(8192, 16, 128).transpose(1, 0, 2)),
        "rg": t64(inp["L1_ffn_r_gate"]),
        "ru": t64(inp["L1_ffn_r_up"]),
        "rd": np.ascontiguousarray(
            inp["L1_ffn_r_down"].T.reshape(1024, 16, 128).transpose(1, 0, 2)),
    }
    in_maps = []
    for c in range(8):
        b, j = c // 2, c % 2
        m = dict(base)
        m["x3i"] = np.ascontiguousarray(x3[b][:, j * HALF:(j + 1) * HALF])
        in_maps.append(m)
    nc = build_ffn1()
    res = run_bass_kernel_spmd(nc, in_maps, core_ids=list(range(8)))
    out = np.empty((B, S, D), np.float32)
    for c in range(8):
        b, j = c // 2, c % 2
        out[b, j * HALF:(j + 1) * HALF, :] = res.results[c]["xo"].T
    return out



def build_fused():
    nc = bass.Bass("TRN2", target_bir_lowering=False)
    ioA = mamba_drams(nc, kind="Internal")
    ioB = ffn0_drams(nc, yp_kind="Internal", x2_kind="Internal")
    ioC = attn_drams(nc, x2f_kind="Internal", x3_kind="Internal")
    ioD = ffn1_drams(nc, x3_ap=ioC["x3o"])
    cc1_in = nc.dram_tensor("cc1_in", [DSL, HALF], BF16, kind="Internal").ap()
    cc1_out = nc.dram_tensor("cc1_out", [D, HALF], BF16, kind="Internal").ap()
    cc2_out = nc.dram_tensor("cc2_out", [2 * D, HALF], BF16, kind="Internal").ap()

    ARENA = nc.alloc_sbuf_tensor("ARENA", [128, 52900], F32)
    PSAR = nc.alloc_psum_tensor("PSAR", [128, 4096], F32)
    p = Prog(nc, arena=ARENA, psarena=PSAR)
    groups = [[0, 1], [2, 3], [4, 5], [6, 7]]

    emit_mamba(nc, p, ioA)

    p.barrier()
    p.arena_reset()
    cid = nc.partition_id()
    jv = cid % 2
    ov = 1 - jv
    sb1 = p.sem("sb1")
    scc1 = p.sem("scc1")
    ypre = ioA["ypre"]
    yp = ioB["yp"]
    p.add("sync", lambda e: e.dma_start(
        out=cc1_in, in_=ypre[:, bass.ts(ov, HALF)]), inc=sb1, dma=True)
    p.wait("gpsimd", sb1, sb1.n)
    p.add("gpsimd", lambda e: e.collective_compute(
        "AllGather", ALU.bypass, replica_groups=groups,
        ins=[cc1_in], outs=[cc1_out]), inc=scc1)
    p.wait("sync", scc1, 1)
    p.add("sync", lambda e: e.dma_start(
        out=yp[bass.ts(jv, DSL), :], in_=ypre[:, bass.ts(jv, HALF)]), inc=sb1, dma=True)
    p.add("sync", lambda e: e.dma_start(
        out=yp[bass.ts(ov, DSL), :], in_=cc1_out[bass.ts(ov, DSL), :]), inc=sb1, dma=True)
    p.wait("sync", sb1, sb1.n)
    p.barrier()
    p.arena_reset()

    emit_ffn0(nc, p, ioB)

    p.barrier()
    p.arena_reset()
    sb2 = p.sem("sb2")
    scc2 = p.sem("scc2")
    x2o = ioB["x2o"]
    x2f = ioC["x2f"]
    p.add("gpsimd", lambda e: e.collective_compute(
        "AllGather", ALU.bypass, replica_groups=groups,
        ins=[x2o], outs=[cc2_out]), inc=scc2)
    p.wait("sync", scc2, 1)
    p.add("sync", lambda e: e.dma_start(
        out=x2f[:, 0:HALF], in_=cc2_out[bass.ts(ov, D), :]), inc=sb2, dma=True)
    p.add("sync", lambda e: e.dma_start(
        out=x2f[:, HALF:S], in_=x2o), inc=sb2, dma=True)
    p.wait("sync", sb2, sb2.n)
    p.barrier()
    p.arena_reset()

    emit_attn(nc, p, ioC)

    p.barrier()
    p.arena_reset()

    emit_ffn1(nc, p, ioD)
    p.finalize()
    return nc


def prep_fused(inp, c):
    b, j = c // 2, c % 2
    m = {}
    x = inp["L0_attn_x"]; xpw = inp["L0_attn_x_proj_w"]
    jsl = slice(j * DSL, (j + 1) * DSL)
    if j == 0:
        xb, xpwb = x[b], xpw
    else:
        xb = np.roll(x[b], -DSL, axis=1)
        xpwb = np.roll(xpw, -DSL, axis=1)
    m["xT"] = np.ascontiguousarray(xb.T)
    m["xpwT"] = np.ascontiguousarray(xpwb.T)
    m["dtwT"] = np.ascontiguousarray(inp["L0_attn_dt_proj_w"][jsl].T)
    m["dtb"] = np.ascontiguousarray(inp["L0_attn_dt_proj_b"][jsl].reshape(8, 128).T)
    m["alog"] = np.ascontiguousarray(
        inp["L0_attn_A_log"][jsl].reshape(8, 128, NST).transpose(1, 0, 2).reshape(128, 128))
    m["dp"] = np.ascontiguousarray(inp["L0_attn_D_param"][jsl].reshape(8, 128).T)
    m["wg"] = np.ascontiguousarray(inp["L0_ffn_Wg"])
    m["bg"] = np.ascontiguousarray(inp["L0_ffn_bg"].reshape(1, 8))
    m["wgate"] = np.ascontiguousarray(inp["L0_ffn_Wgate"])
    m["wup"] = np.ascontiguousarray(inp["L0_ffn_Wup"])
    m["wdown"] = np.ascontiguousarray(inp["L0_ffn_Wdown"])
    m["bgate"] = np.ascontiguousarray(
        inp["L0_ffn_bgate"].reshape(6, 8, 128).transpose(2, 0, 1).reshape(128, 48))
    m["bup"] = np.ascontiguousarray(
        inp["L0_ffn_bup"].reshape(6, 8, 128).transpose(2, 0, 1).reshape(128, 48))
    m["bdown6"] = np.ascontiguousarray(inp["L0_ffn_bdown"])
    m["wsup"] = np.ascontiguousarray(inp["L0_ffn_Ws_up"])
    m["bsup"] = np.ascontiguousarray(inp["L0_ffn_bs_up"].reshape(16, 128).T)
    m["wsdown"] = np.ascontiguousarray(inp["L0_ffn_Ws_down"])
    m["bsdown"] = np.ascontiguousarray(inp["L0_ffn_bs_down"].reshape(16, 128).T)
    col16 = lambda v: np.ascontiguousarray(v.reshape(16, 128).T)
    m["wqd"] = np.ascontiguousarray(inp["L1_attn_Wq"])
    m["wkd"] = np.ascontiguousarray(inp["L1_attn_Wk"])
    m["wvd"] = np.ascontiguousarray(inp["L1_attn_Wv"])
    m["wod"] = np.ascontiguousarray(inp["L1_attn_Wo"])
    m["bq"] = col16(inp["L1_attn_bq"]); m["bk"] = col16(inp["L1_attn_bk"])
    m["bvr"] = np.ascontiguousarray(inp["L1_attn_bv"].reshape(1, D))
    m["bo"] = col16(inp["L1_attn_bo"])
    m["lng"] = col16(inp["L1_attn_ln_g"]); m["lnb"] = col16(inp["L1_attn_ln_b"])
    mv = np.zeros((1, S), np.float32)
    if j == 0:
        mv[0, 0:HALF] = -1.0e30
    m["minv"] = mv.astype(BF)
    rw = inp["L1_ffn_router_weight"]
    t64 = lambda w: np.ascontiguousarray(
        w.T.reshape(D, w.shape[0] // 128, 128).transpose(1, 0, 2))
    m["rwt"] = np.ascontiguousarray(rw.T.reshape(16, 128, 8).transpose(1, 0, 2).reshape(128, 128))
    m["shg"] = t64(inp["L1_ffn_sh_gate"])
    m["shu"] = t64(inp["L1_ffn_sh_up"])
    m["shd"] = np.ascontiguousarray(
        inp["L1_ffn_sh_down"].T.reshape(8192, 16, 128).transpose(1, 0, 2))
    m["rg"] = t64(inp["L1_ffn_r_gate"])
    m["ru"] = t64(inp["L1_ffn_r_up"])
    m["rd"] = np.ascontiguousarray(
        inp["L1_ffn_r_down"].T.reshape(1024, 16, 128).transpose(1, 0, 2))
    return m


def kernel(**inputs):
    nc = build_fused()
    in_maps = [prep_fused(inputs, c) for c in range(8)]
    res = run_bass_kernel_spmd(nc, in_maps, core_ids=list(range(8)))
    out = np.empty((B, S, D), np.float32)
    for c in range(8):
        b, j = c // 2, c % 2
        out[b, j * HALF:(j + 1) * HALF, :] = res.results[c]["xo"].T
    return out


def kernel_multilaunch(**inputs):
    ypre = run_mamba(inputs)
    x2 = run_ffn0(inputs, ypre)
    x3 = run_attn(inputs, x2)
    out = run_ffn1(inputs, x3)
    return out




# version 03
# speedup vs baseline: 1.0665x; 1.0665x over previous
import numpy as np
import ml_dtypes
from contextlib import ExitStack

import concourse.bass as bass
import concourse.mybir as mybir
from concourse.bass_utils import run_bass_kernel_spmd

F32 = mybir.dt.float32
BF16 = mybir.dt.bfloat16
AF = mybir.ActivationFunctionType
ALU = mybir.AluOpType
AX = mybir.AxisListType

ENGINES = ("sync", "scalar", "vector", "tensor", "gpsimd")

B, S, D = 4, 2048, 2048
HALF = S // 2
DSL = D // 2
NST = 16
CH = 512
NCHUNK = S // CH

BF = ml_dtypes.bfloat16
BIGF = 1.0e30
B_STAGE = None
B_NOSILU = False
B_SKIP = set()


class _Ctr:
    __slots__ = ("h", "n")


class Prog:
    def __init__(self, nc, arena=None, psarena=None):
        self.nc = nc
        self.q = {e: [] for e in ENGINES}
        self.stack = ExitStack()
        self.arena = arena
        self.psarena = psarena
        self.aptr = 0
        self.pptr = 0
        self.ctrs = []
        self.nsem = 0

    def sem(self, name):
        c = _Ctr()
        self.nsem += 1
        c.h = self.stack.enter_context(self.nc.semaphore(f"{name}_{self.nsem}"))
        c.n = 0
        self.ctrs.append(c)
        return c

    def arena_reset(self):
        self.aptr = 0
        self.pptr = 0

    def barrier(self):
        for eng in ENGINES:
            for c in self.ctrs:
                self.wait(eng, c, c.n)

    def sbuf(self, name, shape, dtype):
        if self.arena is None:
            return self.stack.enter_context(self.nc.sbuf_tensor(name, shape, dtype))
        cols = shape[1]
        f32c = cols if dtype == F32 else (cols + 1) // 2
        start = (self.aptr + 15) // 16 * 16
        assert start + f32c <= self.arena.shape[1], (name, shape, start)
        self.aptr = start + f32c
        view = self.arena[0:shape[0], start:start + f32c]
        if dtype == F32:
            return view
        return view.bitcast(BF16)[:, 0:cols]

    def psum(self, name, shape, dtype=F32):
        if self.psarena is None:
            return self.stack.enter_context(self.nc.psum_tensor(name, shape, dtype))
        assert dtype == F32
        cols = shape[1]
        start = (self.pptr + 511) // 512 * 512
        assert start + cols <= self.psarena.shape[1], (name, shape, start)
        self.pptr = start + cols
        return self.psarena[0:shape[0], start:start + cols]

    def wait(self, eng, c, val=None):
        v = c.n if val is None else val
        if v <= 0:
            return
        self.q[eng].append(lambda e, h=c.h, v=v: e.wait_ge(h, v))

    safe = False

    def add(self, eng, fn, inc=None, dma=False):
        k = 16 if dma else 1
        if inc is not None:
            inc.n += k

        def emit(e, fn=fn, inc=inc, k=k):
            inst = fn(e)
            if inc is not None:
                inst.then_inc(inc.h, k)
        self.q[eng].append(emit)
        if self.safe and inc is not None and not dma and eng in ("vector", "scalar"):
            self.q[eng].append(lambda e, h=inc.h, v=inc.n: e.wait_ge(h, v))
        return inc.n if inc is not None else 0

    def finalize(self):
        with self.nc.Block() as block:
            for eng in ENGINES:
                if self.q[eng]:
                    getattr(block, eng)(
                        lambda e, fns=tuple(self.q[eng]): [f(e) for f in fns])
        self.stack.close()



def mamba_drams(nc, kind="ExternalOutput"):
    io = {}
    io["xT"] = nc.dram_tensor("xT", [D, S], F32, kind="ExternalInput").ap()
    io["xpwT"] = nc.dram_tensor("xpwT", [D, 160], F32, kind="ExternalInput").ap()
    io["dtwT"] = nc.dram_tensor("dtwT", [128, DSL], F32, kind="ExternalInput").ap()
    io["dtb"] = nc.dram_tensor("dtb", [128, 8], F32, kind="ExternalInput").ap()
    io["alog"] = nc.dram_tensor("alog", [128, 128], F32, kind="ExternalInput").ap()
    io["dp"] = nc.dram_tensor("dp", [128, 8], F32, kind="ExternalInput").ap()
    io["ypre"] = nc.dram_tensor("ypre", [DSL, S], BF16, kind=kind).ap()
    e32 = np.zeros((32, 32 * 128), dtype=np.float32)
    for n in range(32):
        e32[n, n * 128:(n + 1) * 128] = 1.0
    io["E32"] = nc.inline_tensor(e32.astype(BF), name="E32").ap()
    return io


def build_mamba():
    nc = bass.Bass("TRN2", target_bir_lowering=False)
    io = mamba_drams(nc)
    p = Prog(nc)
    emit_mamba(nc, p, io)
    p.finalize()
    return nc


def emit_mamba(nc, p, io):
    xT = io["xT"]; xpwT = io["xpwT"]; dtwT = io["dtwT"]; dtb = io["dtb"]
    alog = io["alog"]; dp = io["dp"]; ypre = io["ypre"]; E32 = io["E32"]
    sq = p.sem("sq")
    sqx = [p.sem("sqx0"), p.sem("sqx1")]
    sg = p.sem("sg")
    sgo = [p.sem("sgo0"), p.sem("sgo1")]
    st = p.sem("st")
    ss = p.sem("ss")
    sv = p.sem("sv")

    xpw_f = p.sbuf("xpw_f", [128, 16 * 160], F32)
    xpw_b = p.sbuf("xpw_b", [128, 16 * 160], BF16)
    dtw_f = p.sbuf("dtw_f", [128, DSL], F32)
    dtw_b = p.sbuf("dtw_b", [128, DSL], BF16)
    dtb_t = p.sbuf("dtb_t", [128, 8], F32)
    alog_t = p.sbuf("alog_t", [128, 128], F32)
    a_t = p.sbuf("a_t", [128, 128], F32)
    dp_t = p.sbuf("dp_t", [128, 8], F32)
    dp1_t = p.sbuf("dp1_t", [128, 8], F32)
    e32_t = p.sbuf("e32_t", [32, 32 * 128], BF16)
    hstate = p.sbuf("hstate", [128, 128], F32)

    xf32 = [p.sbuf(f"xf32_{i}", [128, 16 * CH], F32) for i in range(2)]
    xbf = [p.sbuf(f"xbf_{i}", [128, 16 * CH], BF16) for i in range(2)]
    dtl_b = p.sbuf("dtl_b", [128, CH], BF16)
    bc16 = p.sbuf("bc16", [32, CH], BF16)
    dt_b = p.sbuf("dt_b", [128, 8 * CH], BF16)
    sp1 = p.sbuf("sp1", [128, CH], F32)
    sp2 = p.sbuf("sp2", [128, CH], F32)
    v_b = p.sbuf("v_b", [128, 8 * CH], BF16)
    b_bc = [p.sbuf(f"b_bc{i}", [128, CH], BF16) for i in range(2)]
    c_bc = [p.sbuf(f"c_bc{i}", [128, CH], BF16) for i in range(2)]
    ab_b = [p.sbuf(f"ab_b{i}", [128, CH], F32) for i in range(2)]
    bx_b = [p.sbuf(f"bx_b{i}", [128, CH], BF16) for i in range(2)]
    h_b = p.sbuf("h_b", [128, CH], BF16)
    tmp_b = p.sbuf("tmp_b", [128, CH], BF16)
    y32 = p.sbuf("y32", [128, 8 * CH], F32)
    out_b = [p.sbuf(f"out_b{i}", [128, 8 * CH], BF16) for i in range(2)]

    ps_xp = p.psum("ps_xp", [128, CH], F32)
    ps_bc = p.psum("ps_bc", [32, CH], F32)
    ps_dt = p.psum("ps_dt", [128, CH], F32)
    ps_bb = [p.psum(f"ps_bb{i}", [128, CH], F32) for i in range(2)]
    ps_cb = [p.psum(f"ps_cb{i}", [128, CH], F32) for i in range(2)]

    p.add("sync", lambda e: e.dma_start(
        out=xpw_f[:, :].rearrange("p (k m) -> p k m", k=16),
        in_=xpwT.rearrange("(k p) m -> p k m", p=128)), inc=sq, dma=True)
    p.add("sync", lambda e: e.dma_start(out=dtw_f[:, :], in_=dtwT), inc=sq, dma=True)
    p.add("sync", lambda e: e.dma_start(out=dtb_t[:, :], in_=dtb), inc=sq, dma=True)
    p.add("sync", lambda e: e.dma_start(out=alog_t[:, :], in_=alog), inc=sq, dma=True)
    p.add("sync", lambda e: e.dma_start(out=dp_t[:, :], in_=dp), inc=sq, dma=True)
    p.add("sync", lambda e: e.dma_start(out=e32_t[:, :], in_=E32), inc=sq, dma=True)
    n_static = sq.n

    xld = []
    for k in range(2):
        p.add("sync", lambda e, k=k: e.dma_start(
            out=xf32[k % 2][:, :].rearrange("p (k2 m) -> p k2 m", k2=16),
            in_=xT[:, k * CH:(k + 1) * CH].rearrange("(k2 p) m -> p k2 m", p=128)),
            inc=sqx[k % 2], dma=True)
        xld.append(sqx[k % 2].n)

    p.wait("vector", sq, n_static)
    p.add("vector", lambda e: e.tensor_copy(xpw_b[:, :], xpw_f[:, :]), inc=sv)
    p.add("vector", lambda e: e.tensor_copy(dtw_b[:, :], dtw_f[:, :]), inc=sv)
    p.add("vector", lambda e: e.tensor_scalar_add(dp1_t[:, :], dp_t[:, :], 1.0), inc=sv)
    p.add("vector", lambda e: e.memset(hstate[:, :], 0.0), inc=sv)
    p.wait("scalar", sq, n_static)
    p.add("scalar", lambda e: e.activation(a_t[:, :], alog_t[:, :], AF.Exp), inc=ss)
    p.wait("vector", ss, ss.n)
    p.add("vector", lambda e: e.tensor_scalar_mul(a_t[:, :], a_t[:, :], -1.0), inc=sv)

    prev = {
        "dtl_evac": 0, "bc_evac": 0, "sp_done": 0, "dt_add": 0,
        "bcast_done": [0, 0], "scan_done": [0, 0], "bx_done": [0, 0],
        "ymul_done": [0, 0], "out_store": [0, 0], "stt_done": [0, 0],
        "xcast": [0, 0],
    }

    for k in range(NCHUNK):
        buf = k % 2
        c0 = k * CH

        p.wait("vector", sqx[buf], xld[k])
        p.add("vector", lambda e, buf=buf: e.tensor_copy(xbf[buf][:, :], xf32[buf][:, :]), inc=sv)
        xcast = sv.n
        prev["xcast"][buf] = xcast

        if k + 2 < NCHUNK:
            p.wait("sync", sv, xcast)
            p.add("sync", lambda e, k=k, buf=buf: e.dma_start(
                out=xf32[buf][:, :].rearrange("p (k2 m) -> p k2 m", k2=16),
                in_=xT[:, (k + 2) * CH:(k + 3) * CH].rearrange("(k2 p) m -> p k2 m", p=128)),
                inc=sqx[buf], dma=True)
            xld.append(sqx[buf].n)

        p.wait("tensor", sv, xcast)
        if prev["dtl_evac"]:
            p.wait("tensor", ss, prev["dtl_evac"])
        for kt in range(16):
            p.add("tensor", lambda e, kt=kt, buf=buf: e.matmul(
                ps_xp[:, :], xpw_b[:, kt * 160:kt * 160 + 128],
                xbf[buf][:, kt * CH:(kt + 1) * CH],
                start=(kt == 0), stop=(kt == 15)), inc=st)
        xp_mm = st.n
        if prev["bc_evac"]:
            p.wait("tensor", ss, prev["bc_evac"])
        for kt in range(16):
            p.add("tensor", lambda e, kt=kt, buf=buf: e.matmul(
                ps_bc[:, :], xpw_b[:, kt * 160 + 128:(kt + 1) * 160],
                xbf[buf][:, kt * CH:(kt + 1) * CH],
                start=(kt == 0), stop=(kt == 15)), inc=st)
        bc_mm = st.n

        p.wait("scalar", st, xp_mm)
        if k > 0:
            p.wait("scalar", st, prev["dtmm_last"])
            p.wait("scalar", st, prev["bcast_last"])
        p.add("scalar", lambda e: e.copy(dtl_b[:, :], ps_xp[:, :]), inc=ss)
        prev["dtl_evac"] = ss.n
        p.wait("scalar", st, bc_mm)
        p.add("scalar", lambda e: e.copy(bc16[:, :], ps_bc[:, :]), inc=ss)
        prev["bc_evac"] = ss.n

        p.wait("tensor", ss, prev["dtl_evac"])
        for pt in range(8):
            if pt > 0 or k > 0:
                p.wait("tensor", ss, prev["sp_done"])
            p.add("tensor", lambda e, pt=pt: e.matmul(
                ps_dt[:, :], dtw_b[:, pt * 128:(pt + 1) * 128], dtl_b[:, :],
                start=True, stop=True), inc=st)
            mm_done = st.n
            p.wait("scalar", st, mm_done)
            if prev["dt_add"]:
                p.wait("scalar", sv, prev["dt_add"])
            p.add("scalar", lambda e, pt=pt: e.activation(
                sp1[:, :], ps_dt[:, :], AF.Abs, bias=dtb_t[:, pt:pt + 1]), inc=ss)
            p.add("scalar", lambda e: e.activation(
                sp2[:, :], sp1[:, :], AF.Exp, scale=-1.0), inc=ss)
            p.add("scalar", lambda e: e.activation(
                sp2[:, :], sp2[:, :], AF.Ln, bias=1.0), inc=ss)
            p.add("scalar", lambda e, pt=pt: e.activation(
                sp1[:, :], ps_dt[:, :], AF.Relu, bias=dtb_t[:, pt:pt + 1]), inc=ss)
            prev["sp_done"] = ss.n
            p.wait("vector", ss, prev["sp_done"])
            p.add("vector", lambda e, pt=pt: e.tensor_add(
                dt_b[:, pt * CH:(pt + 1) * CH], sp1[:, :], sp2[:, :]), inc=sv)
            prev["dt_add"] = sv.n
        prev["dtmm_last"] = st.n

        for pt in range(8):
            p.add("vector", lambda e, pt=pt, buf=buf: e.tensor_mul(
                v_b[:, pt * CH:(pt + 1) * CH],
                dt_b[:, pt * CH:(pt + 1) * CH],
                xbf[buf][:, pt * CH:(pt + 1) * CH]), inc=sv)
        v_done = sv.n

        p.wait("scalar", sv, v_done)
        p.wait("tensor", ss, prev["bc_evac"])

        for n in range(NST):
            par = n % 2
            if prev["bcast_done"][par]:
                p.wait("tensor", ss, prev["bcast_done"][par])
            p.add("tensor", lambda e, n=n, par=par: e.matmul(
                ps_bb[par][:, :], e32_t[:, n * 128:(n + 1) * 128], bc16[:, :],
                start=True, stop=True), inc=st)
            p.add("tensor", lambda e, n=n, par=par: e.matmul(
                ps_cb[par][:, :], e32_t[:, (16 + n) * 128:(17 + n) * 128], bc16[:, :],
                start=True, stop=True), inc=st)
            bb_mm = st.n
            prev["bcast_last"] = st.n
            p.wait("scalar", st, bb_mm)
            if prev["ymul_done"][par]:
                p.wait("scalar", sv, prev["ymul_done"][par])
            p.add("scalar", lambda e, par=par: e.copy(b_bc[par][:, :], ps_bb[par][:, :]), inc=ss)
            p.add("scalar", lambda e, par=par: e.copy(c_bc[par][:, :], ps_cb[par][:, :]), inc=ss)
            prev["bcast_done"][par] = ss.n

            for pt in range(8):
                ipar = (n * 8 + pt) % 2
                col = pt * NST + n
                if prev["scan_done"][ipar]:
                    p.wait("scalar", sv, prev["scan_done"][ipar])
                p.add("scalar", lambda e, pt=pt, col=col, ipar=ipar: e.activation(
                    ab_b[ipar][:, :], dt_b[:, pt * CH:(pt + 1) * CH], AF.Exp,
                    scale=a_t[:, col:col + 1]), inc=ss)
                ab_done = ss.n
                p.wait("vector", ss, prev["bcast_done"][par])
                p.add("vector", lambda e, pt=pt, par=par, ipar=ipar: e.tensor_mul(
                    bx_b[ipar][:, :], v_b[:, pt * CH:(pt + 1) * CH], b_bc[par][:, :]), inc=sv)
                prev["bx_done"][ipar] = sv.n
                p.wait("vector", ss, ab_done)
                p.add("vector", lambda e, col=col, ipar=ipar: e.tensor_tensor_scan(
                    h_b[:, :], ab_b[ipar][:, :], bx_b[ipar][:, :], hstate[:, col:col + 1],
                    ALU.mult, ALU.add), inc=sv)
                prev["scan_done"][ipar] = sv.n
                p.wait("vector", sv, prev["scan_done"][ipar])
                p.add("vector", lambda e, col=col: e.tensor_copy(
                    hstate[:, col:col + 1], h_b[:, CH - 1:CH]), inc=sv)
                if n == 0:
                    p.add("vector", lambda e, pt=pt, par=par: e.tensor_mul(
                        y32[:, pt * CH:(pt + 1) * CH], h_b[:, :], c_bc[par][:, :]), inc=sv)
                else:
                    p.add("vector", lambda e, par=par: e.tensor_mul(
                        tmp_b[:, :], h_b[:, :], c_bc[par][:, :]), inc=sv)
                    p.add("vector", lambda e, pt=pt: e.tensor_add(
                        y32[:, pt * CH:(pt + 1) * CH],
                        y32[:, pt * CH:(pt + 1) * CH], tmp_b[:, :]), inc=sv)
                prev["ymul_done"][par] = sv.n

        if prev["out_store"][buf]:
            p.wait("vector", sgo[buf], prev["out_store"][buf])
        for pt in range(8):
            p.add("vector", lambda e, pt=pt, buf=buf: e.scalar_tensor_tensor(
                out_b[buf][:, pt * CH:(pt + 1) * CH],
                xbf[buf][:, pt * CH:(pt + 1) * CH],
                dp1_t[:, pt:pt + 1],
                y32[:, pt * CH:(pt + 1) * CH],
                ALU.mult, ALU.add), inc=sv)
        stt_done = sv.n
        prev["stt_done"][buf] = stt_done
        p.wait("gpsimd", sv, stt_done)
        p.add("gpsimd", lambda e, c0=c0, buf=buf: e.dma_start(
            out=ypre[:, c0:c0 + CH].rearrange("(k2 p) m -> p k2 m", p=128),
            in_=out_b[buf][:, :].rearrange("p (k2 m) -> p k2 m", k2=8)), inc=sgo[buf], dma=True)
        prev["out_store"][buf] = sgo[buf].n

    p.wait("gpsimd", sgo[0], sgo[0].n)
    p.wait("gpsimd", sgo[1], sgo[1].n)
    return {"stores": [sgo[0], sgo[1]]}


def run_mamba(inp):
    x = inp["L0_attn_x"]
    xpw = inp["L0_attn_x_proj_w"]
    dtw = inp["L0_attn_dt_proj_w"]
    dtbv = inp["L0_attn_dt_proj_b"]
    alog = inp["L0_attn_A_log"]
    dpv = inp["L0_attn_D_param"]

    in_maps = []
    for c in range(8):
        b, j = c // 2, c % 2
        jsl = slice(j * DSL, (j + 1) * DSL)
        if j == 0:
            xb, xpwb = x[b], xpw
        else:
            xb = np.roll(x[b], -DSL, axis=1)
            xpwb = np.roll(xpw, -DSL, axis=1)
        in_maps.append({
            "xT": np.ascontiguousarray(xb.T),
            "xpwT": np.ascontiguousarray(xpwb.T),
            "dtwT": np.ascontiguousarray(dtw[jsl].T),
            "dtb": np.ascontiguousarray(dtbv[jsl].reshape(8, 128).T),
            "alog": np.ascontiguousarray(
                alog[jsl].reshape(8, 128, NST).transpose(1, 0, 2).reshape(128, 128)),
            "dp": np.ascontiguousarray(dpv[jsl].reshape(8, 128).T),
        })
    nc = build_mamba()
    res = run_bass_kernel_spmd(nc, in_maps, core_ids=list(range(8)))
    ypre = np.empty((B, D, S), BF)
    for c in range(8):
        b, j = c // 2, c % 2
        ypre[b, j * DSL:(j + 1) * DSL] = res.results[c]["ypre"]
    return ypre


NEXP = 6
HEXP = 1024
TH = 512


def ffn0_drams(nc, yp_kind="ExternalInput", x2_kind="ExternalOutput"):
    io = {}
    io["yp"] = nc.dram_tensor("yp", [D, HALF], BF16, kind=yp_kind).ap()
    io["wg"] = nc.dram_tensor("wg", [D, 8], F32, kind="ExternalInput").ap()
    io["bg"] = nc.dram_tensor("bg", [1, 8], F32, kind="ExternalInput").ap()
    io["wgate"] = nc.dram_tensor("wgate", [NEXP, D, HEXP], F32, kind="ExternalInput").ap()
    io["wup"] = nc.dram_tensor("wup", [NEXP, D, HEXP], F32, kind="ExternalInput").ap()
    io["wdown"] = nc.dram_tensor("wdown", [NEXP, HEXP, D], F32, kind="ExternalInput").ap()
    io["bgate"] = nc.dram_tensor("bgate", [128, 48], F32, kind="ExternalInput").ap()
    io["bup"] = nc.dram_tensor("bup", [128, 48], F32, kind="ExternalInput").ap()
    io["bdown6"] = nc.dram_tensor("bdown6", [NEXP, D], F32, kind="ExternalInput").ap()
    io["wsup"] = nc.dram_tensor("wsup", [D, D], F32, kind="ExternalInput").ap()
    io["bsup"] = nc.dram_tensor("bsup", [128, 16], F32, kind="ExternalInput").ap()
    io["wsdown"] = nc.dram_tensor("wsdown", [D, D], F32, kind="ExternalInput").ap()
    io["bsdown"] = nc.dram_tensor("bsdown", [128, 16], F32, kind="ExternalInput").ap()
    io["x2o"] = nc.dram_tensor("x2o", [D, HALF], BF16, kind=x2_kind).ap()
    io["ones_col"] = nc.inline_tensor(np.ones((128, 1), np.float32).astype(BF), name="ones_col").ap()
    io["ones_row"] = nc.inline_tensor(np.ones((1, 128), np.float32).astype(BF), name="ones_row").ap()
    io["idf"] = nc.inline_tensor(np.eye(128, dtype=np.float32), name="idf").ap()
    e6np = np.zeros((32, 6 * 128), dtype=np.float32)
    for n in range(6):
        e6np[n, n * 128:(n + 1) * 128] = 1.0
    io["E6B"] = nc.inline_tensor(e6np.astype(BF), name="E6B").ap()
    return io


def build_ffn0():
    nc = bass.Bass("TRN2", target_bir_lowering=False)
    io = ffn0_drams(nc)
    p = Prog(nc)
    emit_ffn0(nc, p, io)
    p.finalize()
    return nc


def emit_ffn0(nc, p, io):
    yp = io["yp"]; wg = io["wg"]; bg = io["bg"]; wgate = io["wgate"]
    wup = io["wup"]; wdown = io["wdown"]; bgate = io["bgate"]; bup = io["bup"]
    bdown6 = io["bdown6"]; wsup = io["wsup"]; bsup = io["bsup"]
    wsdown = io["wsdown"]; bsdown = io["bsdown"]; x2o = io["x2o"]
    ones_col = io["ones_col"]; ones_row = io["ones_row"]; idf = io["idf"]
    E6B = io["E6B"]
    sq = p.sem("sq"); sg = p.sem("sg"); st = p.sem("st"); ss = p.sem("ss"); sv = p.sem("sv")
    sqw = [p.sem("sqw0"), p.sem("sqw1")]

    x1 = p.sbuf("x1", [128, 16 * HALF], BF16)
    sqt = [p.sbuf(f"sqt{i}", [128, HALF], BF16) for i in range(2)]
    hid = p.sbuf("hid", [128, 64 * TH], BF16)
    sg_t = p.sbuf("sg_t", [128, 8 * TH], BF16)
    tmp_u = p.sbuf("tmp_u", [128, TH], BF16)
    tmp_h = p.sbuf("tmp_h", [128, TH], BF16)
    wq_f = [p.sbuf(f"wq_f{i}", [128, 2048], F32) for i in range(2)]
    wq_b = [p.sbuf(f"wq_b{i}", [128, 2048], BF16) for i in range(2)]
    s_t = p.sbuf("s_t", [128, 16 * TH], BF16)
    x2h = [p.sbuf("x2h0", [128, 16 * TH], BF16)] * 2
    wbc = p.sbuf("wbc", [128, 6 * TH], BF16)
    wgb = p.sbuf("wgb", [128, 8 * 16], BF16)
    wgf = p.sbuf("wgf", [128, 8 * 16], F32)
    bgr = p.sbuf("bgr", [1, 8], F32)
    bgr_b = p.sbuf("bgr_b", [1, 8], BF16)
    bg_bc = p.sbuf("bg_bc", [128, 8], F32)
    lg_t = p.sbuf("lg_t", [128, 8 * 8], F32)
    wt_t = p.sbuf("wt_t", [128, 8 * 8], F32)
    w_fm = p.sbuf("w_fm", [32, HALF], F32)
    w_fmb = p.sbuf("w_fmb", [32, HALF], BF16)
    r1 = p.sbuf("r1", [128, 1], F32)
    r2 = p.sbuf("r2", [128, 1], F32)
    r3 = p.sbuf("r3", [128, 1], F32)
    rt1 = p.sbuf("rt1", [128, 8], F32)
    rt2 = p.sbuf("rt2", [128, 8], F32)
    bgate_t = p.sbuf("bgate_t", [128, 48], F32)
    bup_t = p.sbuf("bup_t", [128, 48], F32)
    bdown_t = p.sbuf("bdown_t", [32, D], BF16)
    bsup_t = p.sbuf("bsup_t", [128, 16], F32)
    bsdown_t = p.sbuf("bsdown_t", [128, 16], F32)
    oc_t = p.sbuf("oc_t", [128, 1], BF16)
    or_t = p.sbuf("or_t", [1, 128], BF16)
    idf_t = p.sbuf("idf_t", [128, 128], F32)
    e6_t = p.sbuf("e6_t", [32, 6 * 128], BF16)
    row_mu = p.sbuf("row_mu", [1, HALF], F32)
    row_a = p.sbuf("row_a", [1, HALF], F32)
    row_b = p.sbuf("row_b", [1, HALF], F32)
    rowb_rstd = p.sbuf("rowb_rstd", [1, HALF], BF16)
    rowb_nmr = p.sbuf("rowb_nmr", [1, HALF], BF16)
    eps_row = p.sbuf("eps_row", [1, 1], F32)

    def ln_rows(sum_c0, sq_c0, tlen, invn=1.0 / 2048.0):
        p.add("scalar", lambda e: e.activation(
            row_mu[:, 0:tlen], PS[0:1, sum_c0:sum_c0 + tlen], AF.Copy, scale=invn), inc=ss)
        p.add("scalar", lambda e: e.activation(
            row_a[:, 0:tlen], PS[0:1, sq_c0:sq_c0 + tlen], AF.Copy, scale=invn), inc=ss)
        p.wait("scalar", ss)
        p.add("scalar", lambda e: e.activation(
            row_b[:, 0:tlen], row_mu[:, 0:tlen], AF.Square), inc=ss)
        p.wait("vector", ss)
        p.add("vector", lambda e: e.tensor_sub(
            row_a[:, 0:tlen], row_a[:, 0:tlen], row_b[:, 0:tlen]), inc=sv)
        p.wait("scalar", sv)
        p.add("scalar", lambda e: e.activation(
            row_a[:, 0:tlen], row_a[:, 0:tlen], AF.Ln, bias=eps_row[0:1, 0:1]), inc=ss)
        p.wait("scalar", ss)
        p.add("scalar", lambda e: e.activation(
            row_a[:, 0:tlen], row_a[:, 0:tlen], AF.Exp, scale=-0.5), inc=ss)
        p.wait("vector", ss)
        p.add("vector", lambda e: e.tensor_mul(
            row_b[:, 0:tlen], row_mu[:, 0:tlen], row_a[:, 0:tlen]), inc=sv)
        p.wait("vector", sv)
        p.add("vector", lambda e: e.tensor_scalar_mul(
            row_b[:, 0:tlen], row_b[:, 0:tlen], -1.0), inc=sv)
        p.wait("vector", sv)
        p.add("vector", lambda e: e.tensor_copy(rowb_rstd[:, 0:tlen], row_a[:, 0:tlen]), inc=sv)
        p.add("vector", lambda e: e.tensor_copy(rowb_nmr[:, 0:tlen], row_b[:, 0:tlen]), inc=sv)
        return sv.n
    ab_bc = p.sbuf("ab_bc", [128, HALF], BF16)
    bb_bc = p.sbuf("bb_bc", [128, HALF], BF16)

    PS = p.psum("PS", [128, 4096], F32)


    p.add("sync", lambda e: e.dma_start(
        out=x1[:, :].rearrange("p (k m) -> p k m", k=16),
        in_=yp.rearrange("(k p) m -> p k m", p=128)), inc=sq, dma=True)
    p.add("sync", lambda e: e.dma_start(
        out=wgf[:, :].rearrange("p (k m) -> p k m", k=16),
        in_=wg.rearrange("(k p) m -> p k m", p=128)), inc=sq, dma=True)
    p.add("sync", lambda e: e.dma_start(out=bgr[:, :], in_=bg), inc=sq, dma=True)
    p.add("sync", lambda e: e.dma_start(out=bgate_t[:, :], in_=bgate), inc=sq, dma=True)
    p.add("sync", lambda e: e.dma_start(out=bup_t[:, :], in_=bup), inc=sq, dma=True)
    p.add("sync", lambda e: e.dma_start(out=x2h[0][0:6, 0:4096].bitcast(F32), in_=bdown6), inc=sq, dma=True)
    p.add("sync", lambda e: e.dma_start(out=bsup_t[:, :], in_=bsup), inc=sq, dma=True)
    p.add("sync", lambda e: e.dma_start(out=bsdown_t[:, :], in_=bsdown), inc=sq, dma=True)
    p.add("sync", lambda e: e.dma_start(out=oc_t[:, :], in_=ones_col), inc=sq, dma=True)
    p.add("sync", lambda e: e.dma_start(out=or_t[:, :], in_=ones_row), inc=sq, dma=True)
    p.add("sync", lambda e: e.dma_start(out=idf_t[:, :], in_=idf), inc=sq, dma=True)
    p.add("sync", lambda e: e.dma_start(out=e6_t[:, :], in_=E6B), inc=sq, dma=True)
    n_static = sq.n

    p.wait("vector", sq, n_static)
    p.add("vector", lambda e: e.tensor_copy(wgb[:, :], wgf[:, :]), inc=sv)
    p.add("vector", lambda e: e.memset(bdown_t[:, :], 0.0), inc=sv)
    p.wait("vector", sv)
    p.add("vector", lambda e: e.tensor_copy(bdown_t[0:6, :], x2h[0][0:6, 0:4096].bitcast(F32)), inc=sv)
    p.add("vector", lambda e: e.memset(w_fm[:, :], 0.0), inc=sv)
    bdown_cast = sv.n
    p.add("vector", lambda e: e.tensor_copy(bgr_b[:, :], bgr[:, :]), inc=sv)
    p.add("vector", lambda e: e.memset(eps_row[:, :], 1e-5), inc=sv)
    p.wait("scalar", sv, sv.n)
    prep_v = sv.n

    p.wait("scalar", sq, n_static)
    p.wait("tensor", sq, n_static)
    sq_mm_done = [0, 0]
    for kt in range(16):
        par = kt % 2
        if sq_mm_done[par]:
            p.wait("scalar", st, sq_mm_done[par])
        p.add("scalar", lambda e, kt=kt, par=par: e.activation(
            sqt[par][:, :], x1[:, kt * HALF:(kt + 1) * HALF], AF.Square), inc=ss)
        sq_done = ss.n
        p.wait("tensor", ss, sq_done)
        for fh in range(2):
            p.add("tensor", lambda e, kt=kt, fh=fh: e.matmul(
                PS[0:1, fh * 512:(fh + 1) * 512], oc_t[:, :],
                x1[:, kt * HALF + fh * 512:kt * HALF + (fh + 1) * 512],
                start=(kt == 0), stop=(kt == 15)), inc=st)
            p.add("tensor", lambda e, kt=kt, fh=fh, par=par: e.matmul(
                PS[0:1, 1024 + fh * 512:1024 + (fh + 1) * 512], oc_t[:, :],
                sqt[par][:, fh * 512:(fh + 1) * 512],
                start=(kt == 0), stop=(kt == 15)), inc=st)
        sq_mm_done[par] = st.n
    stats_mm = st.n
    p.wait("scalar", st, stats_mm)
    rows_done = ln_rows(0, 1024, 1024)
    p.wait("tensor", sv, rows_done)
    for fh in range(2):
        p.add("tensor", lambda e, fh=fh: e.matmul(
            PS[:, 2048 + fh * 512:2048 + (fh + 1) * 512], or_t[0:1, :],
            rowb_rstd[0:1, fh * 512:(fh + 1) * 512], start=True, stop=True), inc=st)
        p.add("tensor", lambda e, fh=fh: e.matmul(
            PS[:, 3072 + fh * 512:3072 + (fh + 1) * 512], or_t[0:1, :],
            rowb_nmr[0:1, fh * 512:(fh + 1) * 512], start=True, stop=True), inc=st)
    bc_mm = st.n
    p.wait("scalar", st, bc_mm)
    p.add("scalar", lambda e: e.copy(ab_bc[:, :], PS[:, 2048:3072]), inc=ss)
    p.add("scalar", lambda e: e.copy(bb_bc[:, :], PS[:, 3072:4096]), inc=ss)
    bc_evac = ss.n
    p.wait("vector", ss, bc_evac)
    for kt in range(16):
        p.add("vector", lambda e, kt=kt: e.tensor_mul(
            x1[:, kt * HALF:(kt + 1) * HALF], x1[:, kt * HALF:(kt + 1) * HALF], ab_bc[:, :]), inc=sv)
        p.add("vector", lambda e, kt=kt: e.tensor_add(
            x1[:, kt * HALF:(kt + 1) * HALF], x1[:, kt * HALF:(kt + 1) * HALF], bb_bc[:, :]), inc=sv)
    x1_done = sv.n

    if B_STAGE == "ln":
        p.wait("gpsimd", sv, x1_done)
        p.add("gpsimd", lambda e: e.dma_start(
            out=x2o[:, :].rearrange("(k p) m -> p k m", p=128),
            in_=x1[:, :].rearrange("p (k m) -> p k m", k=16)), inc=sg, dma=True)
        p.wait("gpsimd", sg, sg.n)
        p.finalize()
        return nc

    p.wait("tensor", sv, x1_done)
    p.wait("tensor", ss, bc_evac)
    p.add("tensor", lambda e: e.matmul(
        PS[:, 2560:2568], or_t[0:1, :], bgr_b[0:1, :], start=True, stop=True), inc=st)
    bgbc_mm = st.n
    p.wait("scalar", st, bgbc_mm)
    p.add("scalar", lambda e: e.copy(bg_bc[:, :], PS[:, 2560:2568]), inc=ss)
    bgbc_done = ss.n
    for tt in range(8):
        for kt in range(16):
            p.add("tensor", lambda e, tt=tt, kt=kt: e.matmul(
                PS[:, 2048 + tt * 8:2048 + (tt + 1) * 8],
                x1[:, kt * HALF + tt * 128:kt * HALF + (tt + 1) * 128],
                wgb[:, kt * 8:(kt + 1) * 8],
                start=(kt == 0), stop=(kt == 15)), inc=st)
    lg_mm = st.n
    p.wait("scalar", st, lg_mm)
    p.add("scalar", lambda e: e.copy(lg_t[:, :], PS[:, 2048:2112]), inc=ss)
    lg_evac = ss.n
    if B_STAGE == "router1":
        p.wait("gpsimd", ss, lg_evac)
        p.add("gpsimd", lambda e: e.dma_start(out=x2o[0:128, 0:64], in_=lg_t[:, :]), inc=sg, dma=True)
        p.wait("gpsimd", sg, sg.n)
        p.finalize()
        return nc
    p.wait("vector", ss, lg_evac)
    BIG = 1.0e30
    for tt in range(8):
        lg = lg_t[:, tt * 8:(tt + 1) * 8]
        wt = wt_t[:, tt * 8:(tt + 1) * 8]
        def v(fn):
            p.add("vector", fn, inc=sv)
            p.wait("vector", sv)
        v(lambda e, lg=lg: e.tensor_add(lg, lg, bg_bc[:, :]))
        v(lambda e, lg=lg: e.tensor_scalar_mul(rt1[:, :], lg, -1.0))
        v(lambda e: e.reduce_max(r1[:, :], rt1[:, :], axis=AX.X))
        v(lambda e: e.tensor_single_scalar(rt2[:, :], rt1[:, :], r1[:, 0:1], ALU.is_equal))
        v(lambda e: e.scalar_tensor_tensor(
            rt2[:, :], rt2[:, :], -BIG, rt1[:, :], ALU.mult, ALU.add))
        v(lambda e: e.reduce_max(r2[:, :], rt2[:, :], axis=AX.X))
        v(lambda e: e.tensor_single_scalar(rt2[:, :], rt1[:, :], r2[:, 0:1], ALU.is_lt))
        v(lambda e, lg=lg: e.reduce_max(r1[:, :], lg, axis=AX.X))
        v(lambda e: e.tensor_scalar_mul(r3[:, :], r1[:, :], -1.0))
        sv_r = sv.n
        p.wait("scalar", sv, sv_r)
        p.add("scalar", lambda e, lg=lg, wt=wt: e.activation(wt, lg, AF.Exp, bias=r3[:, 0:1]), inc=ss)
        p.wait("scalar", ss)
        p.wait("vector", ss)
        v(lambda e, wt=wt: e.tensor_mul(wt, wt, rt2[:, :]))
        v(lambda e, wt=wt: e.reduce_sum(r1[:, :], wt, axis=AX.X))
        v(lambda e: e.reciprocal(r2[:, :], r1[:, :]))
        v(lambda e, wt=wt: e.tensor_single_scalar(wt, wt, r2[:, 0:1], ALU.mult))
    wt_done = sv.n
    if B_STAGE == "router2":
        p.wait("gpsimd", sv, wt_done)
        p.add("gpsimd", lambda e: e.dma_start(out=x2o[0:128, 0:64], in_=wt_t[:, :]), inc=sg, dma=True)
        p.wait("gpsimd", sg, sg.n)
        p.finalize()
        return nc
    p.wait("tensor", sv, wt_done)
    rowmoves = 0
    for n in range(6):
        if n > 0:
            p.wait("tensor", ss, ss.n)
        for tt in range(8):
            p.add("tensor", lambda e, n=n, tt=tt: e.matmul(
                PS[0:1, 2048 + tt * 128:2048 + (tt + 1) * 128],
                wt_t[:, tt * 8 + n:tt * 8 + n + 1], idf_t[:, :],
                start=True, stop=True), inc=st)
        p.wait("scalar", st, st.n)
        if n > 0:
            p.wait("scalar", sg, rowmoves)
        p.add("scalar", lambda e: e.copy(row_mu[0:1, :], PS[0:1, 2048:3072]), inc=ss)
        p.wait("gpsimd", ss, ss.n)
        p.add("gpsimd", lambda e, n=n: e.dma_start(
            out=w_fm[n:n + 1, :], in_=row_mu[0:1, :]), inc=sg, dma=True)
        rowmoves = sg.n
    p.wait("vector", sg, rowmoves)
    p.add("vector", lambda e: e.tensor_copy(w_fmb[:, :], w_fm[:, :]), inc=sv)
    wfmb_done = sv.n
    wbc_done = ss.n
    if B_STAGE == "router":
        p.wait("gpsimd", ss, wbc_done)
        p.wait("gpsimd", sv, sv.n)
        p.add("gpsimd", lambda e: e.dma_start(out=x2o[0:8, 0:HALF], in_=w_fm[0:8, :]), inc=sg, dma=True)
        p.wait("gpsimd", sg, sg.n)
        p.finalize()
        return nc

    wload = 0
    wcast = {}
    wmm = {}
    ecv = {0: 0, 1: 0}

    def stream_weight(par, src_ap, ncols, mw):
        nonlocal wload
        if "stream" in B_SKIP:
            if par in wmm and wmm[par]:
                p.wait("vector", st, wmm[par])
            p.add("vector", lambda e, par=par, ncols=ncols: e.memset(
                wq_b[par][:, 0:ncols], 0.01), inc=sv)
            wcast[par] = sv.n
            return sv.n
        if par in wcast and wcast[par]:
            p.wait("sync", sv, prep_v if wcast[par] == "PREP" else wcast[par])
        p.add("sync", lambda e, par=par, src_ap=src_ap, ncols=ncols, mw=mw: e.dma_start(
            out=wq_f[par][:, 0:ncols].rearrange("p (k m) -> p k m", m=mw),
            in_=src_ap), inc=sqw[par], dma=True)
        wload = sqw[par].n
        p.wait("vector", sqw[par], wload)
        if par in wmm and wmm[par]:
            p.wait("vector", st, wmm[par])
        p.add("vector", lambda e, par=par, ncols=ncols: e.tensor_copy(
            wq_b[par][:, 0:ncols], wq_f[par][:, 0:ncols]), inc=sv)
        wcast[par] = sv.n
        return sv.n

    if B_STAGE == "evac1":
        p.wait("scalar", sv, sv.n)
        p.wait("scalar", ss, ss.n)
        if "noact" not in B_SKIP:
            for ht in range(8):
                srcap = (lambda e, ht=ht: x1[:, ht * 512:ht * 512 + TH]) if "nops" in B_SKIP else (lambda e, ht=ht: PS[:, ht * 512:(ht + 1) * 512])
                p.add("scalar", lambda e, ht=ht, srcap=srcap: e.activation(
                    sg_t[:, ht * TH:(ht + 1) * TH], srcap(e, ht),
                    AF.Identity if B_NOSILU else AF.Silu,
                    bias=bgate_t[:, ht:ht + 1]), inc=ss)
        p.wait("gpsimd", ss, ss.n)
        if "simplestore" in B_SKIP:
            p.add("gpsimd", lambda e: e.dma_start(
                out=x2o[0:8, 0:HALF], in_=w_fm[0:8, :]), inc=sg, dma=True)
        else:
            p.add("gpsimd", lambda e: e.dma_start(
                out=x2o[0:1024, 0:TH].rearrange("(k p) m -> p k m", p=128),
                in_=sg_t[:, :].rearrange("p (k m) -> p k m", k=8)), inc=sg, dma=True)
        p.wait("gpsimd", sg, sg.n)
        p.finalize()
        return nc

    x2_store = [0, 0]
    down_last = 0
    wfmb_ok = sv.n
    for th in range(2):
        p.wait("tensor", sv, wfmb_ok)
        if down_last:
            p.wait("tensor", ss, ss.n)
        if "wbc" in B_SKIP:
            p.add("vector", lambda e: e.memset(wbc[:, :], 0.001), inc=sv)
        else:
            for n in range(6):
                p.add("tensor", lambda e, n=n, th=th: e.matmul(
                    PS[:, 2048:2560], e6_t[:, n * 128:(n + 1) * 128],
                    w_fmb[0:32, th * TH:(th + 1) * TH], start=True, stop=True), inc=st)
                wbc_mm = st.n
                p.wait("scalar", st, wbc_mm)
                p.add("scalar", lambda e, n=n: e.copy(
                    wbc[:, n * TH:(n + 1) * TH], PS[:, 2048:2560]), inc=ss)
                p.wait("tensor", ss, ss.n)
        wbc_ready = ss.n
        p.wait("vector", ss, wbc_ready)
        for n in range(NEXP):
            if "gmm" in B_SKIP:
                for ht in range(8):
                    p.add("vector", lambda e, ht=ht: e.memset(PS[:, ht * 512:(ht + 1) * 512], 0.25), inc=sv)
                p.wait("scalar", sv, sv.n)
            else:
                for q in range(8):
                    par = q % 2
                    src = wgate[n, q * 256:(q + 1) * 256, :].rearrange("(k p) m -> p k m", p=128)
                    cdone = stream_weight(par, src, 2048, 1024)
                    p.wait("tensor", sv, cdone)
                    for kl in range(2):
                        kt = q * 2 + kl
                        for ht in range(8):
                            p.add("tensor", lambda e, kl=kl, kt=kt, ht=ht, par=par, th=th: e.matmul(
                                PS[:, ht * 512:(ht + 1) * 512],
                                wq_b[par][:, kl * 1024 + ht * 128:kl * 1024 + (ht + 1) * 128],
                                x1[:, kt * HALF + th * TH:kt * HALF + th * TH + TH],
                                start=(kt == 0), stop=(kt == 15)), inc=st)
                    wmm[par] = st.n
            g_mm = st.n
            p.wait("scalar", st, g_mm)
            if ecv[0]:
                p.wait("scalar", sv, ecv[0])
            for ht in range(8):
                p.add("scalar", lambda e, n=n, ht=ht: e.activation(
                    sg_t[:, ht * TH:(ht + 1) * TH], PS[:, ht * 512:(ht + 1) * 512],
                    AF.Identity if B_NOSILU else AF.Silu,
                    bias=bgate_t[:, n * 8 + ht:n * 8 + ht + 1]), inc=ss)
            g_evac = ss.n
            if B_STAGE == "hid1" and n == 0 and th == 0:
                p.wait("gpsimd", ss, g_evac)
                p.add("gpsimd", lambda e: e.dma_start(
                    out=x2o[0:1024, 0:TH].rearrange("(k p) m -> p k m", p=128),
                    in_=sg_t[:, :].rearrange("p (k m) -> p k m", k=8)), inc=sg, dma=True)
                p.wait("gpsimd", sg, sg.n)
                p.finalize()
                return nc
            p.wait("tensor", ss, g_evac)
            for q in range(8):
                par = q % 2
                src = wup[n, q * 256:(q + 1) * 256, :].rearrange("(k p) m -> p k m", p=128)
                cdone = stream_weight(par, src, 2048, 1024)
                p.wait("tensor", sv, cdone)
                for kl in range(2):
                    kt = q * 2 + kl
                    for ht in range(8):
                        p.add("tensor", lambda e, kl=kl, kt=kt, ht=ht, par=par, th=th: e.matmul(
                            PS[:, ht * 512:(ht + 1) * 512],
                            wq_b[par][:, kl * 1024 + ht * 128:kl * 1024 + (ht + 1) * 128],
                            x1[:, kt * HALF + th * TH:kt * HALF + th * TH + TH],
                            start=(kt == 0), stop=(kt == 15)), inc=st)
                wmm[par] = st.n
            u_mm = st.n
            p.wait("scalar", st, u_mm)
            for ht in range(8):
                p.add("scalar", lambda e, n=n, ht=ht: e.activation(
                    tmp_u[:, :], PS[:, ht * 512:(ht + 1) * 512],
                    AF.Identity, bias=bup_t[:, n * 8 + ht:n * 8 + ht + 1]), inc=ss)
                u_ev = ss.n
                p.wait("vector", ss, u_ev)
                p.add("vector", lambda e, ht=ht: e.tensor_mul(
                    tmp_h[:, :], tmp_u[:, :], sg_t[:, ht * TH:(ht + 1) * TH]), inc=sv)
                p.wait("vector", sv)
                p.add("vector", lambda e, n=n, ht=ht, th=th: e.tensor_mul(
                    hid[:, (n * 8 + ht) * TH:(n * 8 + ht + 1) * TH],
                    tmp_h[:, :], wbc[:, n * TH:(n + 1) * TH]), inc=sv)
                p.wait("scalar", sv, sv.n - 1)
            ecv[0] = sv.n
        for half2 in range(2):
            for q in range(8):
                par = q % 2
                src = wsup[q * 256:(q + 1) * 256, half2 * 1024:(half2 + 1) * 1024].rearrange(
                    "(k p) m -> p k m", p=128)
                cdone = stream_weight(par, src, 2048, 1024)
                p.wait("tensor", sv, cdone)
                for kl in range(2):
                    kt = q * 2 + kl
                    for ht in range(8):
                        p.add("tensor", lambda e, kl=kl, kt=kt, ht=ht, par=par, th=th: e.matmul(
                            PS[:, ht * 512:(ht + 1) * 512],
                            wq_b[par][:, kl * 1024 + ht * 128:kl * 1024 + (ht + 1) * 128],
                            x1[:, kt * HALF + th * TH:kt * HALF + th * TH + TH],
                            start=(kt == 0), stop=(kt == 15)), inc=st)
                wmm[par] = st.n
            su_mm = st.n
            p.wait("scalar", st, su_mm)
            for ht in range(8):
                hcol = half2 * 8 + ht
                p.add("scalar", lambda e, hcol=hcol, ht=ht: e.activation(
                    hid[:, (48 + hcol) * TH:(48 + hcol + 1) * TH],
                    PS[:, ht * 512:(ht + 1) * 512],
                    AF.Silu, bias=bsup_t[:, hcol:hcol + 1]), inc=ss)
            p.wait("tensor", ss, ss.n)
        hid_done_s = ss.n
        hid_done_v = sv.n
        if B_STAGE == "hid" and th == 0:
            p.wait("gpsimd", ss, hid_done_s)
            p.wait("gpsimd", sv, hid_done_v)
            p.add("gpsimd", lambda e: e.dma_start(
                out=x2o[:, 0:TH].rearrange("(k p) m -> p k m", p=128),
                in_=hid[:, 0:16 * TH].rearrange("p (k m) -> p k m", k=16)), inc=sg, dma=True)
            p.wait("gpsimd", sg, sg.n)
            p.finalize()
            return nc

        p.wait("tensor", ss, hid_done_s)
        p.wait("tensor", sv, hid_done_v)
        for dg in range(4):
            if down_last:
                p.wait("tensor", ss, down_last)
            first = True
            for n in range(NEXP):
                for hc in range(2):
                    src = wdown[n, hc * 512:(hc + 1) * 512, dg * 512:(dg + 1) * 512].rearrange(
                        "(k p) m -> p k m", p=128)
                    par = (n * 2 + hc) % 2
                    cdone = stream_weight(par, src, 2048, 512)
                    p.wait("tensor", sv, cdone)
                    for kl in range(4):
                        kt = hc * 4 + kl
                        for dt_ in range(4):
                            p.add("tensor", lambda e, n=n, kl=kl, kt=kt, dt_=dt_, par=par, fr=first and kt == 0: e.matmul(
                                PS[:, dt_ * 512:(dt_ + 1) * 512],
                                wq_b[par][:, kl * 512 + dt_ * 128:kl * 512 + (dt_ + 1) * 128],
                                hid[:, (n * 8 + kt) * TH:(n * 8 + kt + 1) * TH],
                                start=fr, stop=False), inc=st)
                    first = False
                    wmm[par] = st.n
            for qc in range(4):
                src = wsdown[qc * 512:(qc + 1) * 512, dg * 512:(dg + 1) * 512].rearrange(
                    "(k p) m -> p k m", p=128)
                par = qc % 2
                cdone = stream_weight(par, src, 2048, 512)
                p.wait("tensor", sv, cdone)
                for kl in range(4):
                    kt = qc * 4 + kl
                    for dt_ in range(4):
                        p.add("tensor", lambda e, kl=kl, kt=kt, dt_=dt_, par=par: e.matmul(
                            PS[:, dt_ * 512:(dt_ + 1) * 512],
                            wq_b[par][:, kl * 512 + dt_ * 128:kl * 512 + (dt_ + 1) * 128],
                            hid[:, (48 + kt) * TH:(48 + kt + 1) * TH],
                            start=False, stop=False), inc=st)
                wmm[par] = st.n
            for dt_ in range(4):
                dtile = dg * 4 + dt_
                p.add("tensor", lambda e, dt_=dt_, dtile=dtile, th=th: e.matmul(
                    PS[:, dt_ * 512:(dt_ + 1) * 512],
                    bdown_t[0:32, dtile * 128:(dtile + 1) * 128],
                    w_fmb[0:32, th * TH:(th + 1) * TH],
                    start=False, stop=True), inc=st)
            dg_mm = st.n
            p.wait("scalar", st, dg_mm)
            for dt_ in range(4):
                dtile = dg * 4 + dt_
                p.add("scalar", lambda e, dt_=dt_, dtile=dtile: e.activation(
                    tmp_h[:, :], PS[:, dt_ * 512:(dt_ + 1) * 512],
                    AF.Identity, bias=bsdown_t[:, dtile:dtile + 1]), inc=ss)
                ev = ss.n
                p.wait("vector", ss, ev)
                p.add("vector", lambda e, dtile=dtile, th=th: e.tensor_add(
                    s_t[:, dtile * TH:(dtile + 1) * TH], tmp_h[:, :],
                    x1[:, dtile * HALF + th * TH:dtile * HALF + th * TH + TH]), inc=sv)
                p.wait("scalar", sv, sv.n)
            down_last = ss.n
        s_done_v = sv.n

        p.wait("tensor", sv, s_done_v)
        sq2_done = [0, 0]
        for dtile in range(16):
            par = dtile % 2
            if sq2_done[par]:
                p.wait("scalar", st, sq2_done[par])
            p.add("scalar", lambda e, dtile=dtile, par=par: e.activation(
                sqt[par][:, 0:TH], s_t[:, dtile * TH:(dtile + 1) * TH], AF.Square), inc=ss)
            sqd = ss.n
            p.wait("tensor", ss, sqd)
            p.add("tensor", lambda e, dtile=dtile: e.matmul(
                PS[:, 2048:2560][0:1, :], oc_t[:, :],
                s_t[:, dtile * TH:(dtile + 1) * TH],
                start=(dtile == 0), stop=(dtile == 15)), inc=st)
            p.add("tensor", lambda e, dtile=dtile, par=par: e.matmul(
                PS[:, 2560:3072][0:1, :], oc_t[:, :], sqt[par][:, 0:TH],
                start=(dtile == 0), stop=(dtile == 15)), inc=st)
            sq2_done[par] = st.n
        st_mm = st.n
        p.wait("scalar", st, st_mm)
        rws = ln_rows(2048, 2560, TH)
        p.wait("tensor", sv, rws)
        p.add("tensor", lambda e: e.matmul(
            PS[:, 3072:3584], or_t[0:1, :], rowb_rstd[0:1, 0:TH], start=True, stop=True), inc=st)
        p.add("tensor", lambda e: e.matmul(
            PS[:, 3584:4096], or_t[0:1, :], rowb_nmr[0:1, 0:TH], start=True, stop=True), inc=st)
        bmm = st.n
        p.wait("scalar", st, bmm)
        p.add("scalar", lambda e: e.copy(ab_bc[:, 0:TH], PS[:, 3072:3584]), inc=ss)
        p.add("scalar", lambda e: e.copy(bb_bc[:, 0:TH], PS[:, 3584:4096]), inc=ss)
        bev = ss.n
        p.wait("vector", ss, bev)
        if th == 1 and x2_store[0]:
            p.wait("vector", sg, x2_store[0])
        for dtile in range(16):
            p.add("vector", lambda e, dtile=dtile, th=th: e.tensor_mul(
                x2h[th][:, dtile * TH:(dtile + 1) * TH],
                s_t[:, dtile * TH:(dtile + 1) * TH], ab_bc[:, 0:TH]), inc=sv)
            p.add("vector", lambda e, dtile=dtile, th=th: e.tensor_add(
                x2h[th][:, dtile * TH:(dtile + 1) * TH],
                x2h[th][:, dtile * TH:(dtile + 1) * TH], bb_bc[:, 0:TH]), inc=sv)
        x2_done = sv.n
        p.wait("gpsimd", sv, x2_done)
        p.add("gpsimd", lambda e, th=th: e.dma_start(
            out=x2o[:, th * TH:(th + 1) * TH].rearrange("(k p) m -> p k m", p=128),
            in_=x2h[th][:, :].rearrange("p (k m) -> p k m", k=16)), inc=sg, dma=True)
        x2_store[th] = sg.n
        p.wait("tensor", ss, ss.n)

    p.wait("gpsimd", sg, sg.n)
    return {"stores": [sg]}


def run_ffn0(inp, ypre):
    wgv = inp["L0_ffn_Wg"]; bgv = inp["L0_ffn_bg"]
    wgate = inp["L0_ffn_Wgate"]; bgate = inp["L0_ffn_bgate"]
    wupv = inp["L0_ffn_Wup"]; bupv = inp["L0_ffn_bup"]
    wdownv = inp["L0_ffn_Wdown"]; bdownv = inp["L0_ffn_bdown"]
    wsupv = inp["L0_ffn_Ws_up"]; bsupv = inp["L0_ffn_bs_up"]
    wsdownv = inp["L0_ffn_Ws_down"]; bsdownv = inp["L0_ffn_bs_down"]

    base = {
        "wg": np.ascontiguousarray(wgv),
        "bg": np.ascontiguousarray(bgv.reshape(1, 8)),
        "wgate": np.ascontiguousarray(wgate),
        "wup": np.ascontiguousarray(wupv),
        "wdown": np.ascontiguousarray(wdownv),
        "bgate": np.ascontiguousarray(bgate.reshape(6, 8, 128).transpose(2, 0, 1).reshape(128, 48)),
        "bup": np.ascontiguousarray(bupv.reshape(6, 8, 128).transpose(2, 0, 1).reshape(128, 48)),
        "bdown6": np.ascontiguousarray(bdownv),
        "wsup": np.ascontiguousarray(wsupv),
        "bsup": np.ascontiguousarray(bsupv.reshape(16, 128).T),
        "wsdown": np.ascontiguousarray(wsdownv),
        "bsdown": np.ascontiguousarray(bsdownv.reshape(16, 128).T),
    }
    in_maps = []
    for c in range(8):
        b, j = c // 2, c % 2
        m = dict(base)
        m["yp"] = np.ascontiguousarray(ypre[b][:, j * HALF:(j + 1) * HALF])
        in_maps.append(m)
    nc = build_ffn0()
    res = run_bass_kernel_spmd(nc, in_maps, core_ids=list(range(8)))
    x2 = np.empty((B, D, S), BF)
    for c in range(8):
        b, j = c // 2, c % 2
        x2[b][:, j * HALF:(j + 1) * HALF] = res.results[c]["x2o"]
    return x2


NH = 16
DH = 128


def attn_drams(nc, x2f_kind="ExternalInput", x3_kind="ExternalOutput"):
    io = {}
    io["x2f"] = nc.dram_tensor("x2f", [D, S], BF16, kind=x2f_kind).ap()
    io["wqd"] = nc.dram_tensor("wqd", [D, D], F32, kind="ExternalInput").ap()
    io["wkd"] = nc.dram_tensor("wkd", [D, D], F32, kind="ExternalInput").ap()
    io["wvd"] = nc.dram_tensor("wvd", [D, D], F32, kind="ExternalInput").ap()
    io["wod"] = nc.dram_tensor("wod", [D, D], F32, kind="ExternalInput").ap()
    io["bq"] = nc.dram_tensor("bq", [128, 16], F32, kind="ExternalInput").ap()
    io["bk"] = nc.dram_tensor("bk", [128, 16], F32, kind="ExternalInput").ap()
    io["bvr"] = nc.dram_tensor("bvr", [1, D], F32, kind="ExternalInput").ap()
    io["bo"] = nc.dram_tensor("bo", [128, 16], F32, kind="ExternalInput").ap()
    io["lng"] = nc.dram_tensor("lng", [128, 16], F32, kind="ExternalInput").ap()
    io["lnb"] = nc.dram_tensor("lnb", [128, 16], F32, kind="ExternalInput").ap()
    io["minv"] = nc.dram_tensor("minv", [1, S], BF16, kind="ExternalInput").ap()
    io["x3o"] = nc.dram_tensor("x3o", [D, HALF], BF16, kind=x3_kind).ap()
    io["kscr"] = nc.dram_tensor("kscr", [D, S], BF16, kind="Internal").ap()
    io["vscr"] = nc.dram_tensor("vscr", [S, D], BF16, kind="Internal").ap()
    tri_np = np.where(np.arange(128)[None, :] <= np.arange(128)[:, None], 0.0, -1.0e30)
    io["TRI"] = nc.inline_tensor(tri_np.astype(np.float32).astype(BF), name="TRI").ap()
    io["ones_colC"] = nc.inline_tensor(np.ones((128, 1), np.float32).astype(BF), name="ones_colC").ap()
    io["ones_rowC"] = nc.inline_tensor(np.ones((1, 128), np.float32).astype(BF), name="ones_rowC").ap()
    io["idfC"] = nc.inline_tensor(np.eye(128, dtype=np.float32), name="idfC").ap()
    io["idbC"] = nc.inline_tensor(np.eye(128, dtype=np.float32).astype(BF), name="idbC").ap()
    io["onesf_rowC"] = nc.inline_tensor(np.ones((1, 128), np.float32), name="onesf_rowC").ap()
    return io


def build_attn():
    nc = bass.Bass("TRN2", target_bir_lowering=False)
    io = attn_drams(nc)
    p = Prog(nc)
    emit_attn(nc, p, io)
    p.finalize()
    return nc


def emit_attn(nc, p, io):
    x2f = io["x2f"]; wqd = io["wqd"]; wkd = io["wkd"]; wvd = io["wvd"]; wod = io["wod"]
    bq = io["bq"]; bk = io["bk"]; bvr = io["bvr"]; bo = io["bo"]
    lng = io["lng"]; lnb = io["lnb"]; minv = io["minv"]; x3o = io["x3o"]
    kscr = io["kscr"]; vscr = io["vscr"]
    TRI = io["TRI"]; ones_col = io["ones_colC"]; ones_row = io["ones_rowC"]; idb = io["idfC"]
    idbb = io["idbC"]; onesf_row = io["onesf_rowC"]
    sq = p.sem("sq"); sg = p.sem("sg"); st = p.sem("st"); ss = p.sem("ss"); sv = p.sem("sv")
    sqx = [p.sem("sqx0"), p.sem("sqx1")]
    sqw = [p.sem("sqw0"), p.sem("sqw1")]
    sgk = [p.sem("sgk0"), p.sem("sgk1")]
    sqh = [p.sem("sqh0"), p.sem("sqh1")]

    x2o_t = p.sbuf("x2o_t", [128, 16 * HALF], BF16)
    xn = p.sbuf("xn", [128, 16 * HALF], BF16)
    sqt = [p.sbuf(f"sqt{i}", [128, 512], BF16) for i in range(2)]
    q_sb = p.sbuf("q_sb", [128, 16 * HALF], BF16)
    kv_t = [p.sbuf(f"kv_t{i}", [128, 1024], BF16) for i in range(2)]
    wq_f = [p.sbuf(f"wq_f{i}", [128, 2048], F32) for i in range(2)]
    wq_b = [p.sbuf(f"wq_b{i}", [128, 2048], BF16) for i in range(2)]
    kh = [p.sbuf(f"kh{i}", [128, 2048], BF16) for i in range(2)]
    vh = [p.sbuf(f"vh{i}", [128, 2048], BF16) for i in range(2)]
    a_sb = [p.sbuf(f"a_sb{i}", [128, 2048], F32) for i in range(2)]
    at_sb = [p.sbuf(f"at_sb{i}", [128, 2048], BF16) for i in range(2)]
    av_sb = xn
    s_t = q_sb
    tri_t = p.sbuf("tri_t", [128, 128], BF16)
    idb_b = p.sbuf("idb_b", [128, 128], BF16)
    onesf_r = p.sbuf("onesf_r", [1, 128], F32)
    rlcrow = p.sbuf("rlcrow", [1, 128], F32)
    bc_sb = p.sbuf("bc_sb", [128, 128], F32)
    minv_b = p.sbuf("minv_b", [1, S], BF16)
    bq_t = p.sbuf("bq_t", [128, 16], F32)
    bk_t = p.sbuf("bk_t", [128, 16], F32)
    bv_b = p.sbuf("bv_b", [1, D], BF16)
    bo_t = p.sbuf("bo_t", [128, 16], F32)
    lng_t = p.sbuf("lng_t", [128, 16], F32)
    lnb_t = p.sbuf("lnb_t", [128, 16], F32)
    oc_t = p.sbuf("oc_t", [128, 1], BF16)
    or_t = p.sbuf("or_t", [1, 128], BF16)
    idb_t = p.sbuf("idb_t", [128, 128], F32)
    m1c = [p.sbuf(f"m1c{i}", [128, 1], F32) for i in range(2)]
    nm1c = [p.sbuf(f"nm1c{i}", [128, 1], F32) for i in range(2)]
    lc = [p.sbuf(f"lc{i}", [128, 1], F32) for i in range(2)]
    rlc = [p.sbuf(f"rlc{i}", [128, 1], F32) for i in range(2)]
    row_mu = p.sbuf("row_mu", [1, HALF], F32)
    row_a = p.sbuf("row_a", [1, HALF], F32)
    row_b = p.sbuf("row_b", [1, HALF], F32)
    rowb_rstd = p.sbuf("rowb_rstd", [1, HALF], BF16)
    rowb_nmr = p.sbuf("rowb_nmr", [1, HALF], BF16)
    eps_row = p.sbuf("eps_row", [1, 1], F32)
    ab_bc = p.sbuf("ab_bc", [128, HALF], BF16)
    bb_bc = p.sbuf("bb_bc", [128, HALF], BF16)

    PS = p.psum("PS", [128, 4096], F32)

    SCALE = float(1.0 / np.sqrt(np.float32(DH)))

    for dst, srct in [(bq_t, bq), (bk_t, bk), (bo_t, bo),
                      (lng_t, lng), (lnb_t, lnb), (minv_b, minv),
                      (tri_t, TRI), (oc_t, ones_col), (or_t, ones_row),
                      (idb_t, idb), (idb_b, idbb), (onesf_r, onesf_row)]:
        p.add("sync", lambda e, dst=dst, srct=srct: e.dma_start(out=dst[:, :], in_=srct), inc=sq, dma=True)
    p.add("sync", lambda e: e.dma_start(out=wq_f[0][0:1, :], in_=bvr), inc=sq, dma=True)
    n_static = sq.n
    p.wait("vector", sq, n_static)
    p.add("vector", lambda e: e.memset(eps_row[:, :], 1e-5), inc=sv)
    p.add("vector", lambda e: e.tensor_copy(bv_b[:, :], wq_f[0][0:1, :]), inc=sv)
    prep_v = sv.n
    p.wait("scalar", sv, prep_v)
    p.wait("tensor", sv, prep_v)

    def ln_rows(sum_c0, sq_c0, tlen, invn=1.0 / 2048.0):
        p.add("scalar", lambda e: e.activation(
            row_mu[:, 0:tlen], PS[0:1, sum_c0:sum_c0 + tlen], AF.Copy, scale=invn), inc=ss)
        p.add("scalar", lambda e: e.activation(
            row_a[:, 0:tlen], PS[0:1, sq_c0:sq_c0 + tlen], AF.Copy, scale=invn), inc=ss)
        p.wait("scalar", ss)
        p.add("scalar", lambda e: e.activation(
            row_b[:, 0:tlen], row_mu[:, 0:tlen], AF.Square), inc=ss)
        p.wait("vector", ss)
        p.add("vector", lambda e: e.tensor_sub(
            row_a[:, 0:tlen], row_a[:, 0:tlen], row_b[:, 0:tlen]), inc=sv)
        p.wait("scalar", sv)
        p.add("scalar", lambda e: e.activation(
            row_a[:, 0:tlen], row_a[:, 0:tlen], AF.Ln, bias=eps_row[0:1, 0:1]), inc=ss)
        p.wait("scalar", ss)
        p.add("scalar", lambda e: e.activation(
            row_a[:, 0:tlen], row_a[:, 0:tlen], AF.Exp, scale=-0.5), inc=ss)
        p.wait("vector", ss)
        p.add("vector", lambda e: e.tensor_mul(
            row_b[:, 0:tlen], row_mu[:, 0:tlen], row_a[:, 0:tlen]), inc=sv)
        p.wait("vector", sv)
        p.add("vector", lambda e: e.tensor_scalar_mul(
            row_b[:, 0:tlen], row_b[:, 0:tlen], -1.0), inc=sv)
        p.wait("vector", sv)
        p.add("vector", lambda e: e.tensor_copy(rowb_rstd[:, 0:tlen], row_a[:, 0:tlen]), inc=sv)
        p.add("vector", lambda e: e.tensor_copy(rowb_nmr[:, 0:tlen], row_b[:, 0:tlen]), inc=sv)
        return sv.n

    wmm = {}
    wcast = {0: "PREP"}
    kv_store = [0, 0]

    def stream_w(par, src_ap):
        if par in wcast and wcast[par]:
            p.wait("sync", sv, prep_v if wcast[par] == "PREP" else wcast[par])
        p.add("sync", lambda e, par=par, src_ap=src_ap: e.dma_start(
            out=wq_f[par][:, :].rearrange("p (k m) -> p k m", k=16), in_=src_ap),
            inc=sqw[par], dma=True)
        p.wait("vector", sqw[par], sqw[par].n)
        if par in wmm and wmm[par]:
            p.wait("vector", st, wmm[par])
        p.add("vector", lambda e, par=par: e.tensor_copy(wq_b[par][:, :], wq_f[par][:, :]), inc=sv)
        wcast[par] = sv.n
        return sv.n

    def stream_w_chunk(par, src_ap):
        if par in wcast and wcast[par]:
            p.wait("sync", sv, prep_v if wcast[par] == "PREP" else wcast[par])
        p.add("sync", lambda e, par=par, src_ap=src_ap: e.dma_start(
            out=wq_f[par][:, 0:512], in_=src_ap), inc=sqw[par], dma=True)
        p.wait("vector", sqw[par], sqw[par].n)
        if par in wmm and wmm[par]:
            p.wait("vector", st, wmm[par])
        p.add("vector", lambda e, par=par: e.tensor_copy(wq_b[par][:, 0:512], wq_f[par][:, 0:512]), inc=sv)
        wcast[par] = sv.n
        return sv.n

    xn_users = 0
    for hf in range(2):
        if xn_users:
            p.wait("sync", st, xn_users)
        p.add("sync", lambda e, hf=hf: e.dma_start(
            out=xn[:, :].rearrange("p (k m) -> p k m", k=16),
            in_=x2f[:, hf * HALF:(hf + 1) * HALF].rearrange("(k p) m -> p k m", p=128)),
            inc=sqx[hf], dma=True)
        p.wait("vector", sqx[hf], sqx[hf].n)
        p.wait("scalar", sqx[hf], sqx[hf].n)
        p.wait("tensor", sqx[hf], sqx[hf].n)

        sqd = [0, 0]
        for kt in range(16):
            for fh in range(2):
                par = (kt * 2 + fh) % 2
                if sqd[par]:
                    p.wait("scalar", st, sqd[par])
                p.add("scalar", lambda e, kt=kt, fh=fh, par=par: e.activation(
                    sqt[par][:, :], xn[:, kt * HALF + fh * 512:kt * HALF + (fh + 1) * 512],
                    AF.Square), inc=ss)
                sd = ss.n
                p.wait("tensor", ss, sd)
                p.add("tensor", lambda e, kt=kt, fh=fh: e.matmul(
                    PS[0:1, 2048 + fh * 512:2048 + (fh + 1) * 512], oc_t[:, :],
                    xn[:, kt * HALF + fh * 512:kt * HALF + (fh + 1) * 512],
                    start=(kt == 0), stop=(kt == 15)), inc=st)
                p.add("tensor", lambda e, kt=kt, fh=fh, par=par: e.matmul(
                    PS[0:1, 1024 + fh * 512:1024 + (fh + 1) * 512], oc_t[:, :],
                    sqt[par][:, :], start=(kt == 0), stop=(kt == 15)), inc=st)
                sqd[par] = st.n
        p.wait("scalar", st, st.n)
        rws = ln_rows(2048, 1024, HALF)
        p.wait("tensor", sv, rws)
        for fh in range(2):
            p.add("tensor", lambda e, fh=fh: e.matmul(
                PS[:, 1024 + fh * 512:1024 + (fh + 1) * 512], or_t[0:1, :],
                rowb_rstd[0:1, fh * 512:(fh + 1) * 512], start=True, stop=True), inc=st)
            p.add("tensor", lambda e, fh=fh: e.matmul(
                PS[:, 2048 + fh * 512:2048 + (fh + 1) * 512], or_t[0:1, :],
                rowb_nmr[0:1, fh * 512:(fh + 1) * 512], start=True, stop=True), inc=st)
        bcmm = st.n
        p.wait("scalar", st, bcmm)
        p.add("scalar", lambda e: e.copy(ab_bc[:, :], PS[:, 1024:2048]), inc=ss)
        p.add("scalar", lambda e: e.copy(bb_bc[:, :], PS[:, 2048:3072]), inc=ss)
        bce = ss.n
        p.wait("vector", ss, bce)
        for kt in range(16):
            p.add("vector", lambda e, kt=kt: e.tensor_mul(
                xn[:, kt * HALF:(kt + 1) * HALF], xn[:, kt * HALF:(kt + 1) * HALF], ab_bc[:, :]), inc=sv)
            p.wait("vector", sv)
            p.add("vector", lambda e, kt=kt: e.tensor_add(
                xn[:, kt * HALF:(kt + 1) * HALF], xn[:, kt * HALF:(kt + 1) * HALF], bb_bc[:, :]), inc=sv)
        zdone = sv.n
        p.wait("scalar", sv, zdone)
        for kt in range(16):
            p.add("scalar", lambda e, kt=kt: e.activation(
                xn[:, kt * HALF:(kt + 1) * HALF], xn[:, kt * HALF:(kt + 1) * HALF],
                AF.Identity, bias=lnb_t[:, kt:kt + 1], scale=lng_t[:, kt:kt + 1]), inc=ss)
        xn_done = ss.n
        p.wait("tensor", ss, xn_done)
        p.wait("vector", ss, xn_done)
        if hf == 1:
            p.add("sync", lambda e: e.dma_start(
                out=x2o_t[:, :].rearrange("p (k m) -> p k m", k=16),
                in_=x2f[:, HALF:S].rearrange("(k p) m -> p k m", p=128)), inc=sq, dma=True)

        for et in range(16):
            par = et % 2
            cd = stream_w(par, wkd[:, et * 128:(et + 1) * 128].rearrange("(k p) m -> p k m", p=128))
            p.wait("tensor", sv, cd)
            if kv_store[par]:
                p.wait("tensor", sgk[par], kv_store[par])
            p.wait("tensor", ss, ss.n)
            for fq in range(2):
                bank = fq
                for kt in range(16):
                    p.add("tensor", lambda e, kt=kt, fq=fq, par=par, bank=bank: e.matmul(
                        PS[:, bank * 512:(bank + 1) * 512],
                        wq_b[par][:, kt * 128:(kt + 1) * 128],
                        xn[:, kt * HALF + fq * 512:kt * HALF + (fq + 1) * 512],
                        start=(kt == 0), stop=(kt == 15)), inc=st)
                wmm[par] = st.n
                p.wait("scalar", st, st.n)
                p.add("scalar", lambda e, et=et, fq=fq, par=par, bank=bank: e.activation(
                    kv_t[par][:, fq * 512:(fq + 1) * 512], PS[:, bank * 512:(bank + 1) * 512],
                    AF.Identity, bias=bk_t[:, et:et + 1]), inc=ss)
            ev = ss.n
            p.wait("gpsimd", ss, ev)
            p.add("gpsimd", lambda e, et=et, hf=hf, par=par: e.dma_start(
                out=kscr[et * 128:(et + 1) * 128, hf * HALF:(hf + 1) * HALF],
                in_=kv_t[par][:, :]), inc=sgk[par], dma=True)
            kv_store[par] = sgk[par].n

        for eh in range(4):
            for ttg in range(2):
                for kt in range(16):
                    par = kt % 2
                    cd = stream_w_chunk(par, wvd[kt * 128:(kt + 1) * 128, eh * 512:(eh + 1) * 512])
                    p.wait("tensor", sv, cd)
                    if kt == 0:
                        p.wait("tensor", ss, ss.n)
                    if kt < 2 and kv_store[(ttg * 4) % 2] == 0:
                        pass
                    for tl in range(4):
                        tt = ttg * 4 + tl
                        p.add("tensor", lambda e, kt=kt, tt=tt, tl=tl, par=par: e.matmul(
                            PS[:, tl * 512:(tl + 1) * 512],
                            xn[:, kt * HALF + tt * 128:kt * HALF + (tt + 1) * 128],
                            wq_b[par][:, 0:512],
                            start=(kt == 0), stop=False), inc=st)
                    wmm[par] = st.n
                for tl in range(4):
                    p.add("tensor", lambda e, tl=tl, eh=eh: e.matmul(
                        PS[:, tl * 512:(tl + 1) * 512], or_t[0:1, :],
                        bv_b[0:1, eh * 512:(eh + 1) * 512], start=False, stop=True), inc=st)
                vmm = st.n
                p.wait("scalar", st, vmm)
                for tl in range(4):
                    tt = ttg * 4 + tl
                    par2 = tl % 2
                    if kv_store[par2]:
                        p.wait("scalar", sgk[par2], kv_store[par2])
                    p.add("scalar", lambda e, tl=tl, par2=par2: e.copy(
                        kv_t[par2][:, 0:512], PS[:, tl * 512:(tl + 1) * 512]), inc=ss)
                    p.wait("gpsimd", ss, ss.n)
                    p.add("gpsimd", lambda e, hf=hf, tt=tt, eh=eh, par2=par2: e.dma_start(
                        out=vscr[hf * HALF + tt * 128:hf * HALF + (tt + 1) * 128,
                                 eh * 512:(eh + 1) * 512],
                        in_=kv_t[par2][:, 0:512]), inc=sgk[par2], dma=True)
                    kv_store[par2] = sgk[par2].n

        if hf == 1:
            for et in range(16):
                par = et % 2
                cd = stream_w(par, wqd[:, et * 128:(et + 1) * 128].rearrange("(k p) m -> p k m", p=128))
                p.wait("tensor", sv, cd)
                p.wait("tensor", ss, ss.n)
                for fq in range(2):
                    bank = fq
                    for kt in range(16):
                        p.add("tensor", lambda e, kt=kt, fq=fq, par=par, bank=bank: e.matmul(
                            PS[:, bank * 512:(bank + 1) * 512],
                            wq_b[par][:, kt * 128:(kt + 1) * 128],
                            xn[:, kt * HALF + fq * 512:kt * HALF + (fq + 1) * 512],
                            start=(kt == 0), stop=(kt == 15)), inc=st)
                    wmm[par] = st.n
                    p.wait("scalar", st, st.n)
                    p.add("scalar", lambda e, et=et, fq=fq, bank=bank: e.activation(
                        q_sb[:, et * HALF + fq * 512:et * HALF + (fq + 1) * 512],
                        PS[:, bank * 512:(bank + 1) * 512],
                        AF.Identity, bias=bq_t[:, et:et + 1]), inc=ss)
        xn_users = st.n

    p.wait("gpsimd", sgk[0], sgk[0].n)
    p.wait("gpsimd", sgk[1], sgk[1].n)
    p.wait("sync", sgk[0], sgk[0].n)
    p.wait("sync", sgk[1], sgk[1].n)
    q_ready = ss.n
    p.wait("tensor", ss, q_ready)

    khready = [0, 0]
    kvuse = [0, 0]
    copyev = {0: 0, 1: 0}
    aswar = [0, 0]
    rlcevac_pp = 0
    avTT_last = 0
    state = {"avTT": 0}

    def load_head(h):
        hp = h % 2
        if kvuse[hp]:
            p.wait("sync", st, kvuse[hp])
        p.add("sync", lambda e, h=h, hp=hp: e.dma_start(
            out=kh[hp][:, :], in_=kscr[h * 128:(h + 1) * 128, :]), inc=sqh[hp], dma=True)
        p.add("sync", lambda e, h=h, hp=hp: e.dma_start(
            out=vh[hp][:, :].rearrange("p (k m) -> p k m", m=128),
            in_=vscr[:, h * 128:(h + 1) * 128].rearrange("(k p) m -> p k m", p=128)),
            inc=sqh[hp], dma=True)
        khready[hp] = sqh[hp].n

    def emit_tpav(pr):
        par = pr["par"]; hpar = pr["hpar"]; nkb = pr["nkb"]; h = pr["h"]; qt = pr["qt"]
        groups = [list(range(g, min(g + 4, nkb))) for g in range(0, nkb, 4)]
        cpds = []
        for gi in range(len(groups) + 1):
            if gi < len(groups):
                blocks = groups[gi]
                pstb = gi % 2
                c0p = 2048 + pstb * 512
                if copyev[pstb]:
                    p.wait("tensor", ss, copyev[pstb])
                for i, kb in enumerate(blocks):
                    p.add("tensor", lambda e, kb=kb, i=i, par=par, c0p=c0p: e.transpose(
                        PS[:, c0p + i * 128:c0p + (i + 1) * 128],
                        a_sb[par][:, kb * 128:(kb + 1) * 128], idb_t[:, :]), inc=st)
                tpd = st.n
                gw = len(blocks)
                p.wait("scalar", st, tpd)
                p.add("scalar", lambda e, blocks=blocks, par=par, c0p=c0p, gw=gw: e.copy(
                    at_sb[par][:, blocks[0] * 128:blocks[0] * 128 + gw * 128],
                    PS[:, c0p:c0p + gw * 128]), inc=ss)
                copyev[pstb] = ss.n
                cpds.append(ss.n)
                aswar[par] = tpd
            if gi >= 1:
                blocks = groups[gi - 1]
                p.wait("tensor", ss, cpds[gi - 1])
                if gi == 1 and state["avTT"]:
                    p.wait("tensor", sv, state["avTT"])
                for kb in blocks:
                    p.add("tensor", lambda e, kb=kb, hpar=hpar, par=par, nkb=nkb: e.matmul(
                        PS[:, 3072:3200], vh[hpar][:, kb * 128:(kb + 1) * 128],
                        at_sb[par][:, kb * 128:(kb + 1) * 128],
                        start=(kb == 0), stop=(kb == nkb - 1)), inc=st)
        avmm = st.n
        kvuse[hpar] = avmm
        return {"avmm": avmm, "h": h, "qt": qt, "bcsb": pr["bcsb"]}

    def emit_avtt(pr2):
        p.wait("vector", st, pr2["avmm"])
        p.wait("vector", ss, pr2["bcsb"])
        p.add("vector", lambda e, h=pr2["h"], qt=pr2["qt"]: e.tensor_mul(
            av_sb[:, h * HALF + qt * 128:h * HALF + (qt + 1) * 128],
            PS[:, 3072:3200], bc_sb[:, :]), inc=sv)
        state["avTT"] = sv.n

    load_head(0)
    prev = None
    idx = 0
    for h in range(NH):
        for qt in range(8):
            kv = 1024 + (qt + 1) * 128
            nkb = kv // 128
            par = idx % 2
            hpar = h % 2
            p.wait("tensor", sqh[hpar], khready[hpar])
            if prev:
                p.wait("tensor", ss, prev["expd"])
            c0 = 0
            while c0 < kv:
                fl = min(512, kv - c0)
                p.add("tensor", lambda e, h=h, qt=qt, c0=c0, fl=fl, hpar=hpar: e.matmul(
                    PS[:, c0:c0 + fl],
                    q_sb[:, h * HALF + qt * 128:h * HALF + (qt + 1) * 128],
                    kh[hpar][:, c0:c0 + fl], start=True, stop=False), inc=st)
                c0 += fl
            p.add("tensor", lambda e, kv=kv: e.matmul(
                PS[:, kv - 128:kv], idb_b[:, :], tri_t[:, :], start=False, stop=False), inc=st)
            c0 = 0
            while c0 < kv:
                fl = min(512, kv - c0)
                p.add("tensor", lambda e, c0=c0, fl=fl: e.matmul(
                    PS[:, c0:c0 + fl], or_t[0:1, :], minv_b[0:1, c0:c0 + fl],
                    start=False, stop=True), inc=st)
                c0 += fl
            smm = st.n
            kvuse[hpar] = smm
            pend = None
            if prev:
                pend = emit_tpav(prev)
            if qt == 0 and h + 1 < NH:
                load_head(h + 1)
            p.wait("vector", st, smm)
            p.add("vector", lambda e, kv=kv, par=par: e.reduce_max(
                m1c[par][:, :], PS[:, 0:kv], axis=AX.X), inc=sv)
            p.wait("vector", sv)
            p.add("vector", lambda e, par=par: e.tensor_scalar_mul(
                nm1c[par][:, :], m1c[par][:, :], -SCALE), inc=sv)
            nsc = sv.n
            if pend:
                emit_avtt(pend)
            p.wait("scalar", sv, nsc)
            if aswar[par]:
                p.wait("scalar", st, aswar[par])
            p.add("scalar", lambda e, kv=kv, par=par: e.activation(
                a_sb[par][:, 0:kv], PS[:, 0:kv], AF.Exp, scale=SCALE, bias=nm1c[par][:, 0:1],
                accum_out=lc[par][:, 0:1]), inc=ss)
            expd = ss.n
            p.wait("vector", ss, expd)
            p.add("vector", lambda e, par=par: e.reciprocal(rlc[par][:, :], lc[par][:, :]), inc=sv)
            rcp = sv.n
            p.wait("tensor", sv, rcp)
            if rlcevac_pp:
                p.wait("tensor", ss, rlcevac_pp)
            p.add("tensor", lambda e, par=par: e.matmul(
                PS[0:1, 3584:3712], rlc[par][:, 0:1], idb_t[:, :], start=True, stop=True), inc=st)
            p.wait("scalar", st, st.n)
            p.add("scalar", lambda e: e.copy(rlcrow[:, :], PS[0:1, 3584:3712]), inc=ss)
            rlcevac_pp = ss.n
            p.wait("tensor", ss, rlcevac_pp)
            p.add("tensor", lambda e: e.matmul(
                PS[:, 3712:3840], onesf_r[:, :], rlcrow[:, :], start=True, stop=True), inc=st)
            p.wait("scalar", st, st.n)
            if state["avTT"]:
                p.wait("scalar", sv, state["avTT"])
            p.add("scalar", lambda e: e.copy(bc_sb[:, :], PS[:, 3712:3840]), inc=ss)
            prev = {"par": par, "hpar": hpar, "h": h, "qt": qt, "nkb": nkb,
                    "expd": expd, "bcsb": ss.n}
            idx += 1
    emit_avtt(emit_tpav(prev))
    avTT_last = state["avTT"]

    p.wait("tensor", sv, avTT_last)
    p.wait("tensor", ss, ss.n)
    p.wait("tensor", sq, sq.n)
    p.wait("scalar", sgk[0], sgk[0].n)
    p.wait("scalar", sgk[1], sgk[1].n)
    p.wait("vector", sq, sq.n)
    wo_ev = 0
    for dt_ in range(16):
        par = dt_ % 2
        cd = stream_w(par, wod[:, dt_ * 128:(dt_ + 1) * 128].rearrange("(k p) m -> p k m", p=128))
        p.wait("tensor", sv, cd)
        p.wait("tensor", ss, ss.n)
        for fq in range(2):
            bank = fq
            for kt in range(16):
                p.add("tensor", lambda e, kt=kt, fq=fq, par=par, bank=bank: e.matmul(
                    PS[:, bank * 512:(bank + 1) * 512],
                    wq_b[par][:, kt * 128:(kt + 1) * 128],
                    av_sb[:, kt * HALF + fq * 512:kt * HALF + (fq + 1) * 512],
                    start=(kt == 0), stop=(kt == 15)), inc=st)
            wmm[par] = st.n
            p.wait("scalar", st, st.n)
            p.add("scalar", lambda e, dt_=dt_, fq=fq, bank=bank: e.activation(
                kv_t[fq][:, 0:512], PS[:, bank * 512:(bank + 1) * 512],
                AF.Identity, bias=bo_t[:, dt_:dt_ + 1]), inc=ss)
            oev = ss.n
            p.wait("vector", ss, oev)
            p.add("vector", lambda e, dt_=dt_, fq=fq: e.tensor_add(
                s_t[:, dt_ * HALF + fq * 512:dt_ * HALF + (fq + 1) * 512],
                kv_t[fq][:, 0:512],
                x2o_t[:, dt_ * HALF + fq * 512:dt_ * HALF + (fq + 1) * 512]), inc=sv)
        wo_ev = sv.n

    p.wait("scalar", sv, wo_ev)
    p.wait("tensor", sv, wo_ev)
    sqd = [0, 0]
    for kt in range(16):
        for fh in range(2):
            par = (kt * 2 + fh) % 2
            if sqd[par]:
                p.wait("scalar", st, sqd[par])
            p.add("scalar", lambda e, kt=kt, fh=fh, par=par: e.activation(
                sqt[par][:, :], s_t[:, kt * HALF + fh * 512:kt * HALF + (fh + 1) * 512],
                AF.Square), inc=ss)
            sd = ss.n
            p.wait("tensor", ss, sd)
            p.add("tensor", lambda e, kt=kt, fh=fh: e.matmul(
                PS[0:1, 2048 + fh * 512:2048 + (fh + 1) * 512], oc_t[:, :],
                s_t[:, kt * HALF + fh * 512:kt * HALF + (fh + 1) * 512],
                start=(kt == 0), stop=(kt == 15)), inc=st)
            p.add("tensor", lambda e, kt=kt, fh=fh, par=par: e.matmul(
                PS[0:1, 1024 + fh * 512:1024 + (fh + 1) * 512], oc_t[:, :],
                sqt[par][:, :], start=(kt == 0), stop=(kt == 15)), inc=st)
            sqd[par] = st.n
    p.wait("scalar", st, st.n)
    rws = ln_rows(2048, 1024, HALF)
    p.wait("tensor", sv, rws)
    for fh in range(2):
        p.add("tensor", lambda e, fh=fh: e.matmul(
            PS[:, 1024 + fh * 512:1024 + (fh + 1) * 512], or_t[0:1, :],
            rowb_rstd[0:1, fh * 512:(fh + 1) * 512], start=True, stop=True), inc=st)
        p.add("tensor", lambda e, fh=fh: e.matmul(
            PS[:, 2048 + fh * 512:2048 + (fh + 1) * 512], or_t[0:1, :],
            rowb_nmr[0:1, fh * 512:(fh + 1) * 512], start=True, stop=True), inc=st)
    bcmm = st.n
    p.wait("scalar", st, bcmm)
    p.add("scalar", lambda e: e.copy(ab_bc[:, :], PS[:, 1024:2048]), inc=ss)
    p.add("scalar", lambda e: e.copy(bb_bc[:, :], PS[:, 2048:3072]), inc=ss)
    bce = ss.n
    p.wait("vector", ss, bce)
    for kt in range(16):
        p.add("vector", lambda e, kt=kt: e.tensor_mul(
            s_t[:, kt * HALF:(kt + 1) * HALF], s_t[:, kt * HALF:(kt + 1) * HALF], ab_bc[:, :]), inc=sv)
        p.wait("vector", sv)
        p.add("vector", lambda e, kt=kt: e.tensor_add(
            s_t[:, kt * HALF:(kt + 1) * HALF], s_t[:, kt * HALF:(kt + 1) * HALF], bb_bc[:, :]), inc=sv)
    x3_done = sv.n
    p.wait("gpsimd", sv, x3_done)
    p.add("gpsimd", lambda e: e.dma_start(
        out=x3o.rearrange("(k p) m -> p k m", p=128),
        in_=s_t[:, :].rearrange("p (k m) -> p k m", k=16)), inc=sg, dma=True)
    p.wait("gpsimd", sg, sg.n)
    return {"stores": [sg]}


def run_attn(inp, x2):
    wq = inp["L1_attn_Wq"]; wk = inp["L1_attn_Wk"]; wv = inp["L1_attn_Wv"]; wo = inp["L1_attn_Wo"]
    col16 = lambda v: np.ascontiguousarray(v.reshape(16, 128).T)
    base = {
        "wqd": np.ascontiguousarray(wq), "wkd": np.ascontiguousarray(wk),
        "wvd": np.ascontiguousarray(wv), "wod": np.ascontiguousarray(wo),
        "bq": col16(inp["L1_attn_bq"]), "bk": col16(inp["L1_attn_bk"]),
        "bvr": np.ascontiguousarray(inp["L1_attn_bv"].reshape(1, D)),
        "bo": col16(inp["L1_attn_bo"]),
        "lng": col16(inp["L1_attn_ln_g"]), "lnb": col16(inp["L1_attn_ln_b"]),
    }
    in_maps = []
    for c in range(8):
        b, j = c // 2, c % 2
        own = x2[b][:, j * HALF:(j + 1) * HALF]
        oth = x2[b][:, (1 - j) * HALF:(2 - j) * HALF]
        m = dict(base)
        m["x2f"] = np.ascontiguousarray(np.concatenate([oth, own], axis=1))
        mv = np.zeros((1, S), np.float32)
        if j == 0:
            mv[0, 0:HALF] = -1.0e30
        m["minv"] = mv.astype(BF)
        in_maps.append(m)
    nc = build_attn()
    res = run_bass_kernel_spmd(nc, in_maps, core_ids=list(range(8)))
    x3 = np.empty((B, D, S), BF)
    for c in range(8):
        b, j = c // 2, c % 2
        x3[b][:, j * HALF:(j + 1) * HALF] = res.results[c]["x3o"]
    return x3




def ffn1_drams(nc, x3_kind="ExternalInput", x3_ap=None):
    io = {}
    io["x3i"] = x3_ap if x3_ap is not None else nc.dram_tensor(
        "x3i", [D, HALF], BF16, kind=x3_kind).ap()
    io["rwt"] = nc.dram_tensor("rwt", [128, 128], F32, kind="ExternalInput").ap()
    io["shg"] = nc.dram_tensor("shg", [64, D, 128], F32, kind="ExternalInput").ap()
    io["shu"] = nc.dram_tensor("shu", [64, D, 128], F32, kind="ExternalInput").ap()
    io["shd"] = nc.dram_tensor("shd", [16, 8192, 128], F32, kind="ExternalInput").ap()
    io["rg"] = nc.dram_tensor("rg", [8, D, 128], F32, kind="ExternalInput").ap()
    io["ru"] = nc.dram_tensor("ru", [8, D, 128], F32, kind="ExternalInput").ap()
    io["rd"] = nc.dram_tensor("rd", [16, 1024, 128], F32, kind="ExternalInput").ap()
    io["xo"] = nc.dram_tensor("xo", [D, HALF], F32, kind="ExternalOutput").ap()
    io["ones_colD"] = nc.inline_tensor(np.ones((128, 1), np.float32).astype(BF), name="ones_colD").ap()
    io["ones_rowD"] = nc.inline_tensor(np.ones((1, 128), np.float32).astype(BF), name="ones_rowD").ap()
    io["idfD"] = nc.inline_tensor(np.eye(128, dtype=np.float32), name="idfD").ap()
    io["onesfD"] = nc.inline_tensor(np.ones((1, 128), np.float32), name="onesfD").ap()
    return io


def build_ffn1():
    nc = bass.Bass("TRN2", target_bir_lowering=False)
    io = ffn1_drams(nc)
    p = Prog(nc)
    emit_ffn1(nc, p, io)
    p.finalize()
    return nc


def emit_ffn1(nc, p, io):
    x3i = io["x3i"]; rwt = io["rwt"]; shg = io["shg"]; shu = io["shu"]
    shd = io["shd"]; rg = io["rg"]; ru = io["ru"]; rd = io["rd"]; xo = io["xo"]
    ones_col = io["ones_colD"]; ones_row = io["ones_rowD"]
    idf = io["idfD"]; onesf = io["onesfD"]
    sq = p.sem("sq"); sg = p.sem("sg"); st = p.sem("st"); ss = p.sem("ss"); sv = p.sem("sv")
    sqw = [p.sem("sqw0"), p.sem("sqw1")]

    x3_t = p.sbuf("x3_t", [128, 16 * HALF], BF16)
    hid = p.sbuf("hid", [128, 32 * TH], BF16)
    hid_r = p.sbuf("hid_r", [128, 8 * TH], BF16)
    acc = p.sbuf("acc", [128, 16 * TH], F32)
    s_bf = p.sbuf("s_bf", [128, 16 * TH], BF16)
    tmp_g2 = [p.sbuf(f"tmp_g{i}", [128, TH], BF16) for i in range(2)]
    tmp_u2 = [p.sbuf(f"tmp_u{i}", [128, TH], BF16) for i in range(2)]
    tmp_g = tmp_g2[0]
    tmp_u = tmp_u2[0]
    wq_f = [p.sbuf(f"wq_f{i}", [128, 2048], F32) for i in range(2)]
    wq_b = [p.sbuf(f"wq_b{i}", [128, 2048], BF16) for i in range(2)]
    sqt = [p.sbuf(f"sqt{i}", [128, TH], BF16) for i in range(2)]
    rw_f = p.sbuf("rw_f", [128, 128], F32)
    rw_b = p.sbuf("rw_b", [128, 128], BF16)
    lgr = p.sbuf("lgr", [128, 8 * 8], F32)
    combc = p.sbuf("combc", [128, 8], F32)
    comb_row = p.sbuf("comb_row", [1, HALF], F32)
    comb_rb = p.sbuf("comb_rb", [1, HALF], BF16)
    comb_bc = p.sbuf("comb_bc", [128, HALF], BF16)
    r1 = p.sbuf("r1", [128, 1], F32)
    r2 = p.sbuf("r2", [128, 1], F32)
    r3 = p.sbuf("r3", [128, 1], F32)
    rt1 = p.sbuf("rt1", [128, 8], F32)
    rt2 = p.sbuf("rt2", [128, 8], F32)
    oc_t = p.sbuf("oc_t", [128, 1], BF16)
    or_t = p.sbuf("or_t", [1, 128], BF16)
    idf_t = p.sbuf("idf_t", [128, 128], F32)
    row_mu = p.sbuf("row_mu", [1, TH], F32)
    row_a = p.sbuf("row_a", [1, TH], F32)
    row_b = p.sbuf("row_b", [1, TH], F32)
    eps_row = p.sbuf("eps_row", [1, 1], F32)
    onesf_t = p.sbuf("onesf_t", [1, 128], F32)
    ab_bc = p.sbuf("ab_bc", [128, TH], F32)
    bb_bc = p.sbuf("bb_bc", [128, TH], F32)

    PS = p.psum("PS", [128, 4096], F32)

    p.add("sync", lambda e: e.dma_start(
        out=x3_t[:, :].rearrange("p (k m) -> p k m", k=16),
        in_=x3i.rearrange("(k p) m -> p k m", p=128)), inc=sq, dma=True)
    p.add("sync", lambda e: e.dma_start(out=rw_f[:, :], in_=rwt), inc=sq, dma=True)
    p.add("sync", lambda e: e.dma_start(out=oc_t[:, :], in_=ones_col), inc=sq, dma=True)
    p.add("sync", lambda e: e.dma_start(out=or_t[:, :], in_=ones_row), inc=sq, dma=True)
    p.add("sync", lambda e: e.dma_start(out=idf_t[:, :], in_=idf), inc=sq, dma=True)
    p.add("sync", lambda e: e.dma_start(out=onesf_t[:, :], in_=onesf), inc=sq, dma=True)
    n_static = sq.n
    p.wait("vector", sq, n_static)
    p.add("vector", lambda e: e.memset(eps_row[:, :], 1e-5), inc=sv)
    p.add("vector", lambda e: e.tensor_copy(rw_b[:, :], rw_f[:, :]), inc=sv)
    prep_v = sv.n
    p.wait("scalar", sv, prep_v)
    p.wait("tensor", sv, prep_v)

    def ln_rows(sum_c0, sq_c0, tlen, invn=1.0 / 2048.0):
        p.add("scalar", lambda e: e.activation(
            row_mu[:, 0:tlen], PS[0:1, sum_c0:sum_c0 + tlen], AF.Copy, scale=invn), inc=ss)
        p.add("scalar", lambda e: e.activation(
            row_a[:, 0:tlen], PS[0:1, sq_c0:sq_c0 + tlen], AF.Copy, scale=invn), inc=ss)
        p.wait("scalar", ss)
        p.add("scalar", lambda e: e.activation(
            row_b[:, 0:tlen], row_mu[:, 0:tlen], AF.Square), inc=ss)
        p.wait("vector", ss)
        p.add("vector", lambda e: e.tensor_sub(
            row_a[:, 0:tlen], row_a[:, 0:tlen], row_b[:, 0:tlen]), inc=sv)
        p.wait("scalar", sv)
        p.add("scalar", lambda e: e.activation(
            row_a[:, 0:tlen], row_a[:, 0:tlen], AF.Ln, bias=eps_row[0:1, 0:1]), inc=ss)
        p.wait("scalar", ss)
        p.add("scalar", lambda e: e.activation(
            row_a[:, 0:tlen], row_a[:, 0:tlen], AF.Exp, scale=-0.5), inc=ss)
        p.wait("vector", ss)
        p.add("vector", lambda e: e.tensor_mul(
            row_b[:, 0:tlen], row_mu[:, 0:tlen], row_a[:, 0:tlen]), inc=sv)
        p.wait("vector", sv)
        p.add("vector", lambda e: e.tensor_scalar_mul(
            row_b[:, 0:tlen], row_b[:, 0:tlen], -1.0), inc=sv)
        p.wait("vector", sv)
        return sv.n

    wmm = {}
    wcast = {}

    def stream_w(par, src_ap, ncols):
        if par in wcast and wcast[par]:
            p.wait("sync", sv, wcast[par])
        p.add("sync", lambda e, par=par, src_ap=src_ap, ncols=ncols: e.dma_start(
            out=wq_f[par][:, 0:ncols].rearrange("p (k m) -> p k m", m=128),
            in_=src_ap.rearrange("(k p) m -> p k m", p=128)),
            inc=sqw[par], dma=True)
        p.wait("vector", sqw[par], sqw[par].n)
        if par in wmm and wmm[par]:
            p.wait("vector", st, wmm[par])
        p.add("vector", lambda e, par=par, ncols=ncols: e.tensor_copy(
            wq_b[par][:, 0:ncols], wq_f[par][:, 0:ncols]), inc=sv)
        wcast[par] = sv.n
        return sv.n

    for tt in range(8):
        for kt in range(16):
            p.add("tensor", lambda e, tt=tt, kt=kt: e.matmul(
                PS[:, tt * 8:(tt + 1) * 8],
                x3_t[:, kt * HALF + tt * 128:kt * HALF + (tt + 1) * 128],
                rw_b[:, kt * 8:(kt + 1) * 8],
                start=(kt == 0), stop=(kt == 15)), inc=st)
    lgmm = st.n
    p.wait("scalar", st, lgmm)
    p.add("scalar", lambda e: e.copy(lgr[:, :], PS[:, 0:64]), inc=ss)
    lgev = ss.n
    p.wait("vector", ss, lgev)
    for tt in range(8):
        lg = lgr[:, tt * 8:(tt + 1) * 8]

        def v(fn):
            p.add("vector", fn, inc=sv)
            p.wait("vector", sv)
        v(lambda e, lg=lg: e.reduce_max(r1[:, :], lg, axis=AX.X))
        v(lambda e, lg=lg: e.tensor_single_scalar(rt1[:, :], lg, r1[:, 0:1], ALU.is_equal))
        v(lambda e, lg=lg: e.scalar_tensor_tensor(rt2[:, :], rt1[:, :], -BIGF, lg, ALU.mult, ALU.add))
        v(lambda e: e.reduce_max(r2[:, :], rt2[:, :], axis=AX.X))
        v(lambda e: e.tensor_scalar_mul(r3[:, :], r1[:, :], -1.0))
        sv_r = sv.n
        p.wait("scalar", sv, sv_r)
        p.add("scalar", lambda e: e.activation(r2[:, :], r2[:, :], AF.Exp, bias=r3[:, 0:1]), inc=ss)
        p.wait("scalar", ss)
        p.wait("vector", ss)
        v(lambda e: e.tensor_scalar_add(r1[:, :], r2[:, :], 1.0))
        v(lambda e: e.reciprocal(r3[:, :], r1[:, :]))
        v(lambda e, tt=tt: e.tensor_scalar(
            combc[:, tt:tt + 1], r2[:, :][:, 0:1], 1.0, r3[:, 0:1], ALU.add, ALU.mult))
    comb_done = sv.n
    p.wait("tensor", sv, comb_done)
    for tt in range(8):
        p.add("tensor", lambda e, tt=tt: e.matmul(
            PS[0:1, 1024 + tt * 128:1024 + (tt + 1) * 128],
            combc[:, tt:tt + 1], idf_t[:, :], start=True, stop=True), inc=st)
    ctm = st.n
    p.wait("scalar", st, ctm)
    p.add("scalar", lambda e: e.copy(comb_row[:, :], PS[0:1, 1024:2048]), inc=ss)
    p.wait("scalar", ss)
    crow = ss.n
    p.wait("vector", ss, crow)
    p.add("vector", lambda e: e.tensor_copy(comb_rb[:, :], comb_row[:, :]), inc=sv)
    p.wait("vector", sv)
    crb = sv.n
    p.wait("tensor", sv, crb)
    for fh in range(2):
        p.add("tensor", lambda e, fh=fh: e.matmul(
            PS[:, 2048 + fh * 512:2048 + (fh + 1) * 512], or_t[0:1, :],
            comb_rb[0:1, fh * 512:(fh + 1) * 512], start=True, stop=True), inc=st)
    cbm = st.n
    p.wait("scalar", st, cbm)
    p.add("scalar", lambda e: e.copy(comb_bc[:, :], PS[:, 2048:3072]), inc=ss)
    cbc = ss.n
    p.wait("tensor", ss, cbc)
    p.wait("vector", ss, cbc)

    ghu_ev = 0
    acc_upd = 0
    dmm_last = 0
    out_store = 0
    bank_ev = {0: 0, 1: 0, 2: 0, 3: 0}
    tmp_ev = {0: 0, 1: 0}
    for th in range(2):
        for hgroup in range(2):
            for ht in range(32):
                hta = hgroup * 32 + ht
                par = ht % 2
                bg_ = par
                bu_ = 2 + par
                cd = stream_w(par, shg[hta], 2048)
                p.wait("tensor", sv, cd)
                if bank_ev[bg_]:
                    p.wait("tensor", ss, bank_ev[bg_])
                for kt in range(16):
                    p.add("tensor", lambda e, kt=kt, th=th, par=par, bank=bg_: e.matmul(
                        PS[:, bank * 512:(bank + 1) * 512],
                        wq_b[par][:, kt * 128:(kt + 1) * 128],
                        x3_t[:, kt * HALF + th * TH:kt * HALF + (th + 1) * TH],
                        start=(kt == 0), stop=(kt == 15)), inc=st)
                wmm[par] = st.n
                gmm = st.n
                cd = stream_w(par, shu[hta], 2048)
                p.wait("tensor", sv, cd)
                if bank_ev[bu_]:
                    p.wait("tensor", ss, bank_ev[bu_])
                for kt in range(16):
                    p.add("tensor", lambda e, kt=kt, th=th, par=par, bank=bu_: e.matmul(
                        PS[:, bank * 512:(bank + 1) * 512],
                        wq_b[par][:, kt * 128:(kt + 1) * 128],
                        x3_t[:, kt * HALF + th * TH:kt * HALF + (th + 1) * TH],
                        start=(kt == 0), stop=(kt == 15)), inc=st)
                wmm[par] = st.n
                umm = st.n
                p.wait("scalar", st, gmm)
                if ghu_ev:
                    p.wait("scalar", sv, tmp_ev[par])
                p.add("scalar", lambda e, bank=bg_, par=par: e.activation(
                    tmp_g2[par][:, :], PS[:, bank * 512:(bank + 1) * 512], AF.Silu), inc=ss)
                bank_ev[bg_] = ss.n
                p.wait("scalar", st, umm)
                p.add("scalar", lambda e, bank=bu_, par=par: e.activation(
                    tmp_u2[par][:, :], PS[:, bank * 512:(bank + 1) * 512], AF.Identity), inc=ss)
                bank_ev[bu_] = ss.n
                uev = ss.n
                p.wait("vector", ss, uev)
                p.add("vector", lambda e, ht=ht, par=par: e.tensor_mul(
                    hid[:, ht * TH:(ht + 1) * TH], tmp_g2[par][:, :], tmp_u2[par][:, :]), inc=sv)
                ghu_ev = sv.n
                tmp_ev[par] = sv.n
            p.wait("tensor", sv, ghu_ev)
            for dt_ in range(16):
                par = dt_ % 2
                bank = 4 + par
                if dmm_last:
                    p.wait("tensor", ss, ss.n)
                    p.wait("tensor", sv, sv.n)
                for hhalf in range(2):
                    cd = stream_w(par, shd[dt_, hgroup * 4096 + hhalf * 2048:
                                           hgroup * 4096 + (hhalf + 1) * 2048], 2048)
                    p.wait("tensor", sv, cd)
                    for kl in range(16):
                        kt = hhalf * 16 + kl
                        p.add("tensor", lambda e, kl=kl, kt=kt, par=par, bank=bank, hhalf=hhalf: e.matmul(
                            PS[:, bank * 512:(bank + 1) * 512],
                            wq_b[par][:, kl * 128:(kl + 1) * 128],
                            hid[:, kt * TH:(kt + 1) * TH],
                            start=(hhalf == 0 and kl == 0), stop=(hhalf == 1 and kl == 15)), inc=st)
                    wmm[par] = st.n
                dmm = st.n
                p.wait("scalar", st, dmm) if False else None
                if hgroup == 0 and th >= 0:
                    p.wait("scalar", st, dmm)
                    p.add("scalar", lambda e, dt_=dt_, bank=bank: e.copy(
                        acc[:, dt_ * TH:(dt_ + 1) * TH], PS[:, bank * 512:(bank + 1) * 512]), inc=ss)
                    acc_upd = ss.n
                else:
                    p.wait("vector", st, dmm)
                    p.add("vector", lambda e, dt_=dt_, bank=bank: e.tensor_add(
                        acc[:, dt_ * TH:(dt_ + 1) * TH], acc[:, dt_ * TH:(dt_ + 1) * TH],
                        PS[:, bank * 512:(bank + 1) * 512]), inc=sv)
                    acc_upd = sv.n
                dmm_last = dmm

        for ht in range(8):
            par = ht % 2
            bg_ = par
            bu_ = 2 + par
            cd = stream_w(par, rg[ht], 2048)
            p.wait("tensor", sv, cd)
            if bank_ev[bg_]:
                p.wait("tensor", ss, bank_ev[bg_])
            for kt in range(16):
                p.add("tensor", lambda e, kt=kt, th=th, par=par, bank=bg_: e.matmul(
                    PS[:, bank * 512:(bank + 1) * 512],
                    wq_b[par][:, kt * 128:(kt + 1) * 128],
                    x3_t[:, kt * HALF + th * TH:kt * HALF + (th + 1) * TH],
                    start=(kt == 0), stop=(kt == 15)), inc=st)
            wmm[par] = st.n
            gmm = st.n
            cd = stream_w(par, ru[ht], 2048)
            p.wait("tensor", sv, cd)
            if bank_ev[bu_]:
                p.wait("tensor", ss, bank_ev[bu_])
            for kt in range(16):
                p.add("tensor", lambda e, kt=kt, th=th, par=par, bank=bu_: e.matmul(
                    PS[:, bank * 512:(bank + 1) * 512],
                    wq_b[par][:, kt * 128:(kt + 1) * 128],
                    x3_t[:, kt * HALF + th * TH:kt * HALF + (th + 1) * TH],
                    start=(kt == 0), stop=(kt == 15)), inc=st)
            wmm[par] = st.n
            umm = st.n
            p.wait("scalar", st, gmm)
            if ghu_ev:
                p.wait("scalar", sv, tmp_ev[par])
            p.add("scalar", lambda e, bank=bg_, par=par: e.activation(
                tmp_g2[par][:, :], PS[:, bank * 512:(bank + 1) * 512], AF.Silu), inc=ss)
            bank_ev[bg_] = ss.n
            p.wait("scalar", st, umm)
            p.add("scalar", lambda e, bank=bu_, par=par: e.activation(
                tmp_u2[par][:, :], PS[:, bank * 512:(bank + 1) * 512], AF.Identity), inc=ss)
            bank_ev[bu_] = ss.n
            uev = ss.n
            p.wait("vector", ss, uev)
            p.add("vector", lambda e, ht=ht, par=par: e.tensor_mul(
                hid_r[:, ht * TH:(ht + 1) * TH], tmp_g2[par][:, :], tmp_u2[par][:, :]), inc=sv)
            p.wait("vector", sv)
            p.add("vector", lambda e, ht=ht, th=th: e.tensor_mul(
                hid_r[:, ht * TH:(ht + 1) * TH], hid_r[:, ht * TH:(ht + 1) * TH],
                comb_bc[:, th * TH:(th + 1) * TH]), inc=sv)
            ghu_ev = sv.n
            tmp_ev[par] = sv.n
        p.wait("tensor", sv, ghu_ev)
        for dt_ in range(16):
            par = dt_ % 2
            bank = 4 + par
            cd = stream_w(par, rd[dt_], 1024)
            p.wait("tensor", sv, cd)
            p.wait("tensor", sv, sv.n)
            for kt in range(8):
                p.add("tensor", lambda e, kt=kt, par=par, bank=bank: e.matmul(
                    PS[:, bank * 512:(bank + 1) * 512],
                    wq_b[par][:, kt * 128:(kt + 1) * 128],
                    hid_r[:, kt * TH:(kt + 1) * TH],
                    start=(kt == 0), stop=(kt == 7)), inc=st)
            wmm[par] = st.n
            rdm = st.n
            p.wait("vector", st, rdm)
            p.add("vector", lambda e, dt_=dt_, bank=bank: e.tensor_add(
                acc[:, dt_ * TH:(dt_ + 1) * TH], acc[:, dt_ * TH:(dt_ + 1) * TH],
                PS[:, bank * 512:(bank + 1) * 512]), inc=sv)
            p.wait("vector", sv)
            p.add("vector", lambda e, dt_=dt_, th=th: e.tensor_add(
                acc[:, dt_ * TH:(dt_ + 1) * TH], acc[:, dt_ * TH:(dt_ + 1) * TH],
                x3_t[:, dt_ * HALF + th * TH:dt_ * HALF + (th + 1) * TH]), inc=sv)
            p.wait("vector", sv)
            p.add("vector", lambda e, dt_=dt_: e.tensor_copy(
                s_bf[:, dt_ * TH:(dt_ + 1) * TH], acc[:, dt_ * TH:(dt_ + 1) * TH]), inc=sv)
        s_done = sv.n

        p.wait("scalar", sv, s_done)
        p.wait("tensor", sv, s_done)
        sqd = [0, 0]
        for dt_ in range(16):
            par = dt_ % 2
            if sqd[par]:
                p.wait("scalar", st, sqd[par])
            p.add("scalar", lambda e, dt_=dt_, par=par: e.activation(
                sqt[par][:, :], s_bf[:, dt_ * TH:(dt_ + 1) * TH], AF.Square), inc=ss)
            sd = ss.n
            p.wait("tensor", ss, sd)
            p.add("tensor", lambda e, dt_=dt_: e.matmul(
                PS[0:1, 1024:1536], oc_t[:, :], s_bf[:, dt_ * TH:(dt_ + 1) * TH],
                start=(dt_ == 0), stop=(dt_ == 15)), inc=st)
            p.add("tensor", lambda e, dt_=dt_, par=par: e.matmul(
                PS[0:1, 1536:2048], oc_t[:, :], sqt[par][:, :],
                start=(dt_ == 0), stop=(dt_ == 15)), inc=st)
            sqd[par] = st.n
        p.wait("scalar", st, st.n)
        rws = ln_rows(1024, 1536, TH)
        p.wait("tensor", sv, rws)
        p.add("tensor", lambda e: e.matmul(
            PS[:, 2048:2560], onesf_t[0:1, :], row_a[0:1, 0:TH], start=True, stop=True), inc=st)
        p.add("tensor", lambda e: e.matmul(
            PS[:, 2560:3072], onesf_t[0:1, :], row_b[0:1, 0:TH], start=True, stop=True), inc=st)
        bmm = st.n
        p.wait("scalar", st, bmm)
        p.add("scalar", lambda e: e.copy(ab_bc[:, :], PS[:, 2048:2560]), inc=ss)
        p.add("scalar", lambda e: e.copy(bb_bc[:, :], PS[:, 2560:3072]), inc=ss)
        bev = ss.n
        p.wait("vector", ss, bev)
        if out_store:
            p.wait("vector", sg, out_store)
        for dt_ in range(16):
            p.add("vector", lambda e, dt_=dt_: e.tensor_mul(
                acc[:, dt_ * TH:(dt_ + 1) * TH], acc[:, dt_ * TH:(dt_ + 1) * TH], ab_bc[:, :]), inc=sv)
            p.wait("vector", sv)
            p.add("vector", lambda e, dt_=dt_: e.tensor_add(
                acc[:, dt_ * TH:(dt_ + 1) * TH], acc[:, dt_ * TH:(dt_ + 1) * TH], bb_bc[:, :]), inc=sv)
        x4_done = sv.n
        p.wait("gpsimd", sv, x4_done)
        p.add("gpsimd", lambda e, th=th: e.dma_start(
            out=xo[:, th * TH:(th + 1) * TH].rearrange("(k p) m -> p k m", p=128),
            in_=acc[:, :].rearrange("p (k m) -> p k m", k=16)), inc=sg, dma=True)
        out_store = sg.n
        p.wait("scalar", sg, out_store)
        p.wait("vector", sg, out_store)

    p.wait("gpsimd", sg, sg.n)
    return {"stores": [sg]}


def run_ffn1(inp, x3):
    rw = inp["L1_ffn_router_weight"]
    t64 = lambda w: np.ascontiguousarray(
        w.T.reshape(D, w.shape[0] // 128, 128).transpose(1, 0, 2))
    base = {
        "rwt": np.ascontiguousarray(rw.T.reshape(16, 128, 8).transpose(1, 0, 2).reshape(128, 128)),
        "shg": t64(inp["L1_ffn_sh_gate"]),
        "shu": t64(inp["L1_ffn_sh_up"]),
        "shd": np.ascontiguousarray(
            inp["L1_ffn_sh_down"].T.reshape(8192, 16, 128).transpose(1, 0, 2)),
        "rg": t64(inp["L1_ffn_r_gate"]),
        "ru": t64(inp["L1_ffn_r_up"]),
        "rd": np.ascontiguousarray(
            inp["L1_ffn_r_down"].T.reshape(1024, 16, 128).transpose(1, 0, 2)),
    }
    in_maps = []
    for c in range(8):
        b, j = c // 2, c % 2
        m = dict(base)
        m["x3i"] = np.ascontiguousarray(x3[b][:, j * HALF:(j + 1) * HALF])
        in_maps.append(m)
    nc = build_ffn1()
    res = run_bass_kernel_spmd(nc, in_maps, core_ids=list(range(8)))
    out = np.empty((B, S, D), np.float32)
    for c in range(8):
        b, j = c // 2, c % 2
        out[b, j * HALF:(j + 1) * HALF, :] = res.results[c]["xo"].T
    return out



def build_fused():
    nc = bass.Bass("TRN2", target_bir_lowering=False)
    ioA = mamba_drams(nc, kind="Internal")
    ioB = ffn0_drams(nc, yp_kind="Internal", x2_kind="Internal")
    ioC = attn_drams(nc, x2f_kind="Internal", x3_kind="Internal")
    ioD = ffn1_drams(nc, x3_ap=ioC["x3o"])
    cc1_in = nc.dram_tensor("cc1_in", [DSL, HALF], BF16, kind="Internal").ap()
    cc1_out = nc.dram_tensor("cc1_out", [D, HALF], BF16, kind="Internal").ap()
    cc2_out = nc.dram_tensor("cc2_out", [2 * D, HALF], BF16, kind="Internal").ap()

    ARENA = nc.alloc_sbuf_tensor("ARENA", [128, 52900], F32)
    PSAR = nc.alloc_psum_tensor("PSAR", [128, 4096], F32)
    p = Prog(nc, arena=ARENA, psarena=PSAR)
    groups = [[0, 1], [2, 3], [4, 5], [6, 7]]

    with nc.named_scope("phA"):
        emit_mamba(nc, p, ioA)

    p.barrier()
    p.arena_reset()
    cid = nc.partition_id()
    jv = cid % 2
    ov = 1 - jv
    sb1 = p.sem("sb1")
    scc1 = p.sem("scc1")
    ypre = ioA["ypre"]
    yp = ioB["yp"]
    p.add("sync", lambda e: e.dma_start(
        out=cc1_in, in_=ypre[:, bass.ts(ov, HALF)]), inc=sb1, dma=True)
    p.wait("gpsimd", sb1, sb1.n)
    p.add("gpsimd", lambda e: e.collective_compute(
        "AllGather", ALU.bypass, replica_groups=groups,
        ins=[cc1_in], outs=[cc1_out]), inc=scc1)
    p.wait("sync", scc1, 1)
    p.add("sync", lambda e: e.dma_start(
        out=yp[bass.ts(jv, DSL), :], in_=ypre[:, bass.ts(jv, HALF)]), inc=sb1, dma=True)
    p.add("sync", lambda e: e.dma_start(
        out=yp[bass.ts(ov, DSL), :], in_=cc1_out[bass.ts(ov, DSL), :]), inc=sb1, dma=True)
    p.wait("sync", sb1, sb1.n)
    p.barrier()
    p.arena_reset()

    with nc.named_scope("phB"):
        emit_ffn0(nc, p, ioB)

    p.barrier()
    p.arena_reset()
    sb2 = p.sem("sb2")
    scc2 = p.sem("scc2")
    x2o = ioB["x2o"]
    x2f = ioC["x2f"]
    p.add("gpsimd", lambda e: e.collective_compute(
        "AllGather", ALU.bypass, replica_groups=groups,
        ins=[x2o], outs=[cc2_out]), inc=scc2)
    p.wait("sync", scc2, 1)
    p.add("sync", lambda e: e.dma_start(
        out=x2f[:, 0:HALF], in_=cc2_out[bass.ts(ov, D), :]), inc=sb2, dma=True)
    p.add("sync", lambda e: e.dma_start(
        out=x2f[:, HALF:S], in_=x2o), inc=sb2, dma=True)
    p.wait("sync", sb2, sb2.n)
    p.barrier()
    p.arena_reset()

    with nc.named_scope("phC"):
        emit_attn(nc, p, ioC)

    p.barrier()
    p.arena_reset()

    with nc.named_scope("phD"):
        emit_ffn1(nc, p, ioD)
    p.finalize()
    return nc


def prep_fused(inp, c):
    b, j = c // 2, c % 2
    m = {}
    x = inp["L0_attn_x"]; xpw = inp["L0_attn_x_proj_w"]
    jsl = slice(j * DSL, (j + 1) * DSL)
    if j == 0:
        xb, xpwb = x[b], xpw
    else:
        xb = np.roll(x[b], -DSL, axis=1)
        xpwb = np.roll(xpw, -DSL, axis=1)
    m["xT"] = np.ascontiguousarray(xb.T)
    m["xpwT"] = np.ascontiguousarray(xpwb.T)
    m["dtwT"] = np.ascontiguousarray(inp["L0_attn_dt_proj_w"][jsl].T)
    m["dtb"] = np.ascontiguousarray(inp["L0_attn_dt_proj_b"][jsl].reshape(8, 128).T)
    m["alog"] = np.ascontiguousarray(
        inp["L0_attn_A_log"][jsl].reshape(8, 128, NST).transpose(1, 0, 2).reshape(128, 128))
    m["dp"] = np.ascontiguousarray(inp["L0_attn_D_param"][jsl].reshape(8, 128).T)
    m["wg"] = np.ascontiguousarray(inp["L0_ffn_Wg"])
    m["bg"] = np.ascontiguousarray(inp["L0_ffn_bg"].reshape(1, 8))
    m["wgate"] = np.ascontiguousarray(inp["L0_ffn_Wgate"])
    m["wup"] = np.ascontiguousarray(inp["L0_ffn_Wup"])
    m["wdown"] = np.ascontiguousarray(inp["L0_ffn_Wdown"])
    m["bgate"] = np.ascontiguousarray(
        inp["L0_ffn_bgate"].reshape(6, 8, 128).transpose(2, 0, 1).reshape(128, 48))
    m["bup"] = np.ascontiguousarray(
        inp["L0_ffn_bup"].reshape(6, 8, 128).transpose(2, 0, 1).reshape(128, 48))
    m["bdown6"] = np.ascontiguousarray(inp["L0_ffn_bdown"])
    m["wsup"] = np.ascontiguousarray(inp["L0_ffn_Ws_up"])
    m["bsup"] = np.ascontiguousarray(inp["L0_ffn_bs_up"].reshape(16, 128).T)
    m["wsdown"] = np.ascontiguousarray(inp["L0_ffn_Ws_down"])
    m["bsdown"] = np.ascontiguousarray(inp["L0_ffn_bs_down"].reshape(16, 128).T)
    col16 = lambda v: np.ascontiguousarray(v.reshape(16, 128).T)
    m["wqd"] = np.ascontiguousarray(inp["L1_attn_Wq"])
    m["wkd"] = np.ascontiguousarray(inp["L1_attn_Wk"])
    m["wvd"] = np.ascontiguousarray(inp["L1_attn_Wv"])
    m["wod"] = np.ascontiguousarray(inp["L1_attn_Wo"])
    m["bq"] = col16(inp["L1_attn_bq"]); m["bk"] = col16(inp["L1_attn_bk"])
    m["bvr"] = np.ascontiguousarray(inp["L1_attn_bv"].reshape(1, D))
    m["bo"] = col16(inp["L1_attn_bo"])
    m["lng"] = col16(inp["L1_attn_ln_g"]); m["lnb"] = col16(inp["L1_attn_ln_b"])
    mv = np.zeros((1, S), np.float32)
    if j == 0:
        mv[0, 0:HALF] = -1.0e30
    m["minv"] = mv.astype(BF)
    rw = inp["L1_ffn_router_weight"]
    t64 = lambda w: np.ascontiguousarray(
        w.T.reshape(D, w.shape[0] // 128, 128).transpose(1, 0, 2))
    m["rwt"] = np.ascontiguousarray(rw.T.reshape(16, 128, 8).transpose(1, 0, 2).reshape(128, 128))
    m["shg"] = t64(inp["L1_ffn_sh_gate"])
    m["shu"] = t64(inp["L1_ffn_sh_up"])
    m["shd"] = np.ascontiguousarray(
        inp["L1_ffn_sh_down"].T.reshape(8192, 16, 128).transpose(1, 0, 2))
    m["rg"] = t64(inp["L1_ffn_r_gate"])
    m["ru"] = t64(inp["L1_ffn_r_up"])
    m["rd"] = np.ascontiguousarray(
        inp["L1_ffn_r_down"].T.reshape(1024, 16, 128).transpose(1, 0, 2))
    return m


def kernel(**inputs):
    nc = build_fused()
    in_maps = [prep_fused(inputs, c) for c in range(8)]
    res = run_bass_kernel_spmd(nc, in_maps, core_ids=list(range(8)))
    out = np.empty((B, S, D), np.float32)
    for c in range(8):
        b, j = c // 2, c % 2
        out[b, j * HALF:(j + 1) * HALF, :] = res.results[c]["xo"].T
    return out


def kernel_multilaunch(**inputs):
    ypre = run_mamba(inputs)
    x2 = run_ffn0(inputs, ypre)
    x3 = run_attn(inputs, x2)
    out = run_ffn1(inputs, x3)
    return out




# version 04 (final)
# speedup vs baseline: 1.3853x; 1.2988x over previous
import numpy as np
import ml_dtypes
from contextlib import ExitStack

import concourse.bass as bass
import concourse.mybir as mybir
from concourse.bass_utils import run_bass_kernel_spmd

F32 = mybir.dt.float32
BF16 = mybir.dt.bfloat16
AF = mybir.ActivationFunctionType
ALU = mybir.AluOpType
AX = mybir.AxisListType

ENGINES = ("sync", "scalar", "vector", "tensor", "gpsimd")

B, S, D = 4, 2048, 2048
HALF = S // 2
DSL = D // 2
NST = 16
CH = 512
NCHUNK = S // CH

BF = ml_dtypes.bfloat16
BIGF = 1.0e30
B_STAGE = None
B_NOSILU = False
B_SKIP = set()


class _Ctr:
    __slots__ = ("h", "n")


class Prog:
    def __init__(self, nc, arena=None, psarena=None):
        self.nc = nc
        self.q = {e: [] for e in ENGINES}
        self.stack = ExitStack()
        self.arena = arena
        self.psarena = psarena
        self.aptr = 0
        self.pptr = 0
        self.ctrs = []
        self.nsem = 0

    def sem(self, name):
        c = _Ctr()
        self.nsem += 1
        c.h = self.stack.enter_context(self.nc.semaphore(f"{name}_{self.nsem}"))
        c.n = 0
        self.ctrs.append(c)
        return c

    def arena_reset(self):
        self.aptr = 0
        self.pptr = 0

    def barrier(self):
        for eng in ENGINES:
            for c in self.ctrs:
                self.wait(eng, c, c.n)

    def sbuf(self, name, shape, dtype):
        if self.arena is None:
            return self.stack.enter_context(self.nc.sbuf_tensor(name, shape, dtype))
        cols = shape[1]
        f32c = cols if dtype == F32 else (cols + 1) // 2
        start = (self.aptr + 15) // 16 * 16
        assert start + f32c <= self.arena.shape[1], (name, shape, start)
        self.aptr = start + f32c
        view = self.arena[0:shape[0], start:start + f32c]
        if dtype == F32:
            return view
        return view.bitcast(BF16)[:, 0:cols]

    def psum(self, name, shape, dtype=F32):
        if self.psarena is None:
            return self.stack.enter_context(self.nc.psum_tensor(name, shape, dtype))
        assert dtype == F32
        cols = shape[1]
        start = (self.pptr + 511) // 512 * 512
        assert start + cols <= self.psarena.shape[1], (name, shape, start)
        self.pptr = start + cols
        return self.psarena[0:shape[0], start:start + cols]

    def wait(self, eng, c, val=None):
        v = c.n if val is None else val
        if v <= 0:
            return
        self.q[eng].append(lambda e, h=c.h, v=v: e.wait_ge(h, v))

    safe = False

    def add(self, eng, fn, inc=None, dma=False):
        k = 16 if dma else 1
        if inc is not None:
            inc.n += k

        def emit(e, fn=fn, inc=inc, k=k):
            inst = fn(e)
            if inc is not None:
                inst.then_inc(inc.h, k)
        self.q[eng].append(emit)
        if self.safe and inc is not None and not dma and eng in ("vector", "scalar"):
            self.q[eng].append(lambda e, h=inc.h, v=inc.n: e.wait_ge(h, v))
        return inc.n if inc is not None else 0

    def finalize(self):
        with self.nc.Block() as block:
            for eng in ENGINES:
                if self.q[eng]:
                    getattr(block, eng)(
                        lambda e, fns=tuple(self.q[eng]): [f(e) for f in fns])
        self.stack.close()



def mamba_drams(nc, kind="ExternalOutput"):
    io = {}
    io["xT"] = nc.dram_tensor("xT", [D, S], F32, kind="ExternalInput").ap()
    io["xpwT"] = nc.dram_tensor("xpwT", [D, 160], F32, kind="ExternalInput").ap()
    io["dtwT"] = nc.dram_tensor("dtwT", [128, DSL], F32, kind="ExternalInput").ap()
    io["dtb"] = nc.dram_tensor("dtb", [128, 8], F32, kind="ExternalInput").ap()
    io["alog"] = nc.dram_tensor("alog", [128, 128], F32, kind="ExternalInput").ap()
    io["dp"] = nc.dram_tensor("dp", [128, 8], F32, kind="ExternalInput").ap()
    io["ypre"] = nc.dram_tensor("ypre", [DSL, S], BF16, kind=kind).ap()
    e32 = np.zeros((32, 32 * 128), dtype=np.float32)
    for n in range(32):
        e32[n, n * 128:(n + 1) * 128] = 1.0
    io["E32"] = nc.inline_tensor(e32.astype(BF), name="E32").ap()
    return io


def build_mamba():
    nc = bass.Bass("TRN2", target_bir_lowering=False)
    io = mamba_drams(nc)
    p = Prog(nc)
    emit_mamba(nc, p, io)
    p.finalize()
    return nc


def emit_mamba(nc, p, io):
    xT = io["xT"]; xpwT = io["xpwT"]; dtwT = io["dtwT"]; dtb = io["dtb"]
    alog = io["alog"]; dp = io["dp"]; ypre = io["ypre"]; E32 = io["E32"]
    sq = p.sem("sq")
    sqx = [p.sem("sqx0"), p.sem("sqx1")]
    sg = p.sem("sg")
    sgo = [p.sem("sgo0"), p.sem("sgo1")]
    st = p.sem("st")
    ss = p.sem("ss")
    sv = p.sem("sv")

    xpw_f = p.sbuf("xpw_f", [128, 16 * 160], F32)
    xpw_b = p.sbuf("xpw_b", [128, 16 * 160], BF16)
    dtw_f = p.sbuf("dtw_f", [128, DSL], F32)
    dtw_b = p.sbuf("dtw_b", [128, DSL], BF16)
    dtb_t = p.sbuf("dtb_t", [128, 8], F32)
    alog_t = p.sbuf("alog_t", [128, 128], F32)
    a_t = p.sbuf("a_t", [128, 128], F32)
    dp_t = p.sbuf("dp_t", [128, 8], F32)
    dp1_t = p.sbuf("dp1_t", [128, 8], F32)
    e32_t = p.sbuf("e32_t", [32, 32 * 128], BF16)
    hstate = p.sbuf("hstate", [128, 128], F32)

    xf32 = [p.sbuf(f"xf32_{i}", [128, 16 * CH], F32) for i in range(2)]
    xbf = [p.sbuf(f"xbf_{i}", [128, 16 * CH], BF16) for i in range(2)]
    dtl_b = p.sbuf("dtl_b", [128, CH], BF16)
    bc16 = p.sbuf("bc16", [32, CH], BF16)
    dt_b = p.sbuf("dt_b", [128, 8 * CH], BF16)
    sp1 = p.sbuf("sp1", [128, CH], F32)
    sp2 = p.sbuf("sp2", [128, CH], F32)
    v_b = p.sbuf("v_b", [128, 8 * CH], BF16)
    b_bc = [p.sbuf(f"b_bc{i}", [128, CH], BF16) for i in range(2)]
    c_bc = [p.sbuf(f"c_bc{i}", [128, CH], BF16) for i in range(2)]
    ab_b = [p.sbuf(f"ab_b{i}", [128, CH], F32) for i in range(2)]
    bx_b = [p.sbuf(f"bx_b{i}", [128, CH], BF16) for i in range(2)]
    h_b = p.sbuf("h_b", [128, CH], BF16)
    tmp_b = p.sbuf("tmp_b", [128, CH], BF16)
    y32 = p.sbuf("y32", [128, 8 * CH], F32)
    out_b = [p.sbuf(f"out_b{i}", [128, 8 * CH], BF16) for i in range(2)]

    ps_xp = p.psum("ps_xp", [128, CH], F32)
    ps_bc = p.psum("ps_bc", [32, CH], F32)
    ps_dt = p.psum("ps_dt", [128, CH], F32)
    ps_bb = [p.psum(f"ps_bb{i}", [128, CH], F32) for i in range(2)]
    ps_cb = [p.psum(f"ps_cb{i}", [128, CH], F32) for i in range(2)]

    p.add("sync", lambda e: e.dma_start(
        out=xpw_f[:, :].rearrange("p (k m) -> p k m", k=16),
        in_=xpwT.rearrange("(k p) m -> p k m", p=128)), inc=sq, dma=True)
    p.add("sync", lambda e: e.dma_start(out=dtw_f[:, :], in_=dtwT), inc=sq, dma=True)
    p.add("sync", lambda e: e.dma_start(out=dtb_t[:, :], in_=dtb), inc=sq, dma=True)
    p.add("sync", lambda e: e.dma_start(out=alog_t[:, :], in_=alog), inc=sq, dma=True)
    p.add("sync", lambda e: e.dma_start(out=dp_t[:, :], in_=dp), inc=sq, dma=True)
    p.add("sync", lambda e: e.dma_start(out=e32_t[:, :], in_=E32), inc=sq, dma=True)
    n_static = sq.n

    xld = []
    for k in range(2):
        p.add("sync", lambda e, k=k: e.dma_start(
            out=xf32[k % 2][:, :].rearrange("p (k2 m) -> p k2 m", k2=16),
            in_=xT[:, k * CH:(k + 1) * CH].rearrange("(k2 p) m -> p k2 m", p=128)),
            inc=sqx[k % 2], dma=True)
        xld.append(sqx[k % 2].n)

    p.wait("vector", sq, n_static)
    p.add("vector", lambda e: e.tensor_copy(xpw_b[:, :], xpw_f[:, :]), inc=sv)
    p.add("vector", lambda e: e.tensor_copy(dtw_b[:, :], dtw_f[:, :]), inc=sv)
    p.add("vector", lambda e: e.tensor_scalar_add(dp1_t[:, :], dp_t[:, :], 1.0), inc=sv)
    p.add("vector", lambda e: e.memset(hstate[:, :], 0.0), inc=sv)
    p.wait("scalar", sq, n_static)
    p.add("scalar", lambda e: e.activation(a_t[:, :], alog_t[:, :], AF.Exp), inc=ss)
    p.wait("vector", ss, ss.n)
    p.add("vector", lambda e: e.tensor_scalar_mul(a_t[:, :], a_t[:, :], -1.0), inc=sv)

    prev = {
        "dtl_evac": 0, "bc_evac": 0, "sp_done": 0, "dt_add": 0,
        "bcast_done": [0, 0], "scan_done": [0, 0], "bx_done": [0, 0],
        "ymul_done": [0, 0], "out_store": [0, 0], "stt_done": [0, 0],
        "xcast": [0, 0],
    }

    for k in range(NCHUNK):
        buf = k % 2
        c0 = k * CH

        p.wait("vector", sqx[buf], xld[k])
        p.add("vector", lambda e, buf=buf: e.tensor_copy(xbf[buf][:, :], xf32[buf][:, :]), inc=sv)
        xcast = sv.n
        prev["xcast"][buf] = xcast

        if k + 2 < NCHUNK:
            p.wait("sync", sv, xcast)
            p.add("sync", lambda e, k=k, buf=buf: e.dma_start(
                out=xf32[buf][:, :].rearrange("p (k2 m) -> p k2 m", k2=16),
                in_=xT[:, (k + 2) * CH:(k + 3) * CH].rearrange("(k2 p) m -> p k2 m", p=128)),
                inc=sqx[buf], dma=True)
            xld.append(sqx[buf].n)

        p.wait("tensor", sv, xcast)
        if prev["dtl_evac"]:
            p.wait("tensor", ss, prev["dtl_evac"])
        for kt in range(16):
            p.add("tensor", lambda e, kt=kt, buf=buf: e.matmul(
                ps_xp[:, :], xpw_b[:, kt * 160:kt * 160 + 128],
                xbf[buf][:, kt * CH:(kt + 1) * CH],
                start=(kt == 0), stop=(kt == 15)), inc=st)
        xp_mm = st.n
        if prev["bc_evac"]:
            p.wait("tensor", ss, prev["bc_evac"])
        for kt in range(16):
            p.add("tensor", lambda e, kt=kt, buf=buf: e.matmul(
                ps_bc[:, :], xpw_b[:, kt * 160 + 128:(kt + 1) * 160],
                xbf[buf][:, kt * CH:(kt + 1) * CH],
                start=(kt == 0), stop=(kt == 15)), inc=st)
        bc_mm = st.n

        p.wait("scalar", st, xp_mm)
        if k > 0:
            p.wait("scalar", st, prev["dtmm_last"])
            p.wait("scalar", st, prev["bcast_last"])
        p.add("scalar", lambda e: e.copy(dtl_b[:, :], ps_xp[:, :]), inc=ss)
        prev["dtl_evac"] = ss.n
        p.wait("scalar", st, bc_mm)
        p.add("scalar", lambda e: e.copy(bc16[:, :], ps_bc[:, :]), inc=ss)
        prev["bc_evac"] = ss.n

        p.wait("tensor", ss, prev["dtl_evac"])
        for pt in range(8):
            if pt > 0 or k > 0:
                p.wait("tensor", ss, prev["sp_done"])
            p.add("tensor", lambda e, pt=pt: e.matmul(
                ps_dt[:, :], dtw_b[:, pt * 128:(pt + 1) * 128], dtl_b[:, :],
                start=True, stop=True), inc=st)
            mm_done = st.n
            p.wait("scalar", st, mm_done)
            if prev["dt_add"]:
                p.wait("scalar", sv, prev["dt_add"])
            p.add("scalar", lambda e, pt=pt: e.activation(
                sp1[:, :], ps_dt[:, :], AF.Abs, bias=dtb_t[:, pt:pt + 1]), inc=ss)
            p.add("scalar", lambda e: e.activation(
                sp2[:, :], sp1[:, :], AF.Exp, scale=-1.0), inc=ss)
            p.add("scalar", lambda e: e.activation(
                sp2[:, :], sp2[:, :], AF.Ln, bias=1.0), inc=ss)
            p.add("scalar", lambda e, pt=pt: e.activation(
                sp1[:, :], ps_dt[:, :], AF.Relu, bias=dtb_t[:, pt:pt + 1]), inc=ss)
            prev["sp_done"] = ss.n
            p.wait("vector", ss, prev["sp_done"])
            p.add("vector", lambda e, pt=pt: e.tensor_add(
                dt_b[:, pt * CH:(pt + 1) * CH], sp1[:, :], sp2[:, :]), inc=sv)
            prev["dt_add"] = sv.n
        prev["dtmm_last"] = st.n

        for pt in range(8):
            p.add("vector", lambda e, pt=pt, buf=buf: e.tensor_mul(
                v_b[:, pt * CH:(pt + 1) * CH],
                dt_b[:, pt * CH:(pt + 1) * CH],
                xbf[buf][:, pt * CH:(pt + 1) * CH]), inc=sv)
        v_done = sv.n

        p.wait("scalar", sv, v_done)
        p.wait("tensor", ss, prev["bc_evac"])

        for n in range(NST):
            par = n % 2
            if prev["bcast_done"][par]:
                p.wait("tensor", ss, prev["bcast_done"][par])
            p.add("tensor", lambda e, n=n, par=par: e.matmul(
                ps_bb[par][:, :], e32_t[:, n * 128:(n + 1) * 128], bc16[:, :],
                start=True, stop=True), inc=st)
            p.add("tensor", lambda e, n=n, par=par: e.matmul(
                ps_cb[par][:, :], e32_t[:, (16 + n) * 128:(17 + n) * 128], bc16[:, :],
                start=True, stop=True), inc=st)
            bb_mm = st.n
            prev["bcast_last"] = st.n
            p.wait("scalar", st, bb_mm)
            if prev["ymul_done"][par]:
                p.wait("scalar", sv, prev["ymul_done"][par])
            p.add("scalar", lambda e, par=par: e.copy(b_bc[par][:, :], ps_bb[par][:, :]), inc=ss)
            p.add("scalar", lambda e, par=par: e.copy(c_bc[par][:, :], ps_cb[par][:, :]), inc=ss)
            prev["bcast_done"][par] = ss.n

            for pt in range(8):
                ipar = (n * 8 + pt) % 2
                col = pt * NST + n
                if prev["scan_done"][ipar]:
                    p.wait("scalar", sv, prev["scan_done"][ipar])
                p.add("scalar", lambda e, pt=pt, col=col, ipar=ipar: e.activation(
                    ab_b[ipar][:, :], dt_b[:, pt * CH:(pt + 1) * CH], AF.Exp,
                    scale=a_t[:, col:col + 1]), inc=ss)
                ab_done = ss.n
                p.wait("vector", ss, prev["bcast_done"][par])
                p.add("vector", lambda e, pt=pt, par=par, ipar=ipar: e.tensor_mul(
                    bx_b[ipar][:, :], v_b[:, pt * CH:(pt + 1) * CH], b_bc[par][:, :]), inc=sv)
                prev["bx_done"][ipar] = sv.n
                p.wait("vector", ss, ab_done)
                p.add("vector", lambda e, col=col, ipar=ipar: e.tensor_tensor_scan(
                    h_b[:, :], ab_b[ipar][:, :], bx_b[ipar][:, :], hstate[:, col:col + 1],
                    ALU.mult, ALU.add), inc=sv)
                prev["scan_done"][ipar] = sv.n
                p.wait("vector", sv, prev["scan_done"][ipar])
                p.add("vector", lambda e, col=col: e.tensor_copy(
                    hstate[:, col:col + 1], h_b[:, CH - 1:CH]), inc=sv)
                if n == 0:
                    p.add("vector", lambda e, pt=pt, par=par: e.tensor_mul(
                        y32[:, pt * CH:(pt + 1) * CH], h_b[:, :], c_bc[par][:, :]), inc=sv)
                else:
                    p.add("vector", lambda e, par=par: e.tensor_mul(
                        tmp_b[:, :], h_b[:, :], c_bc[par][:, :]), inc=sv)
                    p.add("vector", lambda e, pt=pt: e.tensor_add(
                        y32[:, pt * CH:(pt + 1) * CH],
                        y32[:, pt * CH:(pt + 1) * CH], tmp_b[:, :]), inc=sv)
                prev["ymul_done"][par] = sv.n

        if prev["out_store"][buf]:
            p.wait("vector", sgo[buf], prev["out_store"][buf])
        for pt in range(8):
            p.add("vector", lambda e, pt=pt, buf=buf: e.scalar_tensor_tensor(
                out_b[buf][:, pt * CH:(pt + 1) * CH],
                xbf[buf][:, pt * CH:(pt + 1) * CH],
                dp1_t[:, pt:pt + 1],
                y32[:, pt * CH:(pt + 1) * CH],
                ALU.mult, ALU.add), inc=sv)
        stt_done = sv.n
        prev["stt_done"][buf] = stt_done
        p.wait("gpsimd", sv, stt_done)
        p.add("gpsimd", lambda e, c0=c0, buf=buf: e.dma_start(
            out=ypre[:, c0:c0 + CH].rearrange("(k2 p) m -> p k2 m", p=128),
            in_=out_b[buf][:, :].rearrange("p (k2 m) -> p k2 m", k2=8)), inc=sgo[buf], dma=True)
        prev["out_store"][buf] = sgo[buf].n

    p.wait("gpsimd", sgo[0], sgo[0].n)
    p.wait("gpsimd", sgo[1], sgo[1].n)
    return {"stores": [sgo[0], sgo[1]]}


def run_mamba(inp):
    x = inp["L0_attn_x"]
    xpw = inp["L0_attn_x_proj_w"]
    dtw = inp["L0_attn_dt_proj_w"]
    dtbv = inp["L0_attn_dt_proj_b"]
    alog = inp["L0_attn_A_log"]
    dpv = inp["L0_attn_D_param"]

    in_maps = []
    for c in range(8):
        b, j = c // 2, c % 2
        jsl = slice(j * DSL, (j + 1) * DSL)
        if j == 0:
            xb, xpwb = x[b], xpw
        else:
            xb = np.roll(x[b], -DSL, axis=1)
            xpwb = np.roll(xpw, -DSL, axis=1)
        in_maps.append({
            "xT": np.ascontiguousarray(xb.T),
            "xpwT": np.ascontiguousarray(xpwb.T),
            "dtwT": np.ascontiguousarray(dtw[jsl].T),
            "dtb": np.ascontiguousarray(dtbv[jsl].reshape(8, 128).T),
            "alog": np.ascontiguousarray(
                alog[jsl].reshape(8, 128, NST).transpose(1, 0, 2).reshape(128, 128)),
            "dp": np.ascontiguousarray(dpv[jsl].reshape(8, 128).T),
        })
    nc = build_mamba()
    res = run_bass_kernel_spmd(nc, in_maps, core_ids=list(range(8)))
    ypre = np.empty((B, D, S), BF)
    for c in range(8):
        b, j = c // 2, c % 2
        ypre[b, j * DSL:(j + 1) * DSL] = res.results[c]["ypre"]
    return ypre


NEXP = 6
HEXP = 1024
TH = 512


def ffn0_drams(nc, yp_kind="ExternalInput", x2_kind="ExternalOutput"):
    io = {}
    io["yp"] = nc.dram_tensor("yp", [D, HALF], BF16, kind=yp_kind).ap()
    io["wg"] = nc.dram_tensor("wg", [D, 8], F32, kind="ExternalInput").ap()
    io["bg"] = nc.dram_tensor("bg", [1, 8], F32, kind="ExternalInput").ap()
    io["wgate"] = nc.dram_tensor("wgate", [NEXP, D, HEXP], F32, kind="ExternalInput").ap()
    io["wup"] = nc.dram_tensor("wup", [NEXP, D, HEXP], F32, kind="ExternalInput").ap()
    io["wdown"] = nc.dram_tensor("wdown", [NEXP, HEXP, D], F32, kind="ExternalInput").ap()
    io["bgate"] = nc.dram_tensor("bgate", [128, 48], F32, kind="ExternalInput").ap()
    io["bup"] = nc.dram_tensor("bup", [128, 48], F32, kind="ExternalInput").ap()
    io["bdown6"] = nc.dram_tensor("bdown6", [NEXP, D], F32, kind="ExternalInput").ap()
    io["wsup"] = nc.dram_tensor("wsup", [D, D], F32, kind="ExternalInput").ap()
    io["bsup"] = nc.dram_tensor("bsup", [128, 16], F32, kind="ExternalInput").ap()
    io["wsdown"] = nc.dram_tensor("wsdown", [D, D], F32, kind="ExternalInput").ap()
    io["bsdown"] = nc.dram_tensor("bsdown", [128, 16], F32, kind="ExternalInput").ap()
    io["x2o"] = nc.dram_tensor("x2o", [D, HALF], BF16, kind=x2_kind).ap()
    io["ones_col"] = nc.inline_tensor(np.ones((128, 1), np.float32).astype(BF), name="ones_col").ap()
    io["ones_row"] = nc.inline_tensor(np.ones((1, 128), np.float32).astype(BF), name="ones_row").ap()
    io["idf"] = nc.inline_tensor(np.eye(128, dtype=np.float32), name="idf").ap()
    e6np = np.zeros((32, 6 * 128), dtype=np.float32)
    for n in range(6):
        e6np[n, n * 128:(n + 1) * 128] = 1.0
    io["E6B"] = nc.inline_tensor(e6np.astype(BF), name="E6B").ap()
    return io


def build_ffn0():
    nc = bass.Bass("TRN2", target_bir_lowering=False)
    io = ffn0_drams(nc)
    p = Prog(nc)
    emit_ffn0(nc, p, io)
    p.finalize()
    return nc


def emit_ffn0(nc, p, io):
    yp = io["yp"]; wg = io["wg"]; bg = io["bg"]; wgate = io["wgate"]
    wup = io["wup"]; wdown = io["wdown"]; bgate = io["bgate"]; bup = io["bup"]
    bdown6 = io["bdown6"]; wsup = io["wsup"]; bsup = io["bsup"]
    wsdown = io["wsdown"]; bsdown = io["bsdown"]; x2o = io["x2o"]
    ones_col = io["ones_col"]; ones_row = io["ones_row"]; idf = io["idf"]
    E6B = io["E6B"]
    sq = p.sem("sq"); sg = p.sem("sg"); st = p.sem("st"); ss = p.sem("ss"); sv = p.sem("sv")
    sqw = [p.sem("sqw0"), p.sem("sqw1")]

    x1 = p.sbuf("x1", [128, 16 * HALF], BF16)
    sqt = [p.sbuf(f"sqt{i}", [128, HALF], BF16) for i in range(2)]
    hid = p.sbuf("hid", [128, 64 * TH], BF16)
    sg_t = p.sbuf("sg_t", [128, 8 * TH], BF16)
    tmp_u = p.sbuf("tmp_u", [128, TH], BF16)
    tmp_h = p.sbuf("tmp_h", [128, TH], BF16)
    wq_f = [p.sbuf(f"wq_f{i}", [128, 2048], F32) for i in range(2)]
    wq_b = [p.sbuf(f"wq_b{i}", [128, 2048], BF16) for i in range(2)]
    s_t = p.sbuf("s_t", [128, 16 * TH], BF16)
    x2h = [p.sbuf("x2h0", [128, 16 * TH], BF16)] * 2
    wbc = p.sbuf("wbc", [128, 6 * TH], BF16)
    wgb = p.sbuf("wgb", [128, 8 * 16], BF16)
    wgf = p.sbuf("wgf", [128, 8 * 16], F32)
    bgr = p.sbuf("bgr", [1, 8], F32)
    bgr_b = p.sbuf("bgr_b", [1, 8], BF16)
    bg_bc = p.sbuf("bg_bc", [128, 8], F32)
    lg_t = p.sbuf("lg_t", [128, 8 * 8], F32)
    wt_t = p.sbuf("wt_t", [128, 8 * 8], F32)
    w_fm = p.sbuf("w_fm", [32, HALF], F32)
    w_fmb = p.sbuf("w_fmb", [32, HALF], BF16)
    r1 = p.sbuf("r1", [128, 1], F32)
    r2 = p.sbuf("r2", [128, 1], F32)
    r3 = p.sbuf("r3", [128, 1], F32)
    rt1 = p.sbuf("rt1", [128, 8], F32)
    rt2 = p.sbuf("rt2", [128, 8], F32)
    bgate_t = p.sbuf("bgate_t", [128, 48], F32)
    bup_t = p.sbuf("bup_t", [128, 48], F32)
    bdown_t = p.sbuf("bdown_t", [32, D], BF16)
    bsup_t = p.sbuf("bsup_t", [128, 16], F32)
    bsdown_t = p.sbuf("bsdown_t", [128, 16], F32)
    oc_t = p.sbuf("oc_t", [128, 1], BF16)
    or_t = p.sbuf("or_t", [1, 128], BF16)
    idf_t = p.sbuf("idf_t", [128, 128], F32)
    e6_t = p.sbuf("e6_t", [32, 6 * 128], BF16)
    row_mu = p.sbuf("row_mu", [1, HALF], F32)
    row_a = p.sbuf("row_a", [1, HALF], F32)
    row_b = p.sbuf("row_b", [1, HALF], F32)
    rowb_rstd = p.sbuf("rowb_rstd", [1, HALF], BF16)
    rowb_nmr = p.sbuf("rowb_nmr", [1, HALF], BF16)
    eps_row = p.sbuf("eps_row", [1, 1], F32)

    def ln_rows(sum_c0, sq_c0, tlen, invn=1.0 / 2048.0):
        p.add("scalar", lambda e: e.activation(
            row_mu[:, 0:tlen], PS[0:1, sum_c0:sum_c0 + tlen], AF.Copy, scale=invn), inc=ss)
        p.add("scalar", lambda e: e.activation(
            row_a[:, 0:tlen], PS[0:1, sq_c0:sq_c0 + tlen], AF.Copy, scale=invn), inc=ss)
        p.wait("scalar", ss)
        p.add("scalar", lambda e: e.activation(
            row_b[:, 0:tlen], row_mu[:, 0:tlen], AF.Square), inc=ss)
        p.wait("vector", ss)
        p.add("vector", lambda e: e.tensor_sub(
            row_a[:, 0:tlen], row_a[:, 0:tlen], row_b[:, 0:tlen]), inc=sv)
        p.wait("scalar", sv)
        p.add("scalar", lambda e: e.activation(
            row_a[:, 0:tlen], row_a[:, 0:tlen], AF.Ln, bias=eps_row[0:1, 0:1]), inc=ss)
        p.wait("scalar", ss)
        p.add("scalar", lambda e: e.activation(
            row_a[:, 0:tlen], row_a[:, 0:tlen], AF.Exp, scale=-0.5), inc=ss)
        p.wait("vector", ss)
        p.add("vector", lambda e: e.tensor_mul(
            row_b[:, 0:tlen], row_mu[:, 0:tlen], row_a[:, 0:tlen]), inc=sv)
        p.wait("vector", sv)
        p.add("vector", lambda e: e.tensor_scalar_mul(
            row_b[:, 0:tlen], row_b[:, 0:tlen], -1.0), inc=sv)
        p.wait("vector", sv)
        p.add("vector", lambda e: e.tensor_copy(rowb_rstd[:, 0:tlen], row_a[:, 0:tlen]), inc=sv)
        p.add("vector", lambda e: e.tensor_copy(rowb_nmr[:, 0:tlen], row_b[:, 0:tlen]), inc=sv)
        return sv.n
    ab_bc = p.sbuf("ab_bc", [128, HALF], BF16)
    bb_bc = p.sbuf("bb_bc", [128, HALF], BF16)

    PS = p.psum("PS", [128, 4096], F32)


    p.add("sync", lambda e: e.dma_start(
        out=x1[:, :].rearrange("p (k m) -> p k m", k=16),
        in_=yp.rearrange("(k p) m -> p k m", p=128)), inc=sq, dma=True)
    p.add("sync", lambda e: e.dma_start(
        out=wgf[:, :].rearrange("p (k m) -> p k m", k=16),
        in_=wg.rearrange("(k p) m -> p k m", p=128)), inc=sq, dma=True)
    p.add("sync", lambda e: e.dma_start(out=bgr[:, :], in_=bg), inc=sq, dma=True)
    p.add("sync", lambda e: e.dma_start(out=bgate_t[:, :], in_=bgate), inc=sq, dma=True)
    p.add("sync", lambda e: e.dma_start(out=bup_t[:, :], in_=bup), inc=sq, dma=True)
    p.add("sync", lambda e: e.dma_start(out=x2h[0][0:6, 0:4096].bitcast(F32), in_=bdown6), inc=sq, dma=True)
    p.add("sync", lambda e: e.dma_start(out=bsup_t[:, :], in_=bsup), inc=sq, dma=True)
    p.add("sync", lambda e: e.dma_start(out=bsdown_t[:, :], in_=bsdown), inc=sq, dma=True)
    p.add("sync", lambda e: e.dma_start(out=oc_t[:, :], in_=ones_col), inc=sq, dma=True)
    p.add("sync", lambda e: e.dma_start(out=or_t[:, :], in_=ones_row), inc=sq, dma=True)
    p.add("sync", lambda e: e.dma_start(out=idf_t[:, :], in_=idf), inc=sq, dma=True)
    p.add("sync", lambda e: e.dma_start(out=e6_t[:, :], in_=E6B), inc=sq, dma=True)
    n_static = sq.n

    p.wait("vector", sq, n_static)
    p.add("vector", lambda e: e.tensor_copy(wgb[:, :], wgf[:, :]), inc=sv)
    p.add("vector", lambda e: e.memset(bdown_t[:, :], 0.0), inc=sv)
    p.wait("vector", sv)
    p.add("vector", lambda e: e.tensor_copy(bdown_t[0:6, :], x2h[0][0:6, 0:4096].bitcast(F32)), inc=sv)
    p.add("vector", lambda e: e.memset(w_fm[:, :], 0.0), inc=sv)
    bdown_cast = sv.n
    p.add("vector", lambda e: e.tensor_copy(bgr_b[:, :], bgr[:, :]), inc=sv)
    p.add("vector", lambda e: e.memset(eps_row[:, :], 1e-5), inc=sv)
    p.wait("scalar", sv, sv.n)
    prep_v = sv.n

    p.wait("scalar", sq, n_static)
    p.wait("tensor", sq, n_static)
    sq_mm_done = [0, 0]
    for kt in range(16):
        par = kt % 2
        if sq_mm_done[par]:
            p.wait("scalar", st, sq_mm_done[par])
        p.add("scalar", lambda e, kt=kt, par=par: e.activation(
            sqt[par][:, :], x1[:, kt * HALF:(kt + 1) * HALF], AF.Square), inc=ss)
        sq_done = ss.n
        p.wait("tensor", ss, sq_done)
        for fh in range(2):
            p.add("tensor", lambda e, kt=kt, fh=fh: e.matmul(
                PS[0:1, fh * 512:(fh + 1) * 512], oc_t[:, :],
                x1[:, kt * HALF + fh * 512:kt * HALF + (fh + 1) * 512],
                start=(kt == 0), stop=(kt == 15)), inc=st)
            p.add("tensor", lambda e, kt=kt, fh=fh, par=par: e.matmul(
                PS[0:1, 1024 + fh * 512:1024 + (fh + 1) * 512], oc_t[:, :],
                sqt[par][:, fh * 512:(fh + 1) * 512],
                start=(kt == 0), stop=(kt == 15)), inc=st)
        sq_mm_done[par] = st.n
    stats_mm = st.n
    p.wait("scalar", st, stats_mm)
    rows_done = ln_rows(0, 1024, 1024)
    p.wait("tensor", sv, rows_done)
    for fh in range(2):
        p.add("tensor", lambda e, fh=fh: e.matmul(
            PS[:, 2048 + fh * 512:2048 + (fh + 1) * 512], or_t[0:1, :],
            rowb_rstd[0:1, fh * 512:(fh + 1) * 512], start=True, stop=True), inc=st)
        p.add("tensor", lambda e, fh=fh: e.matmul(
            PS[:, 3072 + fh * 512:3072 + (fh + 1) * 512], or_t[0:1, :],
            rowb_nmr[0:1, fh * 512:(fh + 1) * 512], start=True, stop=True), inc=st)
    bc_mm = st.n
    p.wait("scalar", st, bc_mm)
    p.add("scalar", lambda e: e.copy(ab_bc[:, :], PS[:, 2048:3072]), inc=ss)
    p.add("scalar", lambda e: e.copy(bb_bc[:, :], PS[:, 3072:4096]), inc=ss)
    bc_evac = ss.n
    p.wait("vector", ss, bc_evac)
    for kt in range(16):
        p.add("vector", lambda e, kt=kt: e.tensor_mul(
            x1[:, kt * HALF:(kt + 1) * HALF], x1[:, kt * HALF:(kt + 1) * HALF], ab_bc[:, :]), inc=sv)
        p.add("vector", lambda e, kt=kt: e.tensor_add(
            x1[:, kt * HALF:(kt + 1) * HALF], x1[:, kt * HALF:(kt + 1) * HALF], bb_bc[:, :]), inc=sv)
    x1_done = sv.n

    if B_STAGE == "ln":
        p.wait("gpsimd", sv, x1_done)
        p.add("gpsimd", lambda e: e.dma_start(
            out=x2o[:, :].rearrange("(k p) m -> p k m", p=128),
            in_=x1[:, :].rearrange("p (k m) -> p k m", k=16)), inc=sg, dma=True)
        p.wait("gpsimd", sg, sg.n)
        p.finalize()
        return nc

    p.wait("tensor", sv, x1_done)
    p.wait("tensor", ss, bc_evac)
    p.add("tensor", lambda e: e.matmul(
        PS[:, 2560:2568], or_t[0:1, :], bgr_b[0:1, :], start=True, stop=True), inc=st)
    bgbc_mm = st.n
    p.wait("scalar", st, bgbc_mm)
    p.add("scalar", lambda e: e.copy(bg_bc[:, :], PS[:, 2560:2568]), inc=ss)
    bgbc_done = ss.n
    for tt in range(8):
        for kt in range(16):
            p.add("tensor", lambda e, tt=tt, kt=kt: e.matmul(
                PS[:, 2048 + tt * 8:2048 + (tt + 1) * 8],
                x1[:, kt * HALF + tt * 128:kt * HALF + (tt + 1) * 128],
                wgb[:, kt * 8:(kt + 1) * 8],
                start=(kt == 0), stop=(kt == 15)), inc=st)
    lg_mm = st.n
    p.wait("scalar", st, lg_mm)
    p.add("scalar", lambda e: e.copy(lg_t[:, :], PS[:, 2048:2112]), inc=ss)
    lg_evac = ss.n
    if B_STAGE == "router1":
        p.wait("gpsimd", ss, lg_evac)
        p.add("gpsimd", lambda e: e.dma_start(out=x2o[0:128, 0:64], in_=lg_t[:, :]), inc=sg, dma=True)
        p.wait("gpsimd", sg, sg.n)
        p.finalize()
        return nc
    p.wait("vector", ss, lg_evac)
    BIG = 1.0e30
    for tt in range(8):
        lg = lg_t[:, tt * 8:(tt + 1) * 8]
        wt = wt_t[:, tt * 8:(tt + 1) * 8]
        def v(fn):
            p.add("vector", fn, inc=sv)
            p.wait("vector", sv)
        v(lambda e, lg=lg: e.tensor_add(lg, lg, bg_bc[:, :]))
        v(lambda e, lg=lg: e.tensor_scalar_mul(rt1[:, :], lg, -1.0))
        v(lambda e: e.reduce_max(r1[:, :], rt1[:, :], axis=AX.X))
        v(lambda e: e.tensor_single_scalar(rt2[:, :], rt1[:, :], r1[:, 0:1], ALU.is_equal))
        v(lambda e: e.scalar_tensor_tensor(
            rt2[:, :], rt2[:, :], -BIG, rt1[:, :], ALU.mult, ALU.add))
        v(lambda e: e.reduce_max(r2[:, :], rt2[:, :], axis=AX.X))
        v(lambda e: e.tensor_single_scalar(rt2[:, :], rt1[:, :], r2[:, 0:1], ALU.is_lt))
        v(lambda e, lg=lg: e.reduce_max(r1[:, :], lg, axis=AX.X))
        v(lambda e: e.tensor_scalar_mul(r3[:, :], r1[:, :], -1.0))
        sv_r = sv.n
        p.wait("scalar", sv, sv_r)
        p.add("scalar", lambda e, lg=lg, wt=wt: e.activation(wt, lg, AF.Exp, bias=r3[:, 0:1]), inc=ss)
        p.wait("scalar", ss)
        p.wait("vector", ss)
        v(lambda e, wt=wt: e.tensor_mul(wt, wt, rt2[:, :]))
        v(lambda e, wt=wt: e.reduce_sum(r1[:, :], wt, axis=AX.X))
        v(lambda e: e.reciprocal(r2[:, :], r1[:, :]))
        v(lambda e, wt=wt: e.tensor_single_scalar(wt, wt, r2[:, 0:1], ALU.mult))
    wt_done = sv.n
    if B_STAGE == "router2":
        p.wait("gpsimd", sv, wt_done)
        p.add("gpsimd", lambda e: e.dma_start(out=x2o[0:128, 0:64], in_=wt_t[:, :]), inc=sg, dma=True)
        p.wait("gpsimd", sg, sg.n)
        p.finalize()
        return nc
    p.wait("tensor", sv, wt_done)
    rowmoves = 0
    for n in range(6):
        if n > 0:
            p.wait("tensor", ss, ss.n)
        for tt in range(8):
            p.add("tensor", lambda e, n=n, tt=tt: e.matmul(
                PS[0:1, 2048 + tt * 128:2048 + (tt + 1) * 128],
                wt_t[:, tt * 8 + n:tt * 8 + n + 1], idf_t[:, :],
                start=True, stop=True), inc=st)
        p.wait("scalar", st, st.n)
        if n > 0:
            p.wait("scalar", sg, rowmoves)
        p.add("scalar", lambda e: e.copy(row_mu[0:1, :], PS[0:1, 2048:3072]), inc=ss)
        p.wait("gpsimd", ss, ss.n)
        p.add("gpsimd", lambda e, n=n: e.dma_start(
            out=w_fm[n:n + 1, :], in_=row_mu[0:1, :]), inc=sg, dma=True)
        rowmoves = sg.n
    p.wait("vector", sg, rowmoves)
    p.add("vector", lambda e: e.tensor_copy(w_fmb[:, :], w_fm[:, :]), inc=sv)
    wfmb_done = sv.n
    wbc_done = ss.n
    if B_STAGE == "router":
        p.wait("gpsimd", ss, wbc_done)
        p.wait("gpsimd", sv, sv.n)
        p.add("gpsimd", lambda e: e.dma_start(out=x2o[0:8, 0:HALF], in_=w_fm[0:8, :]), inc=sg, dma=True)
        p.wait("gpsimd", sg, sg.n)
        p.finalize()
        return nc

    wload = 0
    wcast = {}
    wmm = {}
    ecv = {0: 0, 1: 0}

    def stream_weight(par, src_ap, ncols, mw):
        nonlocal wload
        if "stream" in B_SKIP:
            if par in wmm and wmm[par]:
                p.wait("vector", st, wmm[par])
            p.add("vector", lambda e, par=par, ncols=ncols: e.memset(
                wq_b[par][:, 0:ncols], 0.01), inc=sv)
            wcast[par] = sv.n
            return sv.n
        if par in wcast and wcast[par]:
            p.wait("sync", sv, prep_v if wcast[par] == "PREP" else wcast[par])
        p.add("sync", lambda e, par=par, src_ap=src_ap, ncols=ncols, mw=mw: e.dma_start(
            out=wq_f[par][:, 0:ncols].rearrange("p (k m) -> p k m", m=mw),
            in_=src_ap), inc=sqw[par], dma=True)
        wload = sqw[par].n
        p.wait("vector", sqw[par], wload)
        if par in wmm and wmm[par]:
            p.wait("vector", st, wmm[par])
        p.add("vector", lambda e, par=par, ncols=ncols: e.tensor_copy(
            wq_b[par][:, 0:ncols], wq_f[par][:, 0:ncols]), inc=sv)
        wcast[par] = sv.n
        return sv.n

    if B_STAGE == "evac1":
        p.wait("scalar", sv, sv.n)
        p.wait("scalar", ss, ss.n)
        if "noact" not in B_SKIP:
            for ht in range(8):
                srcap = (lambda e, ht=ht: x1[:, ht * 512:ht * 512 + TH]) if "nops" in B_SKIP else (lambda e, ht=ht: PS[:, ht * 512:(ht + 1) * 512])
                p.add("scalar", lambda e, ht=ht, srcap=srcap: e.activation(
                    sg_t[:, ht * TH:(ht + 1) * TH], srcap(e, ht),
                    AF.Identity if B_NOSILU else AF.Silu,
                    bias=bgate_t[:, ht:ht + 1]), inc=ss)
        p.wait("gpsimd", ss, ss.n)
        if "simplestore" in B_SKIP:
            p.add("gpsimd", lambda e: e.dma_start(
                out=x2o[0:8, 0:HALF], in_=w_fm[0:8, :]), inc=sg, dma=True)
        else:
            p.add("gpsimd", lambda e: e.dma_start(
                out=x2o[0:1024, 0:TH].rearrange("(k p) m -> p k m", p=128),
                in_=sg_t[:, :].rearrange("p (k m) -> p k m", k=8)), inc=sg, dma=True)
        p.wait("gpsimd", sg, sg.n)
        p.finalize()
        return nc

    x2_store = [0, 0]
    down_last = 0
    wfmb_ok = sv.n
    for th in range(2):
        p.wait("tensor", sv, wfmb_ok)
        if down_last:
            p.wait("tensor", ss, ss.n)
        if "wbc" in B_SKIP:
            p.add("vector", lambda e: e.memset(wbc[:, :], 0.001), inc=sv)
        else:
            for n in range(6):
                p.add("tensor", lambda e, n=n, th=th: e.matmul(
                    PS[:, 2048:2560], e6_t[:, n * 128:(n + 1) * 128],
                    w_fmb[0:32, th * TH:(th + 1) * TH], start=True, stop=True), inc=st)
                wbc_mm = st.n
                p.wait("scalar", st, wbc_mm)
                p.add("scalar", lambda e, n=n: e.copy(
                    wbc[:, n * TH:(n + 1) * TH], PS[:, 2048:2560]), inc=ss)
                p.wait("tensor", ss, ss.n)
        wbc_ready = ss.n
        p.wait("vector", ss, wbc_ready)
        for n in range(NEXP):
            if "gmm" in B_SKIP:
                for ht in range(8):
                    p.add("vector", lambda e, ht=ht: e.memset(PS[:, ht * 512:(ht + 1) * 512], 0.25), inc=sv)
                p.wait("scalar", sv, sv.n)
            else:
                for q in range(8):
                    par = q % 2
                    src = wgate[n, q * 256:(q + 1) * 256, :].rearrange("(k p) m -> p k m", p=128)
                    cdone = stream_weight(par, src, 2048, 1024)
                    p.wait("tensor", sv, cdone)
                    for kl in range(2):
                        kt = q * 2 + kl
                        for ht in range(8):
                            p.add("tensor", lambda e, kl=kl, kt=kt, ht=ht, par=par, th=th: e.matmul(
                                PS[:, ht * 512:(ht + 1) * 512],
                                wq_b[par][:, kl * 1024 + ht * 128:kl * 1024 + (ht + 1) * 128],
                                x1[:, kt * HALF + th * TH:kt * HALF + th * TH + TH],
                                start=(kt == 0), stop=(kt == 15)), inc=st)
                    wmm[par] = st.n
            g_mm = st.n
            p.wait("scalar", st, g_mm)
            if ecv[0]:
                p.wait("scalar", sv, ecv[0])
            for ht in range(8):
                p.add("scalar", lambda e, n=n, ht=ht: e.activation(
                    sg_t[:, ht * TH:(ht + 1) * TH], PS[:, ht * 512:(ht + 1) * 512],
                    AF.Identity if B_NOSILU else AF.Silu,
                    bias=bgate_t[:, n * 8 + ht:n * 8 + ht + 1]), inc=ss)
            g_evac = ss.n
            if B_STAGE == "hid1" and n == 0 and th == 0:
                p.wait("gpsimd", ss, g_evac)
                p.add("gpsimd", lambda e: e.dma_start(
                    out=x2o[0:1024, 0:TH].rearrange("(k p) m -> p k m", p=128),
                    in_=sg_t[:, :].rearrange("p (k m) -> p k m", k=8)), inc=sg, dma=True)
                p.wait("gpsimd", sg, sg.n)
                p.finalize()
                return nc
            p.wait("tensor", ss, g_evac)
            for q in range(8):
                par = q % 2
                src = wup[n, q * 256:(q + 1) * 256, :].rearrange("(k p) m -> p k m", p=128)
                cdone = stream_weight(par, src, 2048, 1024)
                p.wait("tensor", sv, cdone)
                for kl in range(2):
                    kt = q * 2 + kl
                    for ht in range(8):
                        p.add("tensor", lambda e, kl=kl, kt=kt, ht=ht, par=par, th=th: e.matmul(
                            PS[:, ht * 512:(ht + 1) * 512],
                            wq_b[par][:, kl * 1024 + ht * 128:kl * 1024 + (ht + 1) * 128],
                            x1[:, kt * HALF + th * TH:kt * HALF + th * TH + TH],
                            start=(kt == 0), stop=(kt == 15)), inc=st)
                wmm[par] = st.n
            u_mm = st.n
            p.wait("scalar", st, u_mm)
            for ht in range(8):
                p.add("scalar", lambda e, n=n, ht=ht: e.activation(
                    tmp_u[:, :], PS[:, ht * 512:(ht + 1) * 512],
                    AF.Identity, bias=bup_t[:, n * 8 + ht:n * 8 + ht + 1]), inc=ss)
                u_ev = ss.n
                p.wait("vector", ss, u_ev)
                p.add("vector", lambda e, ht=ht: e.tensor_mul(
                    tmp_h[:, :], tmp_u[:, :], sg_t[:, ht * TH:(ht + 1) * TH]), inc=sv)
                p.wait("vector", sv)
                p.add("vector", lambda e, n=n, ht=ht, th=th: e.tensor_mul(
                    hid[:, (n * 8 + ht) * TH:(n * 8 + ht + 1) * TH],
                    tmp_h[:, :], wbc[:, n * TH:(n + 1) * TH]), inc=sv)
                p.wait("scalar", sv, sv.n - 1)
            ecv[0] = sv.n
        for half2 in range(2):
            for q in range(8):
                par = q % 2
                src = wsup[q * 256:(q + 1) * 256, half2 * 1024:(half2 + 1) * 1024].rearrange(
                    "(k p) m -> p k m", p=128)
                cdone = stream_weight(par, src, 2048, 1024)
                p.wait("tensor", sv, cdone)
                for kl in range(2):
                    kt = q * 2 + kl
                    for ht in range(8):
                        p.add("tensor", lambda e, kl=kl, kt=kt, ht=ht, par=par, th=th: e.matmul(
                            PS[:, ht * 512:(ht + 1) * 512],
                            wq_b[par][:, kl * 1024 + ht * 128:kl * 1024 + (ht + 1) * 128],
                            x1[:, kt * HALF + th * TH:kt * HALF + th * TH + TH],
                            start=(kt == 0), stop=(kt == 15)), inc=st)
                wmm[par] = st.n
            su_mm = st.n
            p.wait("scalar", st, su_mm)
            for ht in range(8):
                hcol = half2 * 8 + ht
                p.add("scalar", lambda e, hcol=hcol, ht=ht: e.activation(
                    hid[:, (48 + hcol) * TH:(48 + hcol + 1) * TH],
                    PS[:, ht * 512:(ht + 1) * 512],
                    AF.Silu, bias=bsup_t[:, hcol:hcol + 1]), inc=ss)
            p.wait("tensor", ss, ss.n)
        hid_done_s = ss.n
        hid_done_v = sv.n
        if B_STAGE == "hid" and th == 0:
            p.wait("gpsimd", ss, hid_done_s)
            p.wait("gpsimd", sv, hid_done_v)
            p.add("gpsimd", lambda e: e.dma_start(
                out=x2o[:, 0:TH].rearrange("(k p) m -> p k m", p=128),
                in_=hid[:, 0:16 * TH].rearrange("p (k m) -> p k m", k=16)), inc=sg, dma=True)
            p.wait("gpsimd", sg, sg.n)
            p.finalize()
            return nc

        p.wait("tensor", ss, hid_done_s)
        p.wait("tensor", sv, hid_done_v)
        for dg in range(4):
            if down_last:
                p.wait("tensor", ss, down_last)
            first = True
            for n in range(NEXP):
                for hc in range(2):
                    src = wdown[n, hc * 512:(hc + 1) * 512, dg * 512:(dg + 1) * 512].rearrange(
                        "(k p) m -> p k m", p=128)
                    par = (n * 2 + hc) % 2
                    cdone = stream_weight(par, src, 2048, 512)
                    p.wait("tensor", sv, cdone)
                    for kl in range(4):
                        kt = hc * 4 + kl
                        for dt_ in range(4):
                            p.add("tensor", lambda e, n=n, kl=kl, kt=kt, dt_=dt_, par=par, fr=first and kt == 0: e.matmul(
                                PS[:, dt_ * 512:(dt_ + 1) * 512],
                                wq_b[par][:, kl * 512 + dt_ * 128:kl * 512 + (dt_ + 1) * 128],
                                hid[:, (n * 8 + kt) * TH:(n * 8 + kt + 1) * TH],
                                start=fr, stop=False), inc=st)
                    first = False
                    wmm[par] = st.n
            for qc in range(4):
                src = wsdown[qc * 512:(qc + 1) * 512, dg * 512:(dg + 1) * 512].rearrange(
                    "(k p) m -> p k m", p=128)
                par = qc % 2
                cdone = stream_weight(par, src, 2048, 512)
                p.wait("tensor", sv, cdone)
                for kl in range(4):
                    kt = qc * 4 + kl
                    for dt_ in range(4):
                        p.add("tensor", lambda e, kl=kl, kt=kt, dt_=dt_, par=par: e.matmul(
                            PS[:, dt_ * 512:(dt_ + 1) * 512],
                            wq_b[par][:, kl * 512 + dt_ * 128:kl * 512 + (dt_ + 1) * 128],
                            hid[:, (48 + kt) * TH:(48 + kt + 1) * TH],
                            start=False, stop=False), inc=st)
                wmm[par] = st.n
            for dt_ in range(4):
                dtile = dg * 4 + dt_
                p.add("tensor", lambda e, dt_=dt_, dtile=dtile, th=th: e.matmul(
                    PS[:, dt_ * 512:(dt_ + 1) * 512],
                    bdown_t[0:32, dtile * 128:(dtile + 1) * 128],
                    w_fmb[0:32, th * TH:(th + 1) * TH],
                    start=False, stop=True), inc=st)
            dg_mm = st.n
            p.wait("scalar", st, dg_mm)
            for dt_ in range(4):
                dtile = dg * 4 + dt_
                p.add("scalar", lambda e, dt_=dt_, dtile=dtile: e.activation(
                    tmp_h[:, :], PS[:, dt_ * 512:(dt_ + 1) * 512],
                    AF.Identity, bias=bsdown_t[:, dtile:dtile + 1]), inc=ss)
                ev = ss.n
                p.wait("vector", ss, ev)
                p.add("vector", lambda e, dtile=dtile, th=th: e.tensor_add(
                    s_t[:, dtile * TH:(dtile + 1) * TH], tmp_h[:, :],
                    x1[:, dtile * HALF + th * TH:dtile * HALF + th * TH + TH]), inc=sv)
                p.wait("scalar", sv, sv.n)
            down_last = ss.n
        s_done_v = sv.n

        p.wait("tensor", sv, s_done_v)
        sq2_done = [0, 0]
        for dtile in range(16):
            par = dtile % 2
            if sq2_done[par]:
                p.wait("scalar", st, sq2_done[par])
            p.add("scalar", lambda e, dtile=dtile, par=par: e.activation(
                sqt[par][:, 0:TH], s_t[:, dtile * TH:(dtile + 1) * TH], AF.Square), inc=ss)
            sqd = ss.n
            p.wait("tensor", ss, sqd)
            p.add("tensor", lambda e, dtile=dtile: e.matmul(
                PS[:, 2048:2560][0:1, :], oc_t[:, :],
                s_t[:, dtile * TH:(dtile + 1) * TH],
                start=(dtile == 0), stop=(dtile == 15)), inc=st)
            p.add("tensor", lambda e, dtile=dtile, par=par: e.matmul(
                PS[:, 2560:3072][0:1, :], oc_t[:, :], sqt[par][:, 0:TH],
                start=(dtile == 0), stop=(dtile == 15)), inc=st)
            sq2_done[par] = st.n
        st_mm = st.n
        p.wait("scalar", st, st_mm)
        rws = ln_rows(2048, 2560, TH)
        p.wait("tensor", sv, rws)
        p.add("tensor", lambda e: e.matmul(
            PS[:, 3072:3584], or_t[0:1, :], rowb_rstd[0:1, 0:TH], start=True, stop=True), inc=st)
        p.add("tensor", lambda e: e.matmul(
            PS[:, 3584:4096], or_t[0:1, :], rowb_nmr[0:1, 0:TH], start=True, stop=True), inc=st)
        bmm = st.n
        p.wait("scalar", st, bmm)
        p.add("scalar", lambda e: e.copy(ab_bc[:, 0:TH], PS[:, 3072:3584]), inc=ss)
        p.add("scalar", lambda e: e.copy(bb_bc[:, 0:TH], PS[:, 3584:4096]), inc=ss)
        bev = ss.n
        p.wait("vector", ss, bev)
        if th == 1 and x2_store[0]:
            p.wait("vector", sg, x2_store[0])
        for dtile in range(16):
            p.add("vector", lambda e, dtile=dtile, th=th: e.tensor_mul(
                x2h[th][:, dtile * TH:(dtile + 1) * TH],
                s_t[:, dtile * TH:(dtile + 1) * TH], ab_bc[:, 0:TH]), inc=sv)
            p.add("vector", lambda e, dtile=dtile, th=th: e.tensor_add(
                x2h[th][:, dtile * TH:(dtile + 1) * TH],
                x2h[th][:, dtile * TH:(dtile + 1) * TH], bb_bc[:, 0:TH]), inc=sv)
        x2_done = sv.n
        p.wait("gpsimd", sv, x2_done)
        p.add("gpsimd", lambda e, th=th: e.dma_start(
            out=x2o[:, th * TH:(th + 1) * TH].rearrange("(k p) m -> p k m", p=128),
            in_=x2h[th][:, :].rearrange("p (k m) -> p k m", k=16)), inc=sg, dma=True)
        x2_store[th] = sg.n
        p.wait("tensor", ss, ss.n)

    p.wait("gpsimd", sg, sg.n)
    return {"stores": [sg]}


def run_ffn0(inp, ypre):
    wgv = inp["L0_ffn_Wg"]; bgv = inp["L0_ffn_bg"]
    wgate = inp["L0_ffn_Wgate"]; bgate = inp["L0_ffn_bgate"]
    wupv = inp["L0_ffn_Wup"]; bupv = inp["L0_ffn_bup"]
    wdownv = inp["L0_ffn_Wdown"]; bdownv = inp["L0_ffn_bdown"]
    wsupv = inp["L0_ffn_Ws_up"]; bsupv = inp["L0_ffn_bs_up"]
    wsdownv = inp["L0_ffn_Ws_down"]; bsdownv = inp["L0_ffn_bs_down"]

    base = {
        "wg": np.ascontiguousarray(wgv),
        "bg": np.ascontiguousarray(bgv.reshape(1, 8)),
        "wgate": np.ascontiguousarray(wgate),
        "wup": np.ascontiguousarray(wupv),
        "wdown": np.ascontiguousarray(wdownv),
        "bgate": np.ascontiguousarray(bgate.reshape(6, 8, 128).transpose(2, 0, 1).reshape(128, 48)),
        "bup": np.ascontiguousarray(bupv.reshape(6, 8, 128).transpose(2, 0, 1).reshape(128, 48)),
        "bdown6": np.ascontiguousarray(bdownv),
        "wsup": np.ascontiguousarray(wsupv),
        "bsup": np.ascontiguousarray(bsupv.reshape(16, 128).T),
        "wsdown": np.ascontiguousarray(wsdownv),
        "bsdown": np.ascontiguousarray(bsdownv.reshape(16, 128).T),
    }
    in_maps = []
    for c in range(8):
        b, j = c // 2, c % 2
        m = dict(base)
        m["yp"] = np.ascontiguousarray(ypre[b][:, j * HALF:(j + 1) * HALF])
        in_maps.append(m)
    nc = build_ffn0()
    res = run_bass_kernel_spmd(nc, in_maps, core_ids=list(range(8)))
    x2 = np.empty((B, D, S), BF)
    for c in range(8):
        b, j = c // 2, c % 2
        x2[b][:, j * HALF:(j + 1) * HALF] = res.results[c]["x2o"]
    return x2


NH = 16
DH = 128


def attn_drams(nc, x2f_kind="ExternalInput", x3_kind="ExternalOutput"):
    io = {}
    io["x2f"] = nc.dram_tensor("x2f", [D, S], BF16, kind=x2f_kind).ap()
    io["wqd"] = nc.dram_tensor("wqd", [D, D], F32, kind="ExternalInput").ap()
    io["wkd"] = nc.dram_tensor("wkd", [D, D], F32, kind="ExternalInput").ap()
    io["wvd"] = nc.dram_tensor("wvd", [D, D], F32, kind="ExternalInput").ap()
    io["wod"] = nc.dram_tensor("wod", [D, D], F32, kind="ExternalInput").ap()
    io["bq"] = nc.dram_tensor("bq", [128, 16], F32, kind="ExternalInput").ap()
    io["bk"] = nc.dram_tensor("bk", [128, 16], F32, kind="ExternalInput").ap()
    io["bvr"] = nc.dram_tensor("bvr", [1, D], F32, kind="ExternalInput").ap()
    io["bo"] = nc.dram_tensor("bo", [128, 16], F32, kind="ExternalInput").ap()
    io["lng"] = nc.dram_tensor("lng", [128, 16], F32, kind="ExternalInput").ap()
    io["lnb"] = nc.dram_tensor("lnb", [128, 16], F32, kind="ExternalInput").ap()
    io["minv"] = nc.dram_tensor("minv", [1, S], BF16, kind="ExternalInput").ap()
    io["x3o"] = nc.dram_tensor("x3o", [D, HALF], BF16, kind=x3_kind).ap()
    io["kscr"] = nc.dram_tensor("kscr", [D, S], BF16, kind="Internal").ap()
    io["vscr"] = nc.dram_tensor("vscr", [S, D], BF16, kind="Internal").ap()
    tri_np = np.where(np.arange(128)[None, :] <= np.arange(128)[:, None], 0.0, -1.0e30)
    io["TRI"] = nc.inline_tensor(tri_np.astype(np.float32).astype(BF), name="TRI").ap()
    io["ones_colC"] = nc.inline_tensor(np.ones((128, 1), np.float32).astype(BF), name="ones_colC").ap()
    io["ones_rowC"] = nc.inline_tensor(np.ones((1, 128), np.float32).astype(BF), name="ones_rowC").ap()
    io["idfC"] = nc.inline_tensor(np.eye(128, dtype=np.float32), name="idfC").ap()
    io["idbC"] = nc.inline_tensor(np.eye(128, dtype=np.float32).astype(BF), name="idbC").ap()
    io["onesf_rowC"] = nc.inline_tensor(np.ones((1, 128), np.float32), name="onesf_rowC").ap()
    return io


def build_attn():
    nc = bass.Bass("TRN2", target_bir_lowering=False)
    io = attn_drams(nc)
    p = Prog(nc)
    emit_attn(nc, p, io)
    p.finalize()
    return nc


def emit_attn(nc, p, io):
    x2f = io["x2f"]; wqd = io["wqd"]; wkd = io["wkd"]; wvd = io["wvd"]; wod = io["wod"]
    bq = io["bq"]; bk = io["bk"]; bvr = io["bvr"]; bo = io["bo"]
    lng = io["lng"]; lnb = io["lnb"]; minv = io["minv"]; x3o = io["x3o"]
    kscr = io["kscr"]; vscr = io["vscr"]
    TRI = io["TRI"]; ones_col = io["ones_colC"]; ones_row = io["ones_rowC"]; idb = io["idfC"]
    idbb = io["idbC"]; onesf_row = io["onesf_rowC"]
    sq = p.sem("sq"); sg = p.sem("sg"); st = p.sem("st"); ss = p.sem("ss"); sv = p.sem("sv")
    sqx = [p.sem("sqx0"), p.sem("sqx1")]
    sqw = [p.sem("sqw0"), p.sem("sqw1")]
    sgk = [p.sem("sgk0"), p.sem("sgk1")]
    sqh = [p.sem("sqh0"), p.sem("sqh1")]

    x2o_t = p.sbuf("x2o_t", [128, 16 * HALF], BF16)
    xn = p.sbuf("xn", [128, 16 * HALF], BF16)
    sqt = [p.sbuf(f"sqt{i}", [128, 512], BF16) for i in range(2)]
    q_sb = p.sbuf("q_sb", [128, 16 * HALF], BF16)
    kv_t = [p.sbuf(f"kv_t{i}", [128, 1024], BF16) for i in range(2)]
    wq_f = [p.sbuf(f"wq_f{i}", [128, 2048], F32) for i in range(2)]
    wq_b = [p.sbuf(f"wq_b{i}", [128, 2048], BF16) for i in range(2)]
    kh = [p.sbuf(f"kh{i}", [128, 2048], BF16) for i in range(2)]
    vh = [p.sbuf(f"vh{i}", [128, 2048], BF16) for i in range(2)]
    a_sb = [p.sbuf(f"a_sb{i}", [128, 2048], F32) for i in range(2)]
    at_sb = [p.sbuf(f"at_sb{i}", [128, 2048], BF16) for i in range(2)]
    av_sb = xn
    s_t = q_sb
    tri_t = p.sbuf("tri_t", [128, 128], BF16)
    idb_b = p.sbuf("idb_b", [128, 128], BF16)
    onesf_r = p.sbuf("onesf_r", [1, 128], F32)
    rlcrow = p.sbuf("rlcrow", [1, 128], F32)
    bc_sb = p.sbuf("bc_sb", [128, 128], F32)
    minv_b = p.sbuf("minv_b", [1, S], BF16)
    bq_t = p.sbuf("bq_t", [128, 16], F32)
    bk_t = p.sbuf("bk_t", [128, 16], F32)
    bv_b = p.sbuf("bv_b", [1, D], BF16)
    bo_t = p.sbuf("bo_t", [128, 16], F32)
    lng_t = p.sbuf("lng_t", [128, 16], F32)
    lnb_t = p.sbuf("lnb_t", [128, 16], F32)
    oc_t = p.sbuf("oc_t", [128, 1], BF16)
    or_t = p.sbuf("or_t", [1, 128], BF16)
    idb_t = p.sbuf("idb_t", [128, 128], F32)
    m1c = [p.sbuf(f"m1c{i}", [128, 1], F32) for i in range(2)]
    nm1c = [p.sbuf(f"nm1c{i}", [128, 1], F32) for i in range(2)]
    lc = [p.sbuf(f"lc{i}", [128, 1], F32) for i in range(2)]
    rlc = [p.sbuf(f"rlc{i}", [128, 1], F32) for i in range(2)]
    row_mu = p.sbuf("row_mu", [1, HALF], F32)
    row_a = p.sbuf("row_a", [1, HALF], F32)
    row_b = p.sbuf("row_b", [1, HALF], F32)
    rowb_rstd = p.sbuf("rowb_rstd", [1, HALF], BF16)
    rowb_nmr = p.sbuf("rowb_nmr", [1, HALF], BF16)
    eps_row = p.sbuf("eps_row", [1, 1], F32)
    ab_bc = p.sbuf("ab_bc", [128, HALF], BF16)
    bb_bc = p.sbuf("bb_bc", [128, HALF], BF16)

    PS = p.psum("PS", [128, 4096], F32)

    SCALE = float(1.0 / np.sqrt(np.float32(DH)))

    for dst, srct in [(bq_t, bq), (bk_t, bk), (bo_t, bo),
                      (lng_t, lng), (lnb_t, lnb), (minv_b, minv),
                      (tri_t, TRI), (oc_t, ones_col), (or_t, ones_row),
                      (idb_t, idb), (idb_b, idbb), (onesf_r, onesf_row)]:
        p.add("sync", lambda e, dst=dst, srct=srct: e.dma_start(out=dst[:, :], in_=srct), inc=sq, dma=True)
    p.add("sync", lambda e: e.dma_start(out=wq_f[0][0:1, :], in_=bvr), inc=sq, dma=True)
    n_static = sq.n
    p.wait("vector", sq, n_static)
    p.add("vector", lambda e: e.memset(eps_row[:, :], 1e-5), inc=sv)
    p.add("vector", lambda e: e.tensor_copy(bv_b[:, :], wq_f[0][0:1, :]), inc=sv)
    prep_v = sv.n
    p.wait("scalar", sv, prep_v)
    p.wait("tensor", sv, prep_v)

    def ln_rows(sum_c0, sq_c0, tlen, invn=1.0 / 2048.0):
        p.add("scalar", lambda e: e.activation(
            row_mu[:, 0:tlen], PS[0:1, sum_c0:sum_c0 + tlen], AF.Copy, scale=invn), inc=ss)
        p.add("scalar", lambda e: e.activation(
            row_a[:, 0:tlen], PS[0:1, sq_c0:sq_c0 + tlen], AF.Copy, scale=invn), inc=ss)
        p.wait("scalar", ss)
        p.add("scalar", lambda e: e.activation(
            row_b[:, 0:tlen], row_mu[:, 0:tlen], AF.Square), inc=ss)
        p.wait("vector", ss)
        p.add("vector", lambda e: e.tensor_sub(
            row_a[:, 0:tlen], row_a[:, 0:tlen], row_b[:, 0:tlen]), inc=sv)
        p.wait("scalar", sv)
        p.add("scalar", lambda e: e.activation(
            row_a[:, 0:tlen], row_a[:, 0:tlen], AF.Ln, bias=eps_row[0:1, 0:1]), inc=ss)
        p.wait("scalar", ss)
        p.add("scalar", lambda e: e.activation(
            row_a[:, 0:tlen], row_a[:, 0:tlen], AF.Exp, scale=-0.5), inc=ss)
        p.wait("vector", ss)
        p.add("vector", lambda e: e.tensor_mul(
            row_b[:, 0:tlen], row_mu[:, 0:tlen], row_a[:, 0:tlen]), inc=sv)
        p.wait("vector", sv)
        p.add("vector", lambda e: e.tensor_scalar_mul(
            row_b[:, 0:tlen], row_b[:, 0:tlen], -1.0), inc=sv)
        p.wait("vector", sv)
        p.add("vector", lambda e: e.tensor_copy(rowb_rstd[:, 0:tlen], row_a[:, 0:tlen]), inc=sv)
        p.add("vector", lambda e: e.tensor_copy(rowb_nmr[:, 0:tlen], row_b[:, 0:tlen]), inc=sv)
        return sv.n

    wmm = {}
    wcast = {0: "PREP"}
    kv_store = [0, 0]

    def stream_w(par, src_ap):
        if par in wcast and wcast[par]:
            p.wait("sync", sv, prep_v if wcast[par] == "PREP" else wcast[par])
        p.add("sync", lambda e, par=par, src_ap=src_ap: e.dma_start(
            out=wq_f[par][:, :].rearrange("p (k m) -> p k m", k=16), in_=src_ap),
            inc=sqw[par], dma=True)
        p.wait("vector", sqw[par], sqw[par].n)
        if par in wmm and wmm[par]:
            p.wait("vector", st, wmm[par])
        p.add("vector", lambda e, par=par: e.tensor_copy(wq_b[par][:, :], wq_f[par][:, :]), inc=sv)
        wcast[par] = sv.n
        return sv.n

    def stream_w_chunk(par, src_ap):
        if par in wcast and wcast[par]:
            p.wait("sync", sv, prep_v if wcast[par] == "PREP" else wcast[par])
        p.add("sync", lambda e, par=par, src_ap=src_ap: e.dma_start(
            out=wq_f[par][:, 0:512], in_=src_ap), inc=sqw[par], dma=True)
        p.wait("vector", sqw[par], sqw[par].n)
        if par in wmm and wmm[par]:
            p.wait("vector", st, wmm[par])
        p.add("vector", lambda e, par=par: e.tensor_copy(wq_b[par][:, 0:512], wq_f[par][:, 0:512]), inc=sv)
        wcast[par] = sv.n
        return sv.n

    xn_users = 0
    for hf in range(2):
        if xn_users:
            p.wait("sync", st, xn_users)
        p.add("sync", lambda e, hf=hf: e.dma_start(
            out=xn[:, :].rearrange("p (k m) -> p k m", k=16),
            in_=x2f[:, hf * HALF:(hf + 1) * HALF].rearrange("(k p) m -> p k m", p=128)),
            inc=sqx[hf], dma=True)
        p.wait("vector", sqx[hf], sqx[hf].n)
        p.wait("scalar", sqx[hf], sqx[hf].n)
        p.wait("tensor", sqx[hf], sqx[hf].n)

        sqd = [0, 0]
        for kt in range(16):
            for fh in range(2):
                par = (kt * 2 + fh) % 2
                if sqd[par]:
                    p.wait("scalar", st, sqd[par])
                p.add("scalar", lambda e, kt=kt, fh=fh, par=par: e.activation(
                    sqt[par][:, :], xn[:, kt * HALF + fh * 512:kt * HALF + (fh + 1) * 512],
                    AF.Square), inc=ss)
                sd = ss.n
                p.wait("tensor", ss, sd)
                p.add("tensor", lambda e, kt=kt, fh=fh: e.matmul(
                    PS[0:1, 2048 + fh * 512:2048 + (fh + 1) * 512], oc_t[:, :],
                    xn[:, kt * HALF + fh * 512:kt * HALF + (fh + 1) * 512],
                    start=(kt == 0), stop=(kt == 15)), inc=st)
                p.add("tensor", lambda e, kt=kt, fh=fh, par=par: e.matmul(
                    PS[0:1, 1024 + fh * 512:1024 + (fh + 1) * 512], oc_t[:, :],
                    sqt[par][:, :], start=(kt == 0), stop=(kt == 15)), inc=st)
                sqd[par] = st.n
        p.wait("scalar", st, st.n)
        rws = ln_rows(2048, 1024, HALF)
        p.wait("tensor", sv, rws)
        for fh in range(2):
            p.add("tensor", lambda e, fh=fh: e.matmul(
                PS[:, 1024 + fh * 512:1024 + (fh + 1) * 512], or_t[0:1, :],
                rowb_rstd[0:1, fh * 512:(fh + 1) * 512], start=True, stop=True), inc=st)
            p.add("tensor", lambda e, fh=fh: e.matmul(
                PS[:, 2048 + fh * 512:2048 + (fh + 1) * 512], or_t[0:1, :],
                rowb_nmr[0:1, fh * 512:(fh + 1) * 512], start=True, stop=True), inc=st)
        bcmm = st.n
        p.wait("scalar", st, bcmm)
        p.add("scalar", lambda e: e.copy(ab_bc[:, :], PS[:, 1024:2048]), inc=ss)
        p.add("scalar", lambda e: e.copy(bb_bc[:, :], PS[:, 2048:3072]), inc=ss)
        bce = ss.n
        p.wait("vector", ss, bce)
        for kt in range(16):
            p.add("vector", lambda e, kt=kt: e.tensor_mul(
                xn[:, kt * HALF:(kt + 1) * HALF], xn[:, kt * HALF:(kt + 1) * HALF], ab_bc[:, :]), inc=sv)
            p.wait("vector", sv)
            p.add("vector", lambda e, kt=kt: e.tensor_add(
                xn[:, kt * HALF:(kt + 1) * HALF], xn[:, kt * HALF:(kt + 1) * HALF], bb_bc[:, :]), inc=sv)
        zdone = sv.n
        p.wait("scalar", sv, zdone)
        for kt in range(16):
            p.add("scalar", lambda e, kt=kt: e.activation(
                xn[:, kt * HALF:(kt + 1) * HALF], xn[:, kt * HALF:(kt + 1) * HALF],
                AF.Identity, bias=lnb_t[:, kt:kt + 1], scale=lng_t[:, kt:kt + 1]), inc=ss)
        xn_done = ss.n
        p.wait("tensor", ss, xn_done)
        p.wait("vector", ss, xn_done)
        if hf == 1:
            p.add("sync", lambda e: e.dma_start(
                out=x2o_t[:, :].rearrange("p (k m) -> p k m", k=16),
                in_=x2f[:, HALF:S].rearrange("(k p) m -> p k m", p=128)), inc=sq, dma=True)

        for et in range(16):
            par = et % 2
            cd = stream_w(par, wkd[:, et * 128:(et + 1) * 128].rearrange("(k p) m -> p k m", p=128))
            p.wait("tensor", sv, cd)
            if kv_store[par]:
                p.wait("tensor", sgk[par], kv_store[par])
            p.wait("tensor", ss, ss.n)
            for fq in range(2):
                bank = fq
                for kt in range(16):
                    p.add("tensor", lambda e, kt=kt, fq=fq, par=par, bank=bank: e.matmul(
                        PS[:, bank * 512:(bank + 1) * 512],
                        wq_b[par][:, kt * 128:(kt + 1) * 128],
                        xn[:, kt * HALF + fq * 512:kt * HALF + (fq + 1) * 512],
                        start=(kt == 0), stop=(kt == 15)), inc=st)
                wmm[par] = st.n
                p.wait("scalar", st, st.n)
                p.add("scalar", lambda e, et=et, fq=fq, par=par, bank=bank: e.activation(
                    kv_t[par][:, fq * 512:(fq + 1) * 512], PS[:, bank * 512:(bank + 1) * 512],
                    AF.Identity, bias=bk_t[:, et:et + 1]), inc=ss)
            ev = ss.n
            p.wait("gpsimd", ss, ev)
            p.add("gpsimd", lambda e, et=et, hf=hf, par=par: e.dma_start(
                out=kscr[et * 128:(et + 1) * 128, hf * HALF:(hf + 1) * HALF],
                in_=kv_t[par][:, :]), inc=sgk[par], dma=True)
            kv_store[par] = sgk[par].n

        for eh in range(4):
            for ttg in range(2):
                for kt in range(16):
                    par = kt % 2
                    cd = stream_w_chunk(par, wvd[kt * 128:(kt + 1) * 128, eh * 512:(eh + 1) * 512])
                    p.wait("tensor", sv, cd)
                    if kt == 0:
                        p.wait("tensor", ss, ss.n)
                    if kt < 2 and kv_store[(ttg * 4) % 2] == 0:
                        pass
                    for tl in range(4):
                        tt = ttg * 4 + tl
                        p.add("tensor", lambda e, kt=kt, tt=tt, tl=tl, par=par: e.matmul(
                            PS[:, tl * 512:(tl + 1) * 512],
                            xn[:, kt * HALF + tt * 128:kt * HALF + (tt + 1) * 128],
                            wq_b[par][:, 0:512],
                            start=(kt == 0), stop=False), inc=st)
                    wmm[par] = st.n
                for tl in range(4):
                    p.add("tensor", lambda e, tl=tl, eh=eh: e.matmul(
                        PS[:, tl * 512:(tl + 1) * 512], or_t[0:1, :],
                        bv_b[0:1, eh * 512:(eh + 1) * 512], start=False, stop=True), inc=st)
                vmm = st.n
                p.wait("scalar", st, vmm)
                for tl in range(4):
                    tt = ttg * 4 + tl
                    par2 = tl % 2
                    if kv_store[par2]:
                        p.wait("scalar", sgk[par2], kv_store[par2])
                    p.add("scalar", lambda e, tl=tl, par2=par2: e.copy(
                        kv_t[par2][:, 0:512], PS[:, tl * 512:(tl + 1) * 512]), inc=ss)
                    p.wait("gpsimd", ss, ss.n)
                    p.add("gpsimd", lambda e, hf=hf, tt=tt, eh=eh, par2=par2: e.dma_start(
                        out=vscr[hf * HALF + tt * 128:hf * HALF + (tt + 1) * 128,
                                 eh * 512:(eh + 1) * 512],
                        in_=kv_t[par2][:, 0:512]), inc=sgk[par2], dma=True)
                    kv_store[par2] = sgk[par2].n

        if hf == 1:
            for et in range(16):
                par = et % 2
                cd = stream_w(par, wqd[:, et * 128:(et + 1) * 128].rearrange("(k p) m -> p k m", p=128))
                p.wait("tensor", sv, cd)
                p.wait("tensor", ss, ss.n)
                for fq in range(2):
                    bank = fq
                    for kt in range(16):
                        p.add("tensor", lambda e, kt=kt, fq=fq, par=par, bank=bank: e.matmul(
                            PS[:, bank * 512:(bank + 1) * 512],
                            wq_b[par][:, kt * 128:(kt + 1) * 128],
                            xn[:, kt * HALF + fq * 512:kt * HALF + (fq + 1) * 512],
                            start=(kt == 0), stop=(kt == 15)), inc=st)
                    wmm[par] = st.n
                    p.wait("scalar", st, st.n)
                    p.add("scalar", lambda e, et=et, fq=fq, bank=bank: e.activation(
                        q_sb[:, et * HALF + fq * 512:et * HALF + (fq + 1) * 512],
                        PS[:, bank * 512:(bank + 1) * 512],
                        AF.Identity, bias=bq_t[:, et:et + 1]), inc=ss)
        xn_users = st.n

    p.wait("gpsimd", sgk[0], sgk[0].n)
    p.wait("gpsimd", sgk[1], sgk[1].n)
    p.wait("sync", sgk[0], sgk[0].n)
    p.wait("sync", sgk[1], sgk[1].n)
    q_ready = ss.n
    p.wait("tensor", ss, q_ready)

    khready = [0, 0]
    kvuse = [0, 0]
    copyev = {0: 0, 1: 0}
    aswar = [0, 0]
    rlcevac_pp = 0
    avTT_last = 0
    state = {"avTT": 0}

    def load_head(h):
        hp = h % 2
        if kvuse[hp]:
            p.wait("sync", st, kvuse[hp])
        p.add("sync", lambda e, h=h, hp=hp: e.dma_start(
            out=kh[hp][:, :], in_=kscr[h * 128:(h + 1) * 128, :]), inc=sqh[hp], dma=True)
        p.add("sync", lambda e, h=h, hp=hp: e.dma_start(
            out=vh[hp][:, :].rearrange("p (k m) -> p k m", m=128),
            in_=vscr[:, h * 128:(h + 1) * 128].rearrange("(k p) m -> p k m", p=128)),
            inc=sqh[hp], dma=True)
        khready[hp] = sqh[hp].n

    def emit_tpav(pr):
        par = pr["par"]; hpar = pr["hpar"]; nkb = pr["nkb"]; h = pr["h"]; qt = pr["qt"]
        groups = [list(range(g, min(g + 4, nkb))) for g in range(0, nkb, 4)]
        cpds = []
        for gi in range(len(groups) + 1):
            if gi < len(groups):
                blocks = groups[gi]
                pstb = gi % 2
                c0p = 2048 + pstb * 512
                if copyev[pstb]:
                    p.wait("tensor", ss, copyev[pstb])
                for i, kb in enumerate(blocks):
                    p.add("tensor", lambda e, kb=kb, i=i, par=par, c0p=c0p: e.transpose(
                        PS[:, c0p + i * 128:c0p + (i + 1) * 128],
                        a_sb[par][:, kb * 128:(kb + 1) * 128], idb_t[:, :]), inc=st)
                tpd = st.n
                gw = len(blocks)
                p.wait("scalar", st, tpd)
                p.add("scalar", lambda e, blocks=blocks, par=par, c0p=c0p, gw=gw: e.copy(
                    at_sb[par][:, blocks[0] * 128:blocks[0] * 128 + gw * 128],
                    PS[:, c0p:c0p + gw * 128]), inc=ss)
                copyev[pstb] = ss.n
                cpds.append(ss.n)
                aswar[par] = tpd
            if gi >= 1:
                blocks = groups[gi - 1]
                p.wait("tensor", ss, cpds[gi - 1])
                if gi == 1 and state["avTT"]:
                    p.wait("tensor", sv, state["avTT"])
                for kb in blocks:
                    p.add("tensor", lambda e, kb=kb, hpar=hpar, par=par, nkb=nkb: e.matmul(
                        PS[:, 3072:3200], vh[hpar][:, kb * 128:(kb + 1) * 128],
                        at_sb[par][:, kb * 128:(kb + 1) * 128],
                        start=(kb == 0), stop=(kb == nkb - 1)), inc=st)
        avmm = st.n
        kvuse[hpar] = avmm
        return {"avmm": avmm, "h": h, "qt": qt, "bcsb": pr["bcsb"]}

    def emit_avtt(pr2):
        p.wait("vector", st, pr2["avmm"])
        p.wait("vector", ss, pr2["bcsb"])
        p.add("vector", lambda e, h=pr2["h"], qt=pr2["qt"]: e.tensor_mul(
            av_sb[:, h * HALF + qt * 128:h * HALF + (qt + 1) * 128],
            PS[:, 3072:3200], bc_sb[:, :]), inc=sv)
        state["avTT"] = sv.n

    load_head(0)
    prev = None
    idx = 0
    for h in range(NH):
        for qt in range(8):
            kv = 1024 + (qt + 1) * 128
            nkb = kv // 128
            par = idx % 2
            hpar = h % 2
            p.wait("tensor", sqh[hpar], khready[hpar])
            if prev:
                p.wait("tensor", ss, prev["expd"])
            c0 = 0
            while c0 < kv:
                fl = min(512, kv - c0)
                p.add("tensor", lambda e, h=h, qt=qt, c0=c0, fl=fl, hpar=hpar: e.matmul(
                    PS[:, c0:c0 + fl],
                    q_sb[:, h * HALF + qt * 128:h * HALF + (qt + 1) * 128],
                    kh[hpar][:, c0:c0 + fl], start=True, stop=False), inc=st)
                c0 += fl
            p.add("tensor", lambda e, kv=kv: e.matmul(
                PS[:, kv - 128:kv], idb_b[:, :], tri_t[:, :], start=False, stop=False), inc=st)
            c0 = 0
            while c0 < kv:
                fl = min(512, kv - c0)
                p.add("tensor", lambda e, c0=c0, fl=fl: e.matmul(
                    PS[:, c0:c0 + fl], or_t[0:1, :], minv_b[0:1, c0:c0 + fl],
                    start=False, stop=True), inc=st)
                c0 += fl
            smm = st.n
            kvuse[hpar] = smm
            pend = None
            if prev:
                pend = emit_tpav(prev)
            if qt == 0 and h + 1 < NH:
                load_head(h + 1)
            p.wait("vector", st, smm)
            p.add("vector", lambda e, kv=kv, par=par: e.reduce_max(
                m1c[par][:, :], PS[:, 0:kv], axis=AX.X), inc=sv)
            p.wait("vector", sv)
            p.add("vector", lambda e, par=par: e.tensor_scalar_mul(
                nm1c[par][:, :], m1c[par][:, :], -SCALE), inc=sv)
            nsc = sv.n
            if pend:
                emit_avtt(pend)
            p.wait("scalar", sv, nsc)
            if aswar[par]:
                p.wait("scalar", st, aswar[par])
            p.add("scalar", lambda e, kv=kv, par=par: e.activation(
                a_sb[par][:, 0:kv], PS[:, 0:kv], AF.Exp, scale=SCALE, bias=nm1c[par][:, 0:1],
                accum_out=lc[par][:, 0:1]), inc=ss)
            expd = ss.n
            p.wait("vector", ss, expd)
            p.add("vector", lambda e, par=par: e.reciprocal(rlc[par][:, :], lc[par][:, :]), inc=sv)
            rcp = sv.n
            p.wait("tensor", sv, rcp)
            if rlcevac_pp:
                p.wait("tensor", ss, rlcevac_pp)
            p.add("tensor", lambda e, par=par: e.matmul(
                PS[0:1, 3584:3712], rlc[par][:, 0:1], idb_t[:, :], start=True, stop=True), inc=st)
            p.wait("scalar", st, st.n)
            p.add("scalar", lambda e: e.copy(rlcrow[:, :], PS[0:1, 3584:3712]), inc=ss)
            rlcevac_pp = ss.n
            p.wait("tensor", ss, rlcevac_pp)
            p.add("tensor", lambda e: e.matmul(
                PS[:, 3712:3840], onesf_r[:, :], rlcrow[:, :], start=True, stop=True), inc=st)
            p.wait("scalar", st, st.n)
            if state["avTT"]:
                p.wait("scalar", sv, state["avTT"])
            p.add("scalar", lambda e: e.copy(bc_sb[:, :], PS[:, 3712:3840]), inc=ss)
            prev = {"par": par, "hpar": hpar, "h": h, "qt": qt, "nkb": nkb,
                    "expd": expd, "bcsb": ss.n}
            idx += 1
    emit_avtt(emit_tpav(prev))
    avTT_last = state["avTT"]

    p.wait("tensor", sv, avTT_last)
    p.wait("tensor", ss, ss.n)
    p.wait("tensor", sq, sq.n)
    p.wait("scalar", sgk[0], sgk[0].n)
    p.wait("scalar", sgk[1], sgk[1].n)
    p.wait("vector", sq, sq.n)
    wo_ev = 0
    for dt_ in range(16):
        par = dt_ % 2
        cd = stream_w(par, wod[:, dt_ * 128:(dt_ + 1) * 128].rearrange("(k p) m -> p k m", p=128))
        p.wait("tensor", sv, cd)
        p.wait("tensor", ss, ss.n)
        for fq in range(2):
            bank = fq
            for kt in range(16):
                p.add("tensor", lambda e, kt=kt, fq=fq, par=par, bank=bank: e.matmul(
                    PS[:, bank * 512:(bank + 1) * 512],
                    wq_b[par][:, kt * 128:(kt + 1) * 128],
                    av_sb[:, kt * HALF + fq * 512:kt * HALF + (fq + 1) * 512],
                    start=(kt == 0), stop=(kt == 15)), inc=st)
            wmm[par] = st.n
            p.wait("scalar", st, st.n)
            p.add("scalar", lambda e, dt_=dt_, fq=fq, bank=bank: e.activation(
                kv_t[fq][:, 0:512], PS[:, bank * 512:(bank + 1) * 512],
                AF.Identity, bias=bo_t[:, dt_:dt_ + 1]), inc=ss)
            oev = ss.n
            p.wait("vector", ss, oev)
            p.add("vector", lambda e, dt_=dt_, fq=fq: e.tensor_add(
                s_t[:, dt_ * HALF + fq * 512:dt_ * HALF + (fq + 1) * 512],
                kv_t[fq][:, 0:512],
                x2o_t[:, dt_ * HALF + fq * 512:dt_ * HALF + (fq + 1) * 512]), inc=sv)
        wo_ev = sv.n

    p.wait("scalar", sv, wo_ev)
    p.wait("tensor", sv, wo_ev)
    sqd = [0, 0]
    for kt in range(16):
        for fh in range(2):
            par = (kt * 2 + fh) % 2
            if sqd[par]:
                p.wait("scalar", st, sqd[par])
            p.add("scalar", lambda e, kt=kt, fh=fh, par=par: e.activation(
                sqt[par][:, :], s_t[:, kt * HALF + fh * 512:kt * HALF + (fh + 1) * 512],
                AF.Square), inc=ss)
            sd = ss.n
            p.wait("tensor", ss, sd)
            p.add("tensor", lambda e, kt=kt, fh=fh: e.matmul(
                PS[0:1, 2048 + fh * 512:2048 + (fh + 1) * 512], oc_t[:, :],
                s_t[:, kt * HALF + fh * 512:kt * HALF + (fh + 1) * 512],
                start=(kt == 0), stop=(kt == 15)), inc=st)
            p.add("tensor", lambda e, kt=kt, fh=fh, par=par: e.matmul(
                PS[0:1, 1024 + fh * 512:1024 + (fh + 1) * 512], oc_t[:, :],
                sqt[par][:, :], start=(kt == 0), stop=(kt == 15)), inc=st)
            sqd[par] = st.n
    p.wait("scalar", st, st.n)
    rws = ln_rows(2048, 1024, HALF)
    p.wait("tensor", sv, rws)
    for fh in range(2):
        p.add("tensor", lambda e, fh=fh: e.matmul(
            PS[:, 1024 + fh * 512:1024 + (fh + 1) * 512], or_t[0:1, :],
            rowb_rstd[0:1, fh * 512:(fh + 1) * 512], start=True, stop=True), inc=st)
        p.add("tensor", lambda e, fh=fh: e.matmul(
            PS[:, 2048 + fh * 512:2048 + (fh + 1) * 512], or_t[0:1, :],
            rowb_nmr[0:1, fh * 512:(fh + 1) * 512], start=True, stop=True), inc=st)
    bcmm = st.n
    p.wait("scalar", st, bcmm)
    p.add("scalar", lambda e: e.copy(ab_bc[:, :], PS[:, 1024:2048]), inc=ss)
    p.add("scalar", lambda e: e.copy(bb_bc[:, :], PS[:, 2048:3072]), inc=ss)
    bce = ss.n
    p.wait("vector", ss, bce)
    for kt in range(16):
        p.add("vector", lambda e, kt=kt: e.tensor_mul(
            s_t[:, kt * HALF:(kt + 1) * HALF], s_t[:, kt * HALF:(kt + 1) * HALF], ab_bc[:, :]), inc=sv)
        p.wait("vector", sv)
        p.add("vector", lambda e, kt=kt: e.tensor_add(
            s_t[:, kt * HALF:(kt + 1) * HALF], s_t[:, kt * HALF:(kt + 1) * HALF], bb_bc[:, :]), inc=sv)
    x3_done = sv.n
    p.wait("gpsimd", sv, x3_done)
    p.add("gpsimd", lambda e: e.dma_start(
        out=x3o.rearrange("(k p) m -> p k m", p=128),
        in_=s_t[:, :].rearrange("p (k m) -> p k m", k=16)), inc=sg, dma=True)
    p.wait("gpsimd", sg, sg.n)
    return {"stores": [sg]}


def run_attn(inp, x2):
    wq = inp["L1_attn_Wq"]; wk = inp["L1_attn_Wk"]; wv = inp["L1_attn_Wv"]; wo = inp["L1_attn_Wo"]
    col16 = lambda v: np.ascontiguousarray(v.reshape(16, 128).T)
    base = {
        "wqd": np.ascontiguousarray(wq), "wkd": np.ascontiguousarray(wk),
        "wvd": np.ascontiguousarray(wv), "wod": np.ascontiguousarray(wo),
        "bq": col16(inp["L1_attn_bq"]), "bk": col16(inp["L1_attn_bk"]),
        "bvr": np.ascontiguousarray(inp["L1_attn_bv"].reshape(1, D)),
        "bo": col16(inp["L1_attn_bo"]),
        "lng": col16(inp["L1_attn_ln_g"]), "lnb": col16(inp["L1_attn_ln_b"]),
    }
    in_maps = []
    for c in range(8):
        b, j = c // 2, c % 2
        own = x2[b][:, j * HALF:(j + 1) * HALF]
        oth = x2[b][:, (1 - j) * HALF:(2 - j) * HALF]
        m = dict(base)
        m["x2f"] = np.ascontiguousarray(np.concatenate([oth, own], axis=1))
        mv = np.zeros((1, S), np.float32)
        if j == 0:
            mv[0, 0:HALF] = -1.0e30
        m["minv"] = mv.astype(BF)
        in_maps.append(m)
    nc = build_attn()
    res = run_bass_kernel_spmd(nc, in_maps, core_ids=list(range(8)))
    x3 = np.empty((B, D, S), BF)
    for c in range(8):
        b, j = c // 2, c % 2
        x3[b][:, j * HALF:(j + 1) * HALF] = res.results[c]["x3o"]
    return x3




def ffn1_drams(nc, x3_kind="ExternalInput", x3_ap=None):
    io = {}
    io["x3i"] = x3_ap if x3_ap is not None else nc.dram_tensor(
        "x3i", [D, HALF], BF16, kind=x3_kind).ap()
    io["rwt"] = nc.dram_tensor("rwt", [128, 128], F32, kind="ExternalInput").ap()
    io["shg"] = nc.dram_tensor("shg", [64, D, 128], F32, kind="ExternalInput").ap()
    io["shu"] = nc.dram_tensor("shu", [64, D, 128], F32, kind="ExternalInput").ap()
    io["shd"] = nc.dram_tensor("shd", [16, 8192, 128], F32, kind="ExternalInput").ap()
    io["rg"] = nc.dram_tensor("rg", [8, D, 128], F32, kind="ExternalInput").ap()
    io["ru"] = nc.dram_tensor("ru", [8, D, 128], F32, kind="ExternalInput").ap()
    io["rd"] = nc.dram_tensor("rd", [16, 1024, 128], F32, kind="ExternalInput").ap()
    io["xo"] = nc.dram_tensor("xo", [D, HALF], F32, kind="ExternalOutput").ap()
    io["ones_colD"] = nc.inline_tensor(np.ones((128, 1), np.float32).astype(BF), name="ones_colD").ap()
    io["ones_rowD"] = nc.inline_tensor(np.ones((1, 128), np.float32).astype(BF), name="ones_rowD").ap()
    io["idfD"] = nc.inline_tensor(np.eye(128, dtype=np.float32), name="idfD").ap()
    io["onesfD"] = nc.inline_tensor(np.ones((1, 128), np.float32), name="onesfD").ap()
    return io


def build_ffn1():
    nc = bass.Bass("TRN2", target_bir_lowering=False)
    io = ffn1_drams(nc)
    p = Prog(nc)
    emit_ffn1(nc, p, io)
    p.finalize()
    return nc


def emit_ffn1(nc, p, io):
    x3i = io["x3i"]; rwt = io["rwt"]; shg = io["shg"]; shu = io["shu"]
    shd = io["shd"]; rg = io["rg"]; ru = io["ru"]; rd = io["rd"]; xo = io["xo"]
    ones_col = io["ones_colD"]; ones_row = io["ones_rowD"]
    idf = io["idfD"]; onesf = io["onesfD"]
    sq = p.sem("sq"); sg = p.sem("sg"); st = p.sem("st"); ss = p.sem("ss"); sv = p.sem("sv")
    sqw = [p.sem("sqw0"), p.sem("sqw1")]

    x3_t = p.sbuf("x3_t", [128, 16 * HALF], BF16)
    hid = p.sbuf("hid", [128, 16 * HALF], BF16)
    hid_r = hid
    acc = p.sbuf("acc", [128, 16 * HALF], F32)
    s_bf = [p.sbuf(f"s_bf{i}", [128, HALF], BF16) for i in range(2)]
    tmp_g2 = [p.sbuf(f"tmp_g{i}", [128, HALF], BF16) for i in range(2)]
    tmp_u2 = [p.sbuf(f"tmp_u{i}", [128, HALF], BF16) for i in range(2)]
    wq_f = [p.sbuf(f"wq_f{i}", [128, 2048], F32) for i in range(2)]
    wq_b = [p.sbuf(f"wq_b{i}", [128, 2048], BF16) for i in range(2)]
    sqt = [p.sbuf(f"sqt{i}", [128, HALF], BF16) for i in range(2)]
    rw_f = p.sbuf("rw_f", [128, 128], F32)
    rw_b = p.sbuf("rw_b", [128, 128], BF16)
    lgr = p.sbuf("lgr", [128, 8 * 8], F32)
    combc = p.sbuf("combc", [128, 8], F32)
    comb_row = p.sbuf("comb_row", [1, HALF], F32)
    comb_rb = p.sbuf("comb_rb", [1, HALF], BF16)
    comb_bc = p.sbuf("comb_bc", [128, HALF], BF16)
    r1 = p.sbuf("r1", [128, 1], F32)
    r2 = p.sbuf("r2", [128, 1], F32)
    r3 = p.sbuf("r3", [128, 1], F32)
    rt1 = p.sbuf("rt1", [128, 8], F32)
    rt2 = p.sbuf("rt2", [128, 8], F32)
    oc_t = p.sbuf("oc_t", [128, 1], BF16)
    or_t = p.sbuf("or_t", [1, 128], BF16)
    idf_t = p.sbuf("idf_t", [128, 128], F32)
    row_mu = p.sbuf("row_mu", [1, HALF], F32)
    row_a = p.sbuf("row_a", [1, HALF], F32)
    row_b = p.sbuf("row_b", [1, HALF], F32)
    eps_row = p.sbuf("eps_row", [1, 1], F32)
    onesf_t = p.sbuf("onesf_t", [1, 128], F32)
    ab_bc = p.sbuf("ab_bc", [128, HALF], F32)
    bb_bc = p.sbuf("bb_bc", [128, HALF], F32)

    PS = p.psum("PS", [128, 4096], F32)

    p.add("sync", lambda e: e.dma_start(
        out=x3_t[:, :].rearrange("p (k m) -> p k m", k=16),
        in_=x3i.rearrange("(k p) m -> p k m", p=128)), inc=sq, dma=True)
    p.add("sync", lambda e: e.dma_start(out=rw_f[:, :], in_=rwt), inc=sq, dma=True)
    p.add("sync", lambda e: e.dma_start(out=oc_t[:, :], in_=ones_col), inc=sq, dma=True)
    p.add("sync", lambda e: e.dma_start(out=or_t[:, :], in_=ones_row), inc=sq, dma=True)
    p.add("sync", lambda e: e.dma_start(out=idf_t[:, :], in_=idf), inc=sq, dma=True)
    p.add("sync", lambda e: e.dma_start(out=onesf_t[:, :], in_=onesf), inc=sq, dma=True)
    n_static = sq.n
    p.wait("vector", sq, n_static)
    p.add("vector", lambda e: e.memset(eps_row[:, :], 1e-5), inc=sv)
    p.add("vector", lambda e: e.tensor_copy(rw_b[:, :], rw_f[:, :]), inc=sv)
    prep_v = sv.n
    p.wait("scalar", sv, prep_v)
    p.wait("tensor", sv, prep_v)

    def ln_rows(sum_c0, sq_c0, tlen, invn=1.0 / 2048.0):
        p.add("scalar", lambda e: e.activation(
            row_mu[:, 0:tlen], PS[0:1, sum_c0:sum_c0 + tlen], AF.Copy, scale=invn), inc=ss)
        p.add("scalar", lambda e: e.activation(
            row_a[:, 0:tlen], PS[0:1, sq_c0:sq_c0 + tlen], AF.Copy, scale=invn), inc=ss)
        p.wait("scalar", ss)
        p.add("scalar", lambda e: e.activation(
            row_b[:, 0:tlen], row_mu[:, 0:tlen], AF.Square), inc=ss)
        p.wait("vector", ss)
        p.add("vector", lambda e: e.tensor_sub(
            row_a[:, 0:tlen], row_a[:, 0:tlen], row_b[:, 0:tlen]), inc=sv)
        p.wait("scalar", sv)
        p.add("scalar", lambda e: e.activation(
            row_a[:, 0:tlen], row_a[:, 0:tlen], AF.Ln, bias=eps_row[0:1, 0:1]), inc=ss)
        p.wait("scalar", ss)
        p.add("scalar", lambda e: e.activation(
            row_a[:, 0:tlen], row_a[:, 0:tlen], AF.Exp, scale=-0.5), inc=ss)
        p.wait("vector", ss)
        p.add("vector", lambda e: e.tensor_mul(
            row_b[:, 0:tlen], row_mu[:, 0:tlen], row_a[:, 0:tlen]), inc=sv)
        p.wait("vector", sv)
        p.add("vector", lambda e: e.tensor_scalar_mul(
            row_b[:, 0:tlen], row_b[:, 0:tlen], -1.0), inc=sv)
        p.wait("vector", sv)
        return sv.n

    wmm = {}
    wcast = {}
    QENG = ["sync", "scalar"]

    def stream_w(par, src_ap, ncols):
        eng = QENG[par]
        if par in wcast and wcast[par]:
            p.wait(eng, sv, wcast[par])
        p.add(eng, lambda e, par=par, src_ap=src_ap, ncols=ncols: e.dma_start(
            out=wq_f[par][:, 0:ncols].rearrange("p (k m) -> p k m", m=128),
            in_=src_ap.rearrange("(k p) m -> p k m", p=128)),
            inc=sqw[par], dma=True)
        p.wait("vector", sqw[par], sqw[par].n)
        if par in wmm and wmm[par]:
            p.wait("vector", st, wmm[par])
        p.add("vector", lambda e, par=par, ncols=ncols: e.tensor_copy(
            wq_b[par][:, 0:ncols], wq_f[par][:, 0:ncols]), inc=sv)
        wcast[par] = sv.n
        return sv.n

    for tt in range(8):
        for kt in range(16):
            p.add("tensor", lambda e, tt=tt, kt=kt: e.matmul(
                PS[:, tt * 8:(tt + 1) * 8],
                x3_t[:, kt * HALF + tt * 128:kt * HALF + (tt + 1) * 128],
                rw_b[:, kt * 8:(kt + 1) * 8],
                start=(kt == 0), stop=(kt == 15)), inc=st)
    lgmm = st.n
    p.wait("scalar", st, lgmm)
    p.add("scalar", lambda e: e.copy(lgr[:, :], PS[:, 0:64]), inc=ss)
    lgev = ss.n
    p.wait("vector", ss, lgev)
    for tt in range(8):
        lg = lgr[:, tt * 8:(tt + 1) * 8]

        def v(fn):
            p.add("vector", fn, inc=sv)
            p.wait("vector", sv)
        v(lambda e, lg=lg: e.reduce_max(r1[:, :], lg, axis=AX.X))
        v(lambda e, lg=lg: e.tensor_single_scalar(rt1[:, :], lg, r1[:, 0:1], ALU.is_equal))
        v(lambda e, lg=lg: e.scalar_tensor_tensor(rt2[:, :], rt1[:, :], -BIGF, lg, ALU.mult, ALU.add))
        v(lambda e: e.reduce_max(r2[:, :], rt2[:, :], axis=AX.X))
        v(lambda e: e.tensor_scalar_mul(r3[:, :], r1[:, :], -1.0))
        sv_r = sv.n
        p.wait("scalar", sv, sv_r)
        p.add("scalar", lambda e: e.activation(r2[:, :], r2[:, :], AF.Exp, bias=r3[:, 0:1]), inc=ss)
        p.wait("scalar", ss)
        p.wait("vector", ss)
        v(lambda e: e.tensor_scalar_add(r1[:, :], r2[:, :], 1.0))
        v(lambda e: e.reciprocal(r3[:, :], r1[:, :]))
        v(lambda e, tt=tt: e.tensor_scalar(
            combc[:, tt:tt + 1], r2[:, :][:, 0:1], 1.0, r3[:, 0:1], ALU.add, ALU.mult))
    comb_done = sv.n
    p.wait("tensor", sv, comb_done)
    for tt in range(8):
        p.add("tensor", lambda e, tt=tt: e.matmul(
            PS[0:1, 1024 + tt * 128:1024 + (tt + 1) * 128],
            combc[:, tt:tt + 1], idf_t[:, :], start=True, stop=True), inc=st)
    ctm = st.n
    p.wait("scalar", st, ctm)
    p.add("scalar", lambda e: e.copy(comb_row[:, :], PS[0:1, 1024:2048]), inc=ss)
    p.wait("scalar", ss)
    crow = ss.n
    p.wait("vector", ss, crow)
    p.add("vector", lambda e: e.tensor_copy(comb_rb[:, :], comb_row[:, :]), inc=sv)
    p.wait("vector", sv)
    crb = sv.n
    p.wait("tensor", sv, crb)
    for fh in range(2):
        p.add("tensor", lambda e, fh=fh: e.matmul(
            PS[:, 2048 + fh * 512:2048 + (fh + 1) * 512], or_t[0:1, :],
            comb_rb[0:1, fh * 512:(fh + 1) * 512], start=True, stop=True), inc=st)
    cbm = st.n
    p.wait("scalar", st, cbm)
    p.add("scalar", lambda e: e.copy(comb_bc[:, :], PS[:, 2048:3072]), inc=ss)
    cbc = ss.n
    p.wait("tensor", ss, cbc)
    p.wait("vector", ss, cbc)

    bank_ev = {i: (ss, cbc) for i in range(8)}
    seq = [0]
    tile_ctr = [0]
    state = {"tmp": [0, 0], "hidwar": 0, "accev": {}, "dnv": {0: 0, 1: 0},
             "sbfw": [0, 0], "statmm": 0}
    GCOL = [0, 2048]
    UCOL = [1024, 3072]

    def nextpar():
        s = seq[0] % 2
        seq[0] += 1
        return s

    def bank_wait(eng, cols):
        for bk in (cols // 512, cols // 512 + 1):
            sem, cnt = bank_ev[bk]
            if cnt:
                p.wait(eng, sem, cnt)

    def mlp_tile(wg_ap, wu_ap, dstc, comb):
        ti = tile_ctr[0]; tile_ctr[0] += 1
        tb = ti % 2
        gc, uc = GCOL[tb], UCOL[tb]
        parg = nextpar()
        cd = stream_w(parg, wg_ap, 2048)
        p.wait("tensor", sv, cd)
        bank_wait("tensor", gc)
        for thh in range(2):
            for kt in range(16):
                p.add("tensor", lambda e, kt=kt, thh=thh, parg=parg, gc=gc: e.matmul(
                    PS[:, gc + thh * 512:gc + (thh + 1) * 512],
                    wq_b[parg][:, kt * 128:(kt + 1) * 128],
                    x3_t[:, kt * HALF + thh * TH:kt * HALF + (thh + 1) * TH],
                    start=(kt == 0), stop=(kt == 15)), inc=st)
        wmm[parg] = st.n
        gmm = st.n
        paru = nextpar()
        cd = stream_w(paru, wu_ap, 2048)
        p.wait("tensor", sv, cd)
        bank_wait("tensor", uc)
        for thh in range(2):
            for kt in range(16):
                p.add("tensor", lambda e, kt=kt, thh=thh, paru=paru, uc=uc: e.matmul(
                    PS[:, uc + thh * 512:uc + (thh + 1) * 512],
                    wq_b[paru][:, kt * 128:(kt + 1) * 128],
                    x3_t[:, kt * HALF + thh * TH:kt * HALF + (thh + 1) * TH],
                    start=(kt == 0), stop=(kt == 15)), inc=st)
        wmm[paru] = st.n
        umm = st.n
        p.wait("scalar", st, gmm)
        if state["tmp"][tb]:
            p.wait("scalar", sg, state["tmp"][tb])
        p.add("scalar", lambda e, gc=gc, tb=tb: e.activation(
            tmp_g2[tb][:, :], PS[:, gc:gc + HALF], AF.Silu), inc=ss)
        bank_ev[gc // 512] = bank_ev[gc // 512 + 1] = (ss, ss.n)
        p.wait("scalar", st, umm)
        p.add("scalar", lambda e, uc=uc, tb=tb: e.activation(
            tmp_u2[tb][:, :], PS[:, uc:uc + HALF], AF.Identity), inc=ss)
        bank_ev[uc // 512] = bank_ev[uc // 512 + 1] = (ss, ss.n)
        uev = ss.n
        p.wait("gpsimd", ss, uev)
        if state["hidwar"]:
            p.wait("gpsimd", st, state["hidwar"])
            state["hidwar"] = 0
        p.add("gpsimd", lambda e, dstc=dstc, tb=tb: e.tensor_mul(
            hid[:, dstc:dstc + HALF], tmp_g2[tb][:, :], tmp_u2[tb][:, :]), inc=sg)
        if comb:
            p.wait("gpsimd", sg)
            p.add("gpsimd", lambda e, dstc=dstc: e.tensor_mul(
                hid[:, dstc:dstc + HALF], hid[:, dstc:dstc + HALF], comb_bc[:, :]), inc=sg)
        state["tmp"][tb] = sg.n
        return sg.n

    dmm = 0
    for hg in range(4):
        for lt in range(16):
            hta = hg * 16 + lt
            hm = mlp_tile(shg[hta], shu[hta], lt * HALF, False)
        hid_done = hm
        p.wait("tensor", sg, hid_done)
        for dt_ in range(16):
            dp = dt_ % 2
            c0 = dp * 1024
            par = nextpar()
            cd = stream_w(par, shd[dt_, hg * 2048:(hg + 1) * 2048], 2048)
            p.wait("tensor", sv, cd)
            bank_wait("tensor", c0)
            for thh in range(2):
                for kl in range(16):
                    p.add("tensor", lambda e, kl=kl, thh=thh, par=par, c0=c0: e.matmul(
                        PS[:, c0 + thh * 512:c0 + (thh + 1) * 512],
                        wq_b[par][:, kl * 128:(kl + 1) * 128],
                        hid[:, kl * HALF + thh * TH:kl * HALF + (thh + 1) * TH],
                        start=(kl == 0), stop=(kl == 15)), inc=st)
            wmm[par] = st.n
            dmm = st.n
            if hg == 0:
                p.wait("scalar", st, dmm)
                p.add("scalar", lambda e, dt_=dt_, c0=c0: e.copy(
                    acc[:, dt_ * HALF:(dt_ + 1) * HALF], PS[:, c0:c0 + HALF]), inc=ss)
                bank_ev[c0 // 512] = bank_ev[c0 // 512 + 1] = (ss, ss.n)
                state["accev"][dt_] = ss.n
            else:
                p.wait("vector", st, dmm)
                if hg == 1:
                    p.wait("vector", ss, state["accev"][dt_])
                p.add("vector", lambda e, dt_=dt_, c0=c0: e.tensor_add(
                    acc[:, dt_ * HALF:(dt_ + 1) * HALF], acc[:, dt_ * HALF:(dt_ + 1) * HALF],
                    PS[:, c0:c0 + HALF]), inc=sv)
                bank_ev[c0 // 512] = bank_ev[c0 // 512 + 1] = (sv, sv.n)
        state["hidwar"] = dmm

    for ht in range(8):
        hm = mlp_tile(rg[ht], ru[ht], ht * HALF, True)
    p.wait("tensor", sg, hm)

    for dt_ in range(16):
        dp = dt_ % 2
        c0 = dp * 1024
        par = nextpar()
        cd = stream_w(par, rd[dt_], 1024)
        p.wait("tensor", sv, cd)
        bank_wait("tensor", c0)
        for thh in range(2):
            for kl in range(8):
                p.add("tensor", lambda e, kl=kl, thh=thh, par=par, c0=c0: e.matmul(
                    PS[:, c0 + thh * 512:c0 + (thh + 1) * 512],
                    wq_b[par][:, kl * 128:(kl + 1) * 128],
                    hid[:, kl * HALF + thh * TH:kl * HALF + (thh + 1) * TH],
                    start=(kl == 0), stop=(kl == 7)), inc=st)
        wmm[par] = st.n
        rdm = st.n
        p.wait("vector", st, rdm)
        p.add("vector", lambda e, dt_=dt_, c0=c0: e.tensor_add(
            acc[:, dt_ * HALF:(dt_ + 1) * HALF], acc[:, dt_ * HALF:(dt_ + 1) * HALF],
            PS[:, c0:c0 + HALF]), inc=sv)
        bank_ev[c0 // 512] = bank_ev[c0 // 512 + 1] = (sv, sv.n)
        p.wait("vector", sv)
        p.add("vector", lambda e, dt_=dt_: e.tensor_add(
            acc[:, dt_ * HALF:(dt_ + 1) * HALF], acc[:, dt_ * HALF:(dt_ + 1) * HALF],
            x3_t[:, dt_ * HALF:(dt_ + 1) * HALF]), inc=sv)
        p.wait("vector", sv)
        if state["sbfw"][dp]:
            p.wait("vector", st, state["sbfw"][dp])
        p.add("vector", lambda e, dt_=dt_, dp=dp: e.tensor_copy(
            s_bf[dp][:, :], acc[:, dt_ * HALF:(dt_ + 1) * HALF]), inc=sv)
        sbc = sv.n
        p.wait("scalar", sv, sbc)
        p.add("scalar", lambda e, dp=dp: e.activation(
            sqt[dp][:, :], s_bf[dp][:, :], AF.Square), inc=ss)
        sqe = ss.n
        p.wait("tensor", sv, sbc)
        p.wait("tensor", ss, sqe)
        for thh in range(2):
            p.add("tensor", lambda e, thh=thh, dt_=dt_, dp=dp: e.matmul(
                PS[0:1, 2048 + thh * 512:2048 + (thh + 1) * 512], oc_t[:, :],
                s_bf[dp][:, thh * TH:(thh + 1) * TH],
                start=(dt_ == 0), stop=(dt_ == 15)), inc=st)
            p.add("tensor", lambda e, thh=thh, dt_=dt_, dp=dp: e.matmul(
                PS[0:1, 3072 + thh * 512:3072 + (thh + 1) * 512], oc_t[:, :],
                sqt[dp][:, thh * TH:(thh + 1) * TH],
                start=(dt_ == 0), stop=(dt_ == 15)), inc=st)
        state["sbfw"][dp] = st.n
    s_done = sv.n

    p.wait("scalar", st, st.n)
    rws = ln_rows(2048, 3072, HALF)
    p.wait("tensor", sv, rws)
    bank_wait("tensor", 0)
    bank_wait("tensor", 1024)
    for thh in range(2):
        p.add("tensor", lambda e, thh=thh: e.matmul(
            PS[:, thh * 512:(thh + 1) * 512], onesf_t[0:1, :],
            row_a[0:1, thh * TH:(thh + 1) * TH], start=True, stop=True), inc=st)
        p.add("tensor", lambda e, thh=thh: e.matmul(
            PS[:, 1024 + thh * 512:1024 + (thh + 1) * 512], onesf_t[0:1, :],
            row_b[0:1, thh * TH:(thh + 1) * TH], start=True, stop=True), inc=st)
    bmm = st.n
    p.wait("scalar", st, bmm)
    p.add("scalar", lambda e: e.copy(ab_bc[:, :], PS[:, 0:1024]), inc=ss)
    p.add("scalar", lambda e: e.copy(bb_bc[:, :], PS[:, 1024:2048]), inc=ss)
    bev = ss.n
    p.wait("vector", ss, bev)
    for dt_ in range(16):
        p.add("vector", lambda e, dt_=dt_: e.tensor_mul(
            acc[:, dt_ * HALF:(dt_ + 1) * HALF], acc[:, dt_ * HALF:(dt_ + 1) * HALF],
            ab_bc[:, :]), inc=sv)
        p.wait("vector", sv)
        p.add("vector", lambda e, dt_=dt_: e.tensor_add(
            acc[:, dt_ * HALF:(dt_ + 1) * HALF], acc[:, dt_ * HALF:(dt_ + 1) * HALF],
            bb_bc[:, :]), inc=sv)
        nrm = sv.n
        p.wait("sync", sv, nrm)
        p.add("sync", lambda e, dt_=dt_: e.dma_start(
            out=xo[dt_ * 128:(dt_ + 1) * 128, :],
            in_=acc[:, dt_ * HALF:(dt_ + 1) * HALF]), inc=sg, dma=True)
    p.wait("sync", sg, sg.n)
    return {"stores": [sg]}



def run_ffn1(inp, x3):
    rw = inp["L1_ffn_router_weight"]
    t64 = lambda w: np.ascontiguousarray(
        w.T.reshape(D, w.shape[0] // 128, 128).transpose(1, 0, 2))
    base = {
        "rwt": np.ascontiguousarray(rw.T.reshape(16, 128, 8).transpose(1, 0, 2).reshape(128, 128)),
        "shg": t64(inp["L1_ffn_sh_gate"]),
        "shu": t64(inp["L1_ffn_sh_up"]),
        "shd": np.ascontiguousarray(
            inp["L1_ffn_sh_down"].T.reshape(8192, 16, 128).transpose(1, 0, 2)),
        "rg": t64(inp["L1_ffn_r_gate"]),
        "ru": t64(inp["L1_ffn_r_up"]),
        "rd": np.ascontiguousarray(
            inp["L1_ffn_r_down"].T.reshape(1024, 16, 128).transpose(1, 0, 2)),
    }
    in_maps = []
    for c in range(8):
        b, j = c // 2, c % 2
        m = dict(base)
        m["x3i"] = np.ascontiguousarray(x3[b][:, j * HALF:(j + 1) * HALF])
        in_maps.append(m)
    nc = build_ffn1()
    res = run_bass_kernel_spmd(nc, in_maps, core_ids=list(range(8)))
    out = np.empty((B, S, D), np.float32)
    for c in range(8):
        b, j = c // 2, c % 2
        out[b, j * HALF:(j + 1) * HALF, :] = res.results[c]["xo"].T
    return out



def build_fused():
    nc = bass.Bass("TRN2", target_bir_lowering=False)
    ioA = mamba_drams(nc, kind="Internal")
    ioB = ffn0_drams(nc, yp_kind="Internal", x2_kind="Internal")
    ioC = attn_drams(nc, x2f_kind="Internal", x3_kind="Internal")
    ioD = ffn1_drams(nc, x3_ap=ioC["x3o"])
    cc1_in = nc.dram_tensor("cc1_in", [DSL, HALF], BF16, kind="Internal").ap()
    cc1_out = nc.dram_tensor("cc1_out", [D, HALF], BF16, kind="Internal").ap()
    cc2_out = nc.dram_tensor("cc2_out", [2 * D, HALF], BF16, kind="Internal").ap()

    ARENA = nc.alloc_sbuf_tensor("ARENA", [128, 52900], F32)
    PSAR = nc.alloc_psum_tensor("PSAR", [128, 4096], F32)
    p = Prog(nc, arena=ARENA, psarena=PSAR)
    groups = [[0, 1], [2, 3], [4, 5], [6, 7]]

    with nc.named_scope("phA"):
        emit_mamba(nc, p, ioA)

    p.barrier()
    p.arena_reset()
    cid = nc.partition_id()
    jv = cid % 2
    ov = 1 - jv
    sb1 = p.sem("sb1")
    scc1 = p.sem("scc1")
    ypre = ioA["ypre"]
    yp = ioB["yp"]
    p.add("sync", lambda e: e.dma_start(
        out=cc1_in, in_=ypre[:, bass.ts(ov, HALF)]), inc=sb1, dma=True)
    p.wait("gpsimd", sb1, sb1.n)
    p.add("gpsimd", lambda e: e.collective_compute(
        "AllGather", ALU.bypass, replica_groups=groups,
        ins=[cc1_in], outs=[cc1_out]), inc=scc1)
    p.wait("sync", scc1, 1)
    p.add("sync", lambda e: e.dma_start(
        out=yp[bass.ts(jv, DSL), :], in_=ypre[:, bass.ts(jv, HALF)]), inc=sb1, dma=True)
    p.add("sync", lambda e: e.dma_start(
        out=yp[bass.ts(ov, DSL), :], in_=cc1_out[bass.ts(ov, DSL), :]), inc=sb1, dma=True)
    p.wait("sync", sb1, sb1.n)
    p.barrier()
    p.arena_reset()

    with nc.named_scope("phB"):
        emit_ffn0(nc, p, ioB)

    p.barrier()
    p.arena_reset()
    sb2 = p.sem("sb2")
    scc2 = p.sem("scc2")
    x2o = ioB["x2o"]
    x2f = ioC["x2f"]
    p.add("gpsimd", lambda e: e.collective_compute(
        "AllGather", ALU.bypass, replica_groups=groups,
        ins=[x2o], outs=[cc2_out]), inc=scc2)
    p.wait("sync", scc2, 1)
    p.add("sync", lambda e: e.dma_start(
        out=x2f[:, 0:HALF], in_=cc2_out[bass.ts(ov, D), :]), inc=sb2, dma=True)
    p.add("sync", lambda e: e.dma_start(
        out=x2f[:, HALF:S], in_=x2o), inc=sb2, dma=True)
    p.wait("sync", sb2, sb2.n)
    p.barrier()
    p.arena_reset()

    with nc.named_scope("phC"):
        emit_attn(nc, p, ioC)

    p.barrier()
    p.arena_reset()

    with nc.named_scope("phD"):
        emit_ffn1(nc, p, ioD)
    p.finalize()
    return nc


def prep_fused(inp, c):
    b, j = c // 2, c % 2
    m = {}
    x = inp["L0_attn_x"]; xpw = inp["L0_attn_x_proj_w"]
    jsl = slice(j * DSL, (j + 1) * DSL)
    if j == 0:
        xb, xpwb = x[b], xpw
    else:
        xb = np.roll(x[b], -DSL, axis=1)
        xpwb = np.roll(xpw, -DSL, axis=1)
    m["xT"] = np.ascontiguousarray(xb.T)
    m["xpwT"] = np.ascontiguousarray(xpwb.T)
    m["dtwT"] = np.ascontiguousarray(inp["L0_attn_dt_proj_w"][jsl].T)
    m["dtb"] = np.ascontiguousarray(inp["L0_attn_dt_proj_b"][jsl].reshape(8, 128).T)
    m["alog"] = np.ascontiguousarray(
        inp["L0_attn_A_log"][jsl].reshape(8, 128, NST).transpose(1, 0, 2).reshape(128, 128))
    m["dp"] = np.ascontiguousarray(inp["L0_attn_D_param"][jsl].reshape(8, 128).T)
    m["wg"] = np.ascontiguousarray(inp["L0_ffn_Wg"])
    m["bg"] = np.ascontiguousarray(inp["L0_ffn_bg"].reshape(1, 8))
    m["wgate"] = np.ascontiguousarray(inp["L0_ffn_Wgate"])
    m["wup"] = np.ascontiguousarray(inp["L0_ffn_Wup"])
    m["wdown"] = np.ascontiguousarray(inp["L0_ffn_Wdown"])
    m["bgate"] = np.ascontiguousarray(
        inp["L0_ffn_bgate"].reshape(6, 8, 128).transpose(2, 0, 1).reshape(128, 48))
    m["bup"] = np.ascontiguousarray(
        inp["L0_ffn_bup"].reshape(6, 8, 128).transpose(2, 0, 1).reshape(128, 48))
    m["bdown6"] = np.ascontiguousarray(inp["L0_ffn_bdown"])
    m["wsup"] = np.ascontiguousarray(inp["L0_ffn_Ws_up"])
    m["bsup"] = np.ascontiguousarray(inp["L0_ffn_bs_up"].reshape(16, 128).T)
    m["wsdown"] = np.ascontiguousarray(inp["L0_ffn_Ws_down"])
    m["bsdown"] = np.ascontiguousarray(inp["L0_ffn_bs_down"].reshape(16, 128).T)
    col16 = lambda v: np.ascontiguousarray(v.reshape(16, 128).T)
    m["wqd"] = np.ascontiguousarray(inp["L1_attn_Wq"])
    m["wkd"] = np.ascontiguousarray(inp["L1_attn_Wk"])
    m["wvd"] = np.ascontiguousarray(inp["L1_attn_Wv"])
    m["wod"] = np.ascontiguousarray(inp["L1_attn_Wo"])
    m["bq"] = col16(inp["L1_attn_bq"]); m["bk"] = col16(inp["L1_attn_bk"])
    m["bvr"] = np.ascontiguousarray(inp["L1_attn_bv"].reshape(1, D))
    m["bo"] = col16(inp["L1_attn_bo"])
    m["lng"] = col16(inp["L1_attn_ln_g"]); m["lnb"] = col16(inp["L1_attn_ln_b"])
    mv = np.zeros((1, S), np.float32)
    if j == 0:
        mv[0, 0:HALF] = -1.0e30
    m["minv"] = mv.astype(BF)
    rw = inp["L1_ffn_router_weight"]
    t64 = lambda w: np.ascontiguousarray(
        w.T.reshape(D, w.shape[0] // 128, 128).transpose(1, 0, 2))
    m["rwt"] = np.ascontiguousarray(rw.T.reshape(16, 128, 8).transpose(1, 0, 2).reshape(128, 128))
    m["shg"] = t64(inp["L1_ffn_sh_gate"])
    m["shu"] = t64(inp["L1_ffn_sh_up"])
    m["shd"] = np.ascontiguousarray(
        inp["L1_ffn_sh_down"].T.reshape(8192, 16, 128).transpose(1, 0, 2))
    m["rg"] = t64(inp["L1_ffn_r_gate"])
    m["ru"] = t64(inp["L1_ffn_r_up"])
    m["rd"] = np.ascontiguousarray(
        inp["L1_ffn_r_down"].T.reshape(1024, 16, 128).transpose(1, 0, 2))
    return m


def kernel(**inputs):
    nc = build_fused()
    in_maps = [prep_fused(inputs, c) for c in range(8)]
    res = run_bass_kernel_spmd(nc, in_maps, core_ids=list(range(8)))
    out = np.empty((B, S, D), np.float32)
    for c in range(8):
        b, j = c // 2, c % 2
        out[b, j * HALF:(j + 1) * HALF, :] = res.results[c]["xo"].T
    return out


def kernel_multilaunch(**inputs):
    ypre = run_mamba(inputs)
    x2 = run_ffn0(inputs, ypre)
    x3 = run_attn(inputs, x2)
    out = run_ffn1(inputs, x3)
    return out


